# version 06
# baseline (speedup 1.0000x reference)
import contextlib
import numpy as np
import ml_dtypes
import concourse.bass as bass
import concourse.mybir as mybir
from concourse.bass_utils import run_bass_kernel_spmd

F32 = mybir.dt.float32
BF16 = mybir.dt.bfloat16
I32 = mybir.dt.int32
ALU = mybir.AluOpType
ACT = mybir.ActivationFunctionType
AX = mybir.AxisListType.X

NCORES = 8
S = 8192
D = 2048
DC = 16
H = 16
DH = 128
MD = 64
E = 8192
TB = 512
TC = 1024
QKV_O = 6144
INV_SQRT_DH = float(1.0 / np.sqrt(128.0))
LN_EPS = 1e-5
ATTA_EPS = 1e-6
FFN_EPS = 1e-8
C0_ABS = (0.017632, 0.017632, 0.008816)
NIT = 26

DEBUG_TAPS = True
PH_ATTA = True
PH_FFN1 = True
PH_ATTB = True
PH_FFN2 = True


def _bf(x):
    return np.ascontiguousarray(x).astype(ml_dtypes.bfloat16)


def _ct(x):
    return np.ascontiguousarray(x)


def prepare_host(inp):
    x2d = np.asarray(inp["L0_attn_a_hidden_states"], np.float32).reshape(S, D)
    wqT = _ct(np.asarray(inp["L0_attn_a_wq"], np.float32).T)
    wkT = _ct(np.asarray(inp["L0_attn_a_wk"], np.float32).T)
    wvT = _ct(np.asarray(inp["L0_attn_a_wv"], np.float32).T)
    woT = _ct(np.asarray(inp["L0_attn_a_wo"], np.float32).T)
    qkvT = _ct(np.asarray(inp["L0_attn_b_w_qkv"], np.float32).T)
    woutT = _ct(np.asarray(inp["L0_attn_b_w_out"], np.float32).T)
    wuT = _ct(np.asarray(inp["L0_ffn_W_up"], np.float32).T)
    wgT = _ct(np.asarray(inp["L0_ffn_W_gate"], np.float32).T)
    wdT = _ct(np.asarray(inp["L0_ffn_W_down"], np.float32).T)

    def re_bias(b, nch):
        return _ct(np.asarray(b, np.float32).reshape(nch, 128).T)

    biases = {
        "bq": re_bias(inp["L0_attn_a_bq"], DC), "bk": re_bias(inp["L0_attn_a_bk"], DC),
        "bv": re_bias(inp["L0_attn_a_bv"], DC), "bo": re_bias(inp["L0_attn_a_bo"], DC),
        "bqkv": re_bias(inp["L0_attn_b_b_qkv"], 48), "bout": re_bias(inp["L0_attn_b_b_out"], DC),
    }
    wmapqT = _ct(np.asarray(inp["L0_attn_a_wmap_q"], np.float32).T)
    wmapkT = _ct(np.asarray(inp["L0_attn_a_wmap_k"], np.float32).T)

    ident = _bf(np.eye(128, dtype=np.float32))
    pp = np.arange(128)[:, None]
    ff = np.arange(TB)[None, :]
    masks = _bf(np.concatenate([(sub * 128 + pp <= ff) for sub in range(4)], 1).astype(np.float32))
    oh = np.zeros((128, 16 * H), np.float32)
    for h in range(H):
        oh[:, 16 * h + h] = 1.0
    ohot = _bf(oh)
    ohr = np.zeros((H, 128 * H), np.float32)
    for h in range(H):
        ohr[h, 128 * h:128 * (h + 1)] = float(2 ** 25)

    in_maps = []
    for i in range(NCORES):
        xs = np.concatenate([x2d[i * TB:(i + 1) * TB], x2d[(15 - i) * TB:(16 - i) * TB]], 0)
        hsel = np.zeros(NIT, np.float32)
        for r in range(8):
            hsel[1 + r] = 0.0 if r < i else -200.0
            hsel[18 + r] = 0.0 if r > i else -200.0
        d = {
            "xT": _ct(xs.T),
            "wq_sl": _ct(wqT[256 * i:256 * (i + 1)]),
            "wk_sl": _ct(wkT[256 * i:256 * (i + 1)]),
            "wv_sl": _ct(wvT[256 * i:256 * (i + 1)]),
            "wo_sl": _ct(woT[256 * i:256 * (i + 1)]),
            "qkv_sl": _ct(qkvT[256 * i:256 * (i + 1)]),
            "wout_sl": _ct(woutT[256 * i:256 * (i + 1)]),
            "wu_sl": _ct(wuT[256 * i:256 * (i + 1)]),
            "wg_sl": _ct(wgT[256 * i:256 * (i + 1)]),
            "wd_sl": _ct(wdT[1024 * i:1024 * (i + 1)]),
            "wmapqT": wmapqT, "wmapkT": wmapkT,
            "ident": ident, "masks": masks, "ohot": ohot, "ohr": ohr,
            "hsel": hsel.reshape(1, NIT),
        }
        d.update(biases)
        in_maps.append(d)
    return in_maps


def assemble_host(results):
    y = np.empty((S, D), np.float32)
    for i in range(NCORES):
        o = results[i]["out"]
        y[i * TB:(i + 1) * TB] = o[:, :TB].T
        y[(15 - i) * TB:(16 - i) * TB] = o[:, TB:].T
    return y.reshape(1, S, D)


def build_nc():
    nc = bass.Bass("TRN2", target_bir_lowering=False, debug=False,
                   detect_race_conditions=False)
    uid = [0]

    def nm(base):
        uid[0] += 1
        return f"{base}_{uid[0]}"

    xT = nc.dram_tensor("xT", [D, TC], F32, kind="ExternalInput").ap()
    out_d = nc.dram_tensor("out", [D, TC], F32, kind="ExternalOutput").ap()
    dbg = {}
    if DEBUG_TAPS:
        for i_ in range(1, 4):
            dbg[i_] = nc.dram_tensor(f"dbg{i_}", [D, TC], F32, kind="ExternalOutput").ap()
        dbgwu = nc.dram_tensor("dbgwu", [NCORES, 2, 256, E], BF16, kind="ExternalOutput").ap()
        dbgwd = nc.dram_tensor("dbgwd", [NCORES, 1024, D], BF16, kind="ExternalOutput").ap()
        dbgs = nc.dram_tensor("dbgs", [128, 24], F32, kind="ExternalOutput").ap()
    din = {}
    for n, shp, dt in [
        ("wq_sl", [256, D], F32), ("wk_sl", [256, D], F32), ("wv_sl", [256, D], F32),
        ("wo_sl", [256, D], F32), ("qkv_sl", [256, QKV_O], F32), ("wout_sl", [256, D], F32),
        ("wu_sl", [256, E], F32), ("wg_sl", [256, E], F32), ("wd_sl", [1024, D], F32),
        ("bq", [128, DC], F32), ("bk", [128, DC], F32), ("bv", [128, DC], F32),
        ("bo", [128, DC], F32), ("bqkv", [128, 48], F32), ("bout", [128, DC], F32),
        ("wmapqT", [128, MD], F32), ("wmapkT", [128, MD], F32),
        ("ident", [128, 128], BF16), ("masks", [128, 4 * TB], BF16),
        ("ohot", [128, 16 * H], BF16), ("ohr", [H, 128 * H], F32),
        ("hsel", [1, NIT], F32),
    ]:
        din[n] = nc.dram_tensor(n, shp, dt, kind="ExternalInput").ap()

    ag_a_in = nc.dram_tensor("ag_a_in", [4, 256, D], BF16, kind="Internal").ap()
    ag_a_out = nc.dram_tensor("ag_a_out", [NCORES, 4, 256, D], BF16, kind="Internal").ap()
    ag_b_in = nc.dram_tensor("ag_b_in", [256, QKV_O + D], BF16, kind="Internal").ap()
    ag_b_out = nc.dram_tensor("ag_b_out", [NCORES, 256, QKV_O + D], BF16, kind="Internal").ap()
    ag_ug_in = nc.dram_tensor("ag_ug_in", [2, 256, E], BF16, kind="Internal").ap()
    ag_ug_out = nc.dram_tensor("ag_ug_out", [NCORES, 2, 256, E], BF16, kind="Internal").ap()
    ag_dn_in = nc.dram_tensor("ag_dn_in", [1024, D], BF16, kind="Internal").ap()
    ag_dn_out = nc.dram_tensor("ag_dn_out", [NCORES, 1024, D], BF16, kind="Internal").ap()
    ar_sc_in = nc.dram_tensor("ar_sc_in", [1, 4], F32, kind="Internal").ap()
    ar_sc_out = nc.dram_tensor("ar_sc_out", [1, 4], F32, kind="Internal").ap()
    ar_kv_in = nc.dram_tensor("ar_kv_in", [MD, H * DH + H], F32, kind="Internal").ap()
    ar_kv_out = nc.dram_tensor("ar_kv_out", [MD, H * DH + H], F32, kind="Internal").ap()
    ar_bd_in = nc.dram_tensor("ar_bd_in", [1, 2], F32, kind="Internal").ap()
    ar_bd_out = nc.dram_tensor("ar_bd_out", [1, 2], F32, kind="Internal").ap()
    ag_k_in = nc.dram_tensor("ag_k_in", [2, H, 128, TB], BF16, kind="Internal").ap()
    ag_k_out_t = nc.dram_tensor("ag_k_out", [NCORES, 2, H, 128, TB], BF16, kind="Internal")
    ag_v_in = nc.dram_tensor("ag_v_in", [2, H, 2, 128, 256], BF16, kind="Internal").ap()
    agv_gat_t = nc.dram_tensor("agv_gat", [NCORES, 2, H, 2, 128, 256], BF16, kind="Internal")
    RG = [list(range(NCORES))]

    with contextlib.ExitStack() as st:
        sb = st.enter_context
        x_res = sb(nc.sbuf_tensor("x_res", [128, DC * TC], F32)).ap()
        xhat = sb(nc.sbuf_tensor("xhat", [128, DC * TC], BF16)).ap()
        ones_b = sb(nc.sbuf_tensor("ones_b", [128, 1], BF16)).ap()
        onesr_f = sb(nc.sbuf_tensor("onesr_f", [1, 128], F32)).ap()
        ones_f = sb(nc.sbuf_tensor("ones_f", [128, 1], F32)).ap()
        csb = {}
        for n in ["bq", "bk", "bv", "bo", "bqkv", "bout", "ident",
                  "ohot", "hsel"]:
            shp = list(din[n].tensor.shape)
            csb[n] = sb(nc.sbuf_tensor("c_" + n, shp, din[n].tensor.dtype)).ap()
        wmapq_b = sb(nc.sbuf_tensor("wmapq_b", [128, MD], BF16)).ap()
        wmapk_b = sb(nc.sbuf_tensor("wmapk_b", [128, MD], BF16)).ap()
        qparams = sb(nc.sbuf_tensor("qparams", [128, 9], F32)).ap()
        qrow = sb(nc.sbuf_tensor("qrow", [1, 9], F32)).ap()
        negc = sb(nc.sbuf_tensor("negc", [128, 1], F32)).ap()
        biasv = sb(nc.sbuf_tensor("biasv", [128, NIT], F32)).ap()
        scl = sb(nc.sbuf_tensor("scl", [1, 16], F32)).ap()
        nmax = sb(nc.sbuf_tensor("nmax", [1, 4], F32)).ap()
        absacc = sb(nc.sbuf_tensor("absacc", [128, 24], F32)).ap()
        scl24 = sb(nc.sbuf_tensor("scl24", [1, 24], F32)).ap()
        spc = sb(nc.sbuf_tensor("spc", [128, 8], F32)).ap()
        block = st.enter_context(nc.Block())

        sems = {}
        C = {}
        _dummies = [sb(nc.semaphore(f"DUMMY{i_}")) for i_ in range(8)]
        for n in ["SY", "G", "S2", "CC", "MM", "VE", "SC",
                  "QL0", "QL1", "QS0", "QS1", "WL0", "WL1", "UG0", "UG1",
                  "DN0", "DN1", "KV0", "KV1", "KO0", "KO1"]:
            sems[n] = sb(nc.semaphore(n))
            C[n] = 0
        SY, G, S2, CCs, MM, VE, SC = (sems[n] for n in ["SY", "G", "S2", "CC", "MM", "VE", "SC"])

        def tick(name, ins, by):
            ins.then_inc(sems[name], by)
            C[name] += by
            return C[name]

        ve = lambda ins: tick("VE", ins, 1)
        scc = lambda ins: tick("SC", ins, 1)
        mmi = lambda ins: tick("MM", ins, 1)
        sy = lambda ins: tick("SY", ins, 16)
        gp = lambda ins: tick("G", ins, 16)
        s2i = lambda ins: tick("S2", ins, 16)
        cci = lambda ins: tick("CC", ins, 1)

        def vspace(v):
            ve(v.tensor_copy(spc[:, 0:4], spc[:, 4:8]))
        dmai = lambda name, ins: tick(name, ins, 16)

        ALL_SEMS = ["SY", "G", "S2", "QS0", "QS1", "QL0", "QL1", "WL0", "WL1",
                    "UG0", "UG1", "DN0", "DN1", "KV0", "KV1", "KO0", "KO1",
                    "MM", "VE", "SC"]

        def wait_snap(e, names):
            for n in names:
                if C[n] > 0:
                    e.wait_ge(sems[n], C[n])

        def tap(idx):
            if not DEBUG_TAPS or idx not in dbg:
                return
            tgt = dict(C)

            @block.scalar
            def _(s):
                for n in ["MM", "VE", "SC"]:
                    if tgt[n]:
                        s.wait_ge(sems[n], tgt[n])
                srcv = x_res.rearrange("p (c t) -> p c t", t=TC)
                dstv = dbg[idx].rearrange("(c p) t -> p c t", p=128)
                for q in range(4):
                    s2i(s.dma_start(dstv[:, 4 * q:4 * (q + 1), :], srcv[:, 4 * q:4 * (q + 1), :]))

        def barrier(engines=("tensor", "vector", "scalar")):
            tgt = dict(C)

            def waits(e):
                for n in ["SY", "G", "S2", "CC", "MM", "VE", "SC"]:
                    if tgt[n] > 0:
                        e.wait_ge(sems[n], tgt[n])
            for name in engines:
                getattr(block, name)(waits)

        @block.sync
        def _(sync):
            src = xT.rearrange("(c p) t -> p c t", p=128)
            dst = x_res.rearrange("p (c t) -> p c t", t=TC)
            for q in range(4):
                sy(sync.dma_start(dst[:, 4 * q:4 * (q + 1), :], src[:, 4 * q:4 * (q + 1), :]))
            for n in csb:
                sy(sync.dma_start(csb[n][:, :], din[n][:, :]))

        @block.gpsimd
        def _(g):
            gp(g.dma_start(wmapq_b[:, :], din["wmapqT"][:, :]))
            gp(g.dma_start(wmapk_b[:, :], din["wmapkT"][:, :]))
        G_MAPS = C["G"]

        @block.vector
        def _(v):
            ve(v.memset(ones_b[:, :], 1.0))
            ve(v.memset(spc[:, :], 0.0))
            ve(v.memset(scl[:, :], 0.0))
            ve(v.memset(nmax[:, :], 0.0))
            ve(v.memset(onesr_f[:, :], 1.0))
            ve(v.memset(ones_f[:, :], 1.0))

        with contextlib.ExitStack() as p0:
            qld = p0.enter_context(nc.sbuf_tensor("qld", [128, 2 * D], F32)).ap()
            qab = p0.enter_context(nc.sbuf_tensor("qab", [128, 2 * D], F32)).ap()
            qa = p0.enter_context(nc.sbuf_tensor("qa", [128, D], BF16)).ap()
            qb2 = p0.enter_context(nc.sbuf_tensor("qb2", [128, D], BF16)).ap()
            qout = p0.enter_context(nc.sbuf_tensor("qout", [128, 2 * D], BF16)).ap()
            ps_t = p0.enter_context(nc.psum_tensor("ps_t", [1, 24], F32)).ap()
            ps_bc9 = p0.enter_context(nc.psum_tensor("ps_bc9", [128, 16], F32)).ap()

            cast_a = []
            for mi, n in enumerate(["wq_sl", "wk_sl", "wv_sl", "wo_sl"]):
                for ch in range(2):
                    cast_a.append((din[n][ch * 128:(ch + 1) * 128, :],
                                   ag_a_in[mi, ch * 128:(ch + 1) * 128, :]))
            st_done, ld_done = [], []
            for j, (src, dst) in enumerate(cast_a):
                par = j % 2

                @block.gpsimd
                def _(g):
                    if j == 0:
                        g.wait_ge(G, G_MAPS)
                    else:
                        g.wait_ge(sems["QS0"], st_done[j - 1])
                    ld_done.append(gp(g.dma_start(qout[:, par * D:(par + 1) * D], src)))

                @block.sync
                def _(sync):
                    sync.wait_ge(G, ld_done[j])
                    st_done.append(dmai("QS0", sync.dma_start(dst, qout[:, par * D:(par + 1) * D])))
            QS_AGA = C["QS0"]

            @block.gpsimd
            def _(g):
                g.wait_ge(sems["QS0"], QS_AGA)
                cci(g.collective_compute("AllGather", ALU.bypass, replica_groups=RG,
                                         ins=[ag_a_in[:, :, :]], outs=[ag_a_out[:, :, :, :]]))
            CC_AGA = C["CC"]

            ffn_pieces = []
            for mi, n in enumerate(["wu_sl", "wg_sl"]):
                for l in range(2):
                    for cp in range(4):
                        ffn_pieces.append((mi, din[n][l * 128:(l + 1) * 128, cp * D:(cp + 1) * D],
                                           ag_ug_in[mi, l * 128:(l + 1) * 128, cp * D:(cp + 1) * D]))
            for ch in range(8):
                ffn_pieces.append((2, din["wd_sl"][ch * 128:(ch + 1) * 128, :],
                                   ag_dn_in[ch * 128:(ch + 1) * 128, :]))

            abs_ld, abs_sc, abs_ve = [], [], []
            for j, (mi, src, _dst) in enumerate(ffn_pieces):
                par = j % 2

                @block.sync
                def _(sync):
                    if j >= 2:
                        sync.wait_ge(SC, abs_sc[j - 2])
                    abs_ld.append(dmai("QL%d" % par, sync.dma_start(qld[:, par * D:(par + 1) * D], src)))

                @block.scalar
                def _(s):
                    s.wait_ge(sems["QL%d" % par], abs_ld[j])
                    if j >= 2:
                        s.wait_ge(VE, abs_ve[j - 2])
                    scc(s.activation(qab[:, par * D:(par + 1) * D],
                                     qld[:, par * D:(par + 1) * D], ACT.Abs))
                    abs_sc.append(scc(s.activation(qab[:, par * D:(par + 1) * D],
                                                   qab[:, par * D:(par + 1) * D],
                                                   ACT.Copy, bias=-C0_ABS[mi])))

                @block.vector
                def _(v):
                    v.wait_ge(SC, abs_sc[j])
                    ve(v.reduce_sum(absacc[:, j:j + 1], qab[:, par * D:(par + 1) * D],
                                    axis=AX))
                    abs_ve.append(C["VE"])
            VE_ABS = C["VE"]

            @block.tensor
            def _(t):
                t.wait_ge(VE, VE_ABS)
                for mi in range(3):
                    mmi(t.matmul(ps_t[:1, mi * 8:(mi + 1) * 8], ones_f[:, :],
                                 absacc[:, mi * 8:(mi + 1) * 8], start=True, stop=True))
            MM_ABS = C["MM"]

            @block.scalar
            def _(s):
                s.wait_ge(MM, MM_ABS)
                scc(s.copy(scl24[:, :], ps_t[:1, 0:24]))
            SC_S24 = C["SC"]

            @block.vector
            def _(v):
                v.wait_ge(SC, SC_S24)
                for mi in range(3):
                    ve(v.reduce_sum(scl[:, mi:mi + 1], scl24[:1, mi * 8:(mi + 1) * 8], axis=AX))
                vspace(v)
                vspace(v)
                ve(v.tensor_scalar(scl[:, 0:3], scl[:1, 0:3], 1.0 / (E * D), None, ALU.mult))
            VE_SCP = C["VE"]

            @block.scalar
            def _(s):
                s.wait_ge(VE, VE_SCP)
                s2i(s.dma_start(ar_sc_in[:, :], scl[:, 0:4]))
            S2_SC = C["S2"]

            @block.gpsimd
            def _(g):
                g.wait_ge(S2, S2_SC)
                cci(g.collective_compute("AllReduce", ALU.add, replica_groups=RG,
                                         ins=[ar_sc_in[:, :]], outs=[ar_sc_out[:, :]]))
                g.wait_ge(CCs, C["CC"])
                gp(g.dma_start(scl[:, 4:8], ar_sc_out[:, :]))
            G_SCL = C["G"]

            @block.vector
            def _(v):
                v.wait_ge(G, G_SCL)
                for mi in range(3):
                    c0e = C0_ABS[mi] + FFN_EPS
                    ve(v.tensor_scalar(qrow[:, 6 + mi:7 + mi], scl[:, 4 + mi:5 + mi],
                                       1.0, c0e, ALU.mult, ALU.add))
                    ve(v.tensor_scalar(qrow[:, 0 + mi:1 + mi], scl[:, 4 + mi:5 + mi],
                                       0.5, 0.5 * c0e, ALU.mult, ALU.add))
                    ve(v.tensor_scalar(qrow[:, 3 + mi:4 + mi], scl[:, 4 + mi:5 + mi],
                                       -0.5, -0.5 * c0e, ALU.mult, ALU.add))
            VE_QROW = C["VE"]

            @block.tensor
            def _(t):
                t.wait_ge(VE, VE_QROW)
                mmi(t.matmul(ps_bc9[:, 0:9], onesr_f[:, :], qrow[:1, :], start=True, stop=True))
            MM_QP = C["MM"]

            @block.vector
            def _(v):
                v.wait_ge(MM, MM_QP)
                ve(v.tensor_copy(qparams[:, :], ps_bc9[:, 0:9]))
            VE_QP = C["VE"]

            q_ld, q_ve, q_st = [], [], []
            SY_UG = None
            for j, (mi, src, dst) in enumerate(ffn_pieces):
                par = j % 2

                @block.sync
                def _(sync):
                    if j >= 2:
                        sync.wait_ge(VE, q_ve[j - 2])
                    q_ld.append(dmai("QL%d" % par, sync.dma_start(qld[:, par * D:(par + 1) * D], src)))

                @block.vector
                def _(v):
                    v.wait_ge(sems["QL%d" % par], q_ld[j])
                    if j == 0:
                        v.wait_ge(VE, VE_QP)
                    if j >= 2:
                        v.wait_ge(sems["QS%d" % par], q_st[j - 2])
                    ve(v.tensor_scalar(qa[:, :], qld[:, par * D:(par + 1) * D],
                                       qparams[:, mi:mi + 1], qparams[:, 6 + mi:7 + mi],
                                       ALU.is_gt, ALU.mult))
                    ve(v.tensor_scalar(qb2[:, :], qld[:, par * D:(par + 1) * D],
                                       qparams[:, 3 + mi:4 + mi], qparams[:, 6 + mi:7 + mi],
                                       ALU.is_lt, ALU.mult))
                    q_ve.append(ve(v.tensor_sub(qout[:, par * D:(par + 1) * D], qa[:, :], qb2[:, :])))

                @block.sync
                def _(sync):
                    sync.wait_ge(VE, q_ve[j])
                    q_st.append(dmai("QS%d" % par, sync.dma_start(dst, qout[:, par * D:(par + 1) * D])))
                if j == 15:
                    QS_UG = (C["QS0"], C["QS1"])
            QS_DN = (C["QS0"], C["QS1"])

            @block.gpsimd
            def _(g):
                g.wait_ge(sems["QS0"], QS_UG[0])
                g.wait_ge(sems["QS1"], QS_UG[1])
                cci(g.collective_compute("AllGather", ALU.bypass, replica_groups=RG,
                                         ins=[ag_ug_in[:, :, :]], outs=[ag_ug_out[:, :, :, :]]))
                g.wait_ge(sems["QS0"], QS_DN[0])
                g.wait_ge(sems["QS1"], QS_DN[1])
                cci(g.collective_compute("AllGather", ALU.bypass, replica_groups=RG,
                                         ins=[ag_dn_in[:, :]], outs=[ag_dn_out[:, :, :]]))
            CC_UG = C["CC"] - 1
            CC_DN = C["CC"]

            cast_b = []
            for l in range(2):
                for cp in range(3):
                    cast_b.append((din["qkv_sl"][l * 128:(l + 1) * 128, cp * D:(cp + 1) * D],
                                   ag_b_in[l * 128:(l + 1) * 128, cp * D:(cp + 1) * D]))
            for l in range(2):
                cast_b.append((din["wout_sl"][l * 128:(l + 1) * 128, :],
                               ag_b_in[l * 128:(l + 1) * 128, QKV_O:QKV_O + D]))
            st_b, ld_b = [], []
            G_PRE_B = C["G"]
            for j, (src, dst) in enumerate(cast_b):
                par = j % 2

                @block.gpsimd
                def _(g):
                    if j == 0:
                        g.wait_ge(G, G_PRE_B)
                    else:
                        g.wait_ge(sems["QS0"], st_b[j - 1])
                    ld_b.append(gp(g.dma_start(qout[:, par * D:(par + 1) * D], src)))

                @block.sync
                def _(sync):
                    sync.wait_ge(G, ld_b[j])
                    st_b.append(dmai("QS0", sync.dma_start(dst, qout[:, par * D:(par + 1) * D])))
            QS_AGB = C["QS0"]

            @block.gpsimd
            def _(g):
                g.wait_ge(sems["QS0"], QS_AGB)
                cci(g.collective_compute("AllGather", ALU.bypass, replica_groups=RG,
                                         ins=[ag_b_in[:, :]], outs=[ag_b_out[:, :, :]]))
            CC_AGB = C["CC"]

        def emit_ln():
            with contextlib.ExitStack() as ls:
                pm_s = ls.enter_context(nc.psum_tensor(nm("pm_s"), [1, TC], F32)).ap()
                pm_q = ls.enter_context(nc.psum_tensor(nm("pm_q"), [1, TC], F32)).ap()
                pm_mb = ls.enter_context(nc.psum_tensor(nm("pm_mb"), [128, TC], F32)).ap()
                pm_rb = ls.enter_context(nc.psum_tensor(nm("pm_rb"), [128, TC], F32)).ap()
                xb_t = ls.enter_context(nc.sbuf_tensor(nm("xb_t"), [128, 2 * TC], BF16)).ap()
                sq_t = ls.enter_context(nc.sbuf_tensor(nm("sq_t"), [128, 2 * TC], BF16)).ap()
                lnm = ls.enter_context(nc.sbuf_tensor(nm("lnm"), [1, TC], F32)).ap()
                lnv = ls.enter_context(nc.sbuf_tensor(nm("lnv"), [1, TC], F32)).ap()
                lnw = ls.enter_context(nc.sbuf_tensor(nm("lnw"), [1, TC], F32)).ap()
                ftmp = ls.enter_context(nc.sbuf_tensor(nm("ftmp"), [128, TB], F32)).ap()
                cast_ve, chunk_mm = [], []
                for c in range(DC):
                    par = c % 2

                    @block.vector
                    def _(v):
                        if c == 0:
                            wait_snap(v, ALL_SEMS)
                        if c >= 2:
                            v.wait_ge(MM, chunk_mm[c - 2])
                        ve(v.tensor_copy(xb_t[:, par * TC:(par + 1) * TC],
                                         x_res[:, c * TC:(c + 1) * TC]))
                        cast_ve.append(ve(v.tensor_mul(sq_t[:, par * TC:(par + 1) * TC],
                                                       xb_t[:, par * TC:(par + 1) * TC],
                                                       xb_t[:, par * TC:(par + 1) * TC])))

                    @block.tensor
                    def _(t):
                        t.wait_ge(VE, cast_ve[c])
                        for th in range(2):
                            t.matmul(pm_s[:1, th * TB:(th + 1) * TB], ones_b[:, :],
                                     xb_t[:, par * TC + th * TB:par * TC + (th + 1) * TB],
                                     start=(c == 0), stop=(c == DC - 1))
                        ins = None
                        for th in range(2):
                            ins = t.matmul(pm_q[:1, th * TB:(th + 1) * TB], ones_b[:, :],
                                           sq_t[:, par * TC + th * TB:par * TC + (th + 1) * TB],
                                           start=(c == 0), stop=(c == DC - 1))
                        chunk_mm.append(mmi(ins))
                MM_STAT = C["MM"]

                @block.vector
                def _(v):
                    v.wait_ge(MM, MM_STAT)
                    ve(v.tensor_scalar(lnm[:, :], pm_s[:1, :], 1.0 / D, None, ALU.mult))
                    ve(v.tensor_scalar(lnv[:, :], pm_q[:1, :], 1.0 / D, LN_EPS, ALU.mult, ALU.add))
                    ve(v.tensor_mul(lnw[:, :], lnm[:, :], lnm[:, :]))
                    ve(v.tensor_sub(lnv[:, :], lnv[:, :], lnw[:, :]))
                    ve(v.reciprocal(lnv[:, :], lnv[:, :]))
                VE_FIN = C["VE"]

                @block.scalar
                def _(s):
                    s.wait_ge(VE, VE_FIN)
                    scc(s.activation(lnw[:, :], lnv[:, :], ACT.Sqrt))
                SC_RSTD = C["SC"]

                @block.tensor
                def _(t):
                    t.wait_ge(SC, SC_RSTD)
                    ins = None
                    for th in range(2):
                        t.matmul(pm_mb[:, th * TB:(th + 1) * TB], onesr_f[:, :],
                                 lnm[:1, th * TB:(th + 1) * TB], start=True, stop=True)
                    for th in range(2):
                        ins = t.matmul(pm_rb[:, th * TB:(th + 1) * TB], onesr_f[:, :],
                                       lnw[:1, th * TB:(th + 1) * TB], start=True, stop=True)
                    mmi(ins)
                MM_BC = C["MM"]

                @block.vector
                def _(v):
                    v.wait_ge(MM, MM_BC)
                    for c in range(DC):
                        for th in range(2):
                            ve(v.tensor_sub(ftmp[:, :], x_res[:, c * TC + th * TB:c * TC + (th + 1) * TB],
                                            pm_mb[:, th * TB:(th + 1) * TB]))
                            ve(v.tensor_mul(xhat[:, c * TC + th * TB:c * TC + (th + 1) * TB],
                                            ftmp[:, :], pm_rb[:, th * TB:(th + 1) * TB]))

        def xhat_slice(dc, th):
            return xhat[:, dc * TC + th * TB:dc * TC + (th + 1) * TB]

        def emit_proj(O, src_slice_fn, mat_slice_fn, evac_fn, wait_cc=None, post_fn=None):
            n_og = O // TB
            with contextlib.ExitStack() as pscope:
                wstr = pscope.enter_context(
                    nc.sbuf_tensor(nm("wstr"), [128, 2 * DC * TB], BF16)).ap()
                pp = [[pscope.enter_context(nc.psum_tensor(nm("pp"), [128, TB], F32)).ap()
                       for _t in range(2)] for _pa in range(2)]
                ld, last_use, free_at = [], [], []
                for og in range(n_og):
                    @block.sync
                    def _(sync):
                        if og >= 2:
                            sync.wait_ge(MM, last_use[4 * (og - 2) + 3])
                        if og == 0 and wait_cc is not None:
                            sync.wait_ge(CCs, wait_cc)
                        if og == 0:
                            wait_snap(sync, ALL_SEMS)
                        par = og % 2
                        dst4 = wstr.rearrange("q (b r l c) -> q b r l c", b=2, r=8, c=TB)
                        for r in range(8):
                            src = mat_slice_fn(r, og * TB)
                            dmai("WL%d" % par, sync.dma_start(dst4[:, par, r, :, :], src))
                        ld.append(C["WL%d" % par])
                    for j in range(4):
                        oc = og * 4 + j
                        pa = oc % 2

                        @block.tensor
                        def _(t):
                            t.wait_ge(sems["WL%d" % (og % 2)], ld[og])
                            if oc == 0:
                                wait_snap(t, ALL_SEMS)
                            if oc >= 2:
                                for th in range(2):
                                    snme, val = free_at[2 * (oc - 2) + th]
                                    t.wait_ge(sems[snme], val)
                            ins = None
                            for th in range(2):
                                for dc in range(DC):
                                    ins = t.matmul(
                                        pp[pa][th],
                                        wstr[:, (og % 2) * DC * TB + dc * TB + j * 128:
                                             (og % 2) * DC * TB + dc * TB + (j + 1) * 128],
                                        src_slice_fn(dc, th),
                                        start=(dc == 0), stop=(dc == DC - 1))
                                mmi(ins)
                        last_use.append(C["MM"])
                        for th in range(2):
                            free_at.append(evac_fn(oc, th, pp[pa][th], C["MM"] - 1 + th))
                        if post_fn is not None:
                            post_fn(oc)
                tgt = dict(C)

                @block.tensor
                def _(t):
                    for nme in ["VE", "SC"]:
                        if tgt[nme]:
                            t.wait_ge(sems[nme], tgt[nme])

        def mk_mat_a(mi):
            def f(r, o0):
                return ag_a_out[r, mi].rearrange("(l p) c -> p l c", p=128)[:, :, o0:o0 + TB]
            return f

        def mk_mat_b(col0):
            def f(r, o0):
                return ag_b_out[r].rearrange("(l p) c -> p l c", p=128)[:, :, col0 + o0:col0 + o0 + TB]
            return f

        if PH_ATTA:
            emit_ln()
            with contextlib.ExitStack() as pa:
                qprime = pa.enter_context(nc.sbuf_tensor("qprime", [MD, H * TC], BF16)).ap()
                kvac = pa.enter_context(nc.sbuf_tensor("kvac", [MD, H * DH], F32)).ap()
                ksum = pa.enter_context(nc.sbuf_tensor("ksum", [MD, H], F32)).ap()
                hh_sb = pa.enter_context(nc.sbuf_tensor("hh_sb", [128, 2 * TC], BF16)).ap()

                with contextlib.ExitStack() as pkv:
                    ktm = pkv.enter_context(nc.sbuf_tensor("ktm", [128, H * 8 * MD], BF16)).ap()
                    kp_sb = pkv.enter_context(nc.sbuf_tensor("kp_sb", [MD, 2 * TC], BF16)).ap()
                    el_t = pkv.enter_context(nc.sbuf_tensor("el_t", [MD, 2 * TB], F32)).ap()
                    el_e = pkv.enter_context(nc.sbuf_tensor("el_e", [MD, 2 * TB], F32)).ap()
                    vtm_sb = pkv.enter_context(nc.sbuf_tensor("vtm_sb", [128, 2 * TC], BF16)).ap()
                    pm_map = [pkv.enter_context(nc.psum_tensor(nm("pmap"), [MD, TB], F32)).ap()
                              for _t in range(2)]
                    ps_tr = pkv.enter_context(nc.psum_tensor("ps_tr", [128, TC], BF16)).ap()
                    ps_kv = pkv.enter_context(nc.psum_tensor("ps_kv", [MD, DH], F32)).ap()

                    def evac_simple(bias_name, dstbuf):
                        def f(oc, th, ps, mmv):
                            @block.vector
                            def _(v):
                                v.wait_ge(MM, mmv)
                                ve(v.tensor_scalar(
                                    dstbuf[:, (oc % 2) * TC + th * TB:(oc % 2) * TC + (th + 1) * TB],
                                    ps, csb[bias_name][:, oc:oc + 1], None, ALU.add))
                            return ("VE", C["VE"])
                        return f

                    last_elu = [0]

                    def post_map_elu(oc, wmap, outbuf, do_ksum):
                        h, par = oc, oc % 2
                        vbase = C["VE"]

                        @block.tensor
                        def _(t):
                            t.wait_ge(VE, vbase)
                            t.wait_ge(VE, last_elu[0])
                            t.wait_ge(G, G_MAPS)
                            for th in range(2):
                                mmi(t.matmul(pm_map[th], wmap,
                                             hh_sb[:, par * TC + th * TB:par * TC + (th + 1) * TB],
                                             start=True, stop=True))
                        mm_map = C["MM"]

                        @block.vector
                        def _(v):
                            v.wait_ge(MM, mm_map)
                            for th in range(2):
                                ve(v.tensor_scalar(el_t[:, th * TB:(th + 1) * TB],
                                                   pm_map[th], 0.0, None, ALU.min))
                        ve_min = C["VE"]

                        @block.scalar
                        def _(s):
                            s.wait_ge(VE, ve_min)
                            scc(s.activation(el_e[:, :], el_t[:, :], ACT.Exp))
                        sc_exp = C["SC"]

                        @block.vector
                        def _(v):
                            v.wait_ge(SC, sc_exp)
                            for th in range(2):
                                ve(v.scalar_tensor_tensor(
                                    outbuf[:, (h if outbuf is qprime else par) * TC + th * TB:
                                           (h if outbuf is qprime else par) * TC + (th + 1) * TB],
                                    pm_map[th], 0.0, el_e[:, th * TB:(th + 1) * TB],
                                    ALU.max, ALU.add))
                            if do_ksum:
                                ve(v.reduce_sum(ksum[:, h:h + 1],
                                                kp_sb[:, par * TC:(par + 1) * TC], axis=AX))
                        last_elu[0] = C["VE"]

                    def post_q(oc):
                        post_map_elu(oc, wmapq_b[:, :], qprime, False)
                    emit_proj(D, xhat_slice, mk_mat_a(0), evac_simple("bq", hh_sb),
                              wait_cc=CC_AGA, post_fn=post_q)

                    ktr_done = []

                    def post_k(oc):
                        h, par = oc, oc % 2
                        post_map_elu(oc, wmapk_b[:, :], kp_sb, True)
                        ve_kp = C["VE"]

                        @block.tensor
                        def _(t):
                            t.wait_ge(VE, ve_kp)
                            ins = None
                            for j2 in range(8):
                                ins = t.transpose(ps_tr[:, j2 * MD:(j2 + 1) * MD],
                                                  kp_sb[:MD, par * TC + j2 * 128:par * TC + (j2 + 1) * 128],
                                                  csb["ident"][:MD, :MD])
                            mmi(ins)
                        mm_tr = C["MM"]

                        @block.vector
                        def _(v):
                            v.wait_ge(MM, mm_tr)
                            ve(v.tensor_copy(ktm[:, h * 8 * MD:(h + 1) * 8 * MD], ps_tr[:, 0:8 * MD]))
                        ktr_done.append(C["VE"])
                    emit_proj(D, xhat_slice, mk_mat_a(1), evac_simple("bk", hh_sb), post_fn=post_k)

                    def post_v(oc):
                        h, par = oc, oc % 2
                        vbase = C["VE"]

                        @block.tensor
                        def _(t):
                            t.wait_ge(VE, vbase)
                            ins = None
                            for j2 in range(8):
                                ins = t.transpose(ps_tr[:, j2 * 128:(j2 + 1) * 128],
                                                  hh_sb[:, par * TC + j2 * 128:par * TC + (j2 + 1) * 128],
                                                  csb["ident"][:, :])
                            mmi(ins)
                        mm_vt = C["MM"]

                        @block.vector
                        def _(v):
                            v.wait_ge(MM, mm_vt)
                            ve(v.tensor_copy(vtm_sb[:, par * TC:(par + 1) * TC], ps_tr[:, :]))
                        ve_vt = C["VE"]

                        @block.tensor
                        def _(t):
                            t.wait_ge(VE, ve_vt)
                            ins = None
                            for j2 in range(8):
                                ins = t.matmul(ps_kv,
                                               ktm[:, h * 8 * MD + j2 * MD:h * 8 * MD + (j2 + 1) * MD],
                                               vtm_sb[:, par * TC + j2 * 128:par * TC + (j2 + 1) * 128],
                                               start=(j2 == 0), stop=(j2 == 7))
                            mmi(ins)
                        mm_kv = C["MM"]

                        @block.vector
                        def _(v):
                            v.wait_ge(MM, mm_kv)
                            ve(v.tensor_copy(kvac[:, h * DH:(h + 1) * DH], ps_kv))
                    emit_proj(D, xhat_slice, mk_mat_a(2), evac_simple("bv", hh_sb), post_fn=post_v)

                VE_KVALL = C["VE"]

                @block.scalar
                def _(s):
                    s.wait_ge(VE, VE_KVALL)
                    s2i(s.dma_start(ar_kv_in[:, 0:H * DH], kvac[:, :]))
                    s2i(s.dma_start(ar_kv_in[:, H * DH:H * DH + H], ksum[:, :]))
                S2_KV = C["S2"]

                @block.gpsimd
                def _(g):
                    g.wait_ge(S2, S2_KV)
                    cci(g.collective_compute("AllReduce", ALU.add, replica_groups=RG,
                                             ins=[ar_kv_in[:, :]], outs=[ar_kv_out[:, :]]))
                    g.wait_ge(CCs, C["CC"])
                    gp(g.dma_start(kvac[:, :], ar_kv_out[:, 0:H * DH]))
                    gp(g.dma_start(ksum[:, :], ar_kv_out[:, H * DH:H * DH + H]))
                G_KVR = C["G"]

                with contextlib.ExitStack() as pt:
                    kvr_bf = pt.enter_context(nc.sbuf_tensor("kvr_bf", [MD, H * DH], BF16)).ap()
                    ksr_bf = pt.enter_context(nc.sbuf_tensor("ksr_bf", [MD, H], BF16)).ap()
                    rn_sb = pt.enter_context(nc.sbuf_tensor("rn_sb", [1, TC], F32)).ap()
                    rb_sb = pt.enter_context(nc.sbuf_tensor("rb_sb", [128, TC], F32)).ap()
                    ps_n = pt.enter_context(nc.psum_tensor("ps_n", [1, TC], F32)).ap()
                    ps_rb = pt.enter_context(nc.psum_tensor("ps_rb", [128, TC], F32)).ap()
                    ps_o = pt.enter_context(nc.psum_tensor("ps_o", [128, TC], F32)).ap()

                    @block.vector
                    def _(v):
                        v.wait_ge(G, G_KVR)
                        ve(v.tensor_copy(kvr_bf[:, :], kvac[:, :]))
                        ve(v.tensor_copy(ksr_bf[:, :], ksum[:, :]))
                    VE_KVBF = C["VE"]

                    for h in range(H):
                        vpre = C["VE"]

                        @block.tensor
                        def _(t):
                            t.wait_ge(VE, max(VE_KVBF, vpre))
                            ins = None
                            for th in range(2):
                                ins = t.matmul(ps_n[:1, th * TB:(th + 1) * TB],
                                               ksr_bf[:, h:h + 1],
                                               qprime[:, h * TC + th * TB:h * TC + (th + 1) * TB],
                                               start=True, stop=True)
                            mmi(ins)
                        mm_n = C["MM"]

                        @block.vector
                        def _(v):
                            v.wait_ge(MM, mm_n)
                            ve(v.tensor_scalar(rn_sb[:, :], ps_n[:1, :], ATTA_EPS, None, ALU.add))
                            ve(v.reciprocal(rn_sb[:, :], rn_sb[:, :]))
                        ve_rn = C["VE"]

                        @block.tensor
                        def _(t):
                            t.wait_ge(VE, ve_rn)
                            ins = None
                            for th in range(2):
                                t.matmul(ps_rb[:, th * TB:(th + 1) * TB], onesr_f[:, :],
                                         rn_sb[:1, th * TB:(th + 1) * TB], start=True, stop=True)
                            for th in range(2):
                                ins = t.matmul(ps_o[:, th * TB:(th + 1) * TB],
                                               kvr_bf[:, h * DH:(h + 1) * DH],
                                               qprime[:, h * TC + th * TB:h * TC + (th + 1) * TB],
                                               start=True, stop=True)
                            mmi(ins)
                        mm_o = C["MM"]

                        @block.vector
                        def _(v):
                            v.wait_ge(MM, mm_o)
                            ve(v.tensor_copy(rb_sb[:, :], ps_rb[:, :]))
                            ve(v.tensor_mul(xhat[:, h * TC:(h + 1) * TC], ps_o[:, :], rb_sb[:, :]))

                def evac_wo(oc, th, ps, mmv):
                    @block.vector
                    def _(v):
                        v.wait_ge(MM, mmv)
                        ve(v.scalar_tensor_tensor(
                            x_res[:, oc * TC + th * TB:oc * TC + (th + 1) * TB],
                            ps, csb["bo"][:, oc:oc + 1],
                            x_res[:, oc * TC + th * TB:oc * TC + (th + 1) * TB],
                            ALU.add, ALU.add))
                    return ("VE", C["VE"])
                emit_proj(D, xhat_slice, mk_mat_a(3), evac_wo)
            tap(1)
            barrier()

        def emit_ffn():
            emit_ln()
            with contextlib.ExitStack() as pf:
                wu_b = pf.enter_context(nc.sbuf_tensor(nm("wu_b"), [128, 2 * 4096], BF16)).ap()
                wg_b = pf.enter_context(nc.sbuf_tensor(nm("wg_b"), [128, 2 * 4096], BF16)).ap()
                wd_b = pf.enter_context(nc.sbuf_tensor(nm("wd_b"), [128, 2 * 2048], BF16)).ap()
                hid = pf.enter_context(nc.sbuf_tensor(nm("hid"), [128, 2 * 8 * TC], BF16)).ap()
                sil = pf.enter_context(nc.sbuf_tensor(nm("sil"), [128, TB], F32)).ap()
                ps_u = [pf.enter_context(nc.psum_tensor(nm("ps_u"), [128, TB], F32)).ap()
                        for _t in range(2)]
                ps_g = [pf.enter_context(nc.psum_tensor(nm("ps_g"), [128, TB], F32)).ap()
                        for _t in range(2)]
                ps_d = [pf.enter_context(nc.psum_tensor(nm("ps_d"), [128, TB], F32)).ap()
                        for _t in range(4)]

                ug_ld, ug_use, dn_ld, dn_use = {}, {}, {}, {}
                ev_ug, dn_ev, hid_ve = {}, {}, {}

                def load_ug(g, pc):
                    @block.sync
                    def _(sync):
                        if (g, pc) == (0, 0):
                            sync.wait_ge(CCs, CC_UG)
                            wait_snap(sync, ALL_SEMS)
                        prev = (g * 4 + pc) - 2
                        if prev >= 0:
                            sync.wait_ge(MM, ug_use[(prev // 4, prev % 4)])
                        par = pc % 2
                        e0 = g * 1024 + pc * 256
                        for buf, m in ((wu_b, 0), (wg_b, 1)):
                            dst = buf.rearrange("q (b r l e) -> q b r l e", b=2, r=8, e=256)
                            for r in range(8):
                                src = ag_ug_out[r, m].rearrange("(l p) e -> p l e", p=128)[:, :, e0:e0 + 256]
                                dmai("UG%d" % par, sync.dma_start(dst[:, par, r, :, :], src))
                        ug_ld[(g, pc)] = C["UG%d" % par]

                def load_dn(g, inst):
                    @block.sync
                    def _(sync):
                        if (g, inst) == (0, 0):
                            sync.wait_ge(CCs, CC_DN)
                            wait_snap(sync, ALL_SEMS)
                        prev = (g * 8 + inst) - 2
                        if prev >= 0:
                            sync.wait_ge(MM, dn_use[(prev // 8, prev % 8)])
                        par = inst % 2
                        dst = wd_b.rearrange("q (b ec d) -> q b ec d", b=2, d=256)
                        src = ag_dn_out[g].rearrange("(ec p) d -> p ec d", p=128)[:, :, inst * 256:(inst + 1) * 256]
                        dmai("DN%d" % par, sync.dma_start(dst[:, par, :, :], src))
                        dn_ld[(g, inst)] = C["DN%d" % par]

                for g in range(8):
                    for pc in range(4):
                        load_ug(g, pc)
                        par = pc % 2
                        for e2 in range(2):
                            ecl = pc * 2 + e2
                            eidx = g * 8 + ecl

                            @block.tensor
                            def _(t):
                                t.wait_ge(sems["UG%d" % (pc % 2)], ug_ld[(g, pc)])
                                if eidx == 0:
                                    wait_snap(t, ALL_SEMS)
                                if eidx >= 1:
                                    psc, pve = ev_ug[eidx - 1]
                                    t.wait_ge(SC, psc)
                                    t.wait_ge(VE, pve)
                                ins = None
                                for th in range(2):
                                    for dc in range(DC):
                                        t.matmul(ps_u[th],
                                                 wu_b[:, par * 4096 + dc * 256 + e2 * 128:
                                                      par * 4096 + dc * 256 + (e2 + 1) * 128],
                                                 xhat_slice(dc, th),
                                                 start=(dc == 0), stop=(dc == DC - 1))
                                for th in range(2):
                                    for dc in range(DC):
                                        ins = t.matmul(ps_g[th],
                                                       wg_b[:, par * 4096 + dc * 256 + e2 * 128:
                                                            par * 4096 + dc * 256 + (e2 + 1) * 128],
                                                       xhat_slice(dc, th),
                                                       start=(dc == 0), stop=(dc == DC - 1))
                                mmi(ins)
                            ug_use[(g, pc)] = C["MM"]
                            mm_ug = C["MM"]

                            for th in range(2):
                                ve_now = C["VE"]

                                @block.scalar
                                def _(s):
                                    s.wait_ge(MM, mm_ug)
                                    s.wait_ge(VE, ve_now)
                                    scc(s.activation(sil[:, :], ps_u[th], ACT.Silu))
                                sc_sil = C["SC"]

                                @block.vector
                                def _(v):
                                    v.wait_ge(SC, sc_sil)
                                    ve(v.tensor_mul(
                                        hid[:, (g % 2) * 8 * TC + ecl * TC + th * TB:
                                            (g % 2) * 8 * TC + ecl * TC + (th + 1) * TB],
                                        ps_g[th], sil[:, :]))
                            ev_ug[eidx] = (C["SC"], C["VE"])
                    hid_ve[g] = C["VE"]

                    for inst in range(8):
                        load_dn(g, inst)

                        @block.tensor
                        def _(t):
                            t.wait_ge(sems["DN%d" % (inst % 2)], dn_ld[(g, inst)])
                            t.wait_ge(VE, hid_ve[g])
                            prev = (g * 8 + inst) - 1
                            if prev >= 0:
                                t.wait_ge(VE, dn_ev[(prev // 8, prev % 8)])
                            for dl in range(2):
                                for th in range(2):
                                    ins = None
                                    for ec in range(8):
                                        ins = t.matmul(
                                            ps_d[dl * 2 + th],
                                            wd_b[:, (inst % 2) * 2048 + ec * 256 + dl * 128:
                                                 (inst % 2) * 2048 + ec * 256 + (dl + 1) * 128],
                                            hid[:, (g % 2) * 8 * TC + ec * TC + th * TB:
                                                (g % 2) * 8 * TC + ec * TC + (th + 1) * TB],
                                            start=(ec == 0), stop=(ec == 7))
                                    mmi(ins)
                        dn_use[(g, inst)] = C["MM"]
                        mm_d = C["MM"]

                        @block.vector
                        def _(v):
                            v.wait_ge(MM, mm_d)
                            for dl in range(2):
                                dcc = inst * 2 + dl
                                for th in range(2):
                                    ve(v.scalar_tensor_tensor(
                                        x_res[:, dcc * TC + th * TB:dcc * TC + (th + 1) * TB],
                                        ps_d[dl * 2 + th], 1.0,
                                        x_res[:, dcc * TC + th * TB:dcc * TC + (th + 1) * TB],
                                        ALU.mult, ALU.add))
                        dn_ev[(g, inst)] = C["VE"]
                tgt = dict(C)

                @block.tensor
                def _(t):
                    t.wait_ge(VE, tgt["VE"])
            barrier()

        if PH_FFN1:
            emit_ffn()
            tap(2)

        if PH_ATTB:
            emit_ln()
            with contextlib.ExitStack() as pb:
                q_sb = pb.enter_context(nc.sbuf_tensor("q_sb", [128, H * TC], BF16)).ap()
                masks_sb = pb.enter_context(nc.sbuf_tensor("masks_sb", [128, 4 * TB], BF16)).ap()
                ohr_sb = pb.enter_context(nc.sbuf_tensor("ohr_sb", [H, 128 * H], F32)).ap()

                @block.sync
                def _(sync):
                    wait_snap(sync, ALL_SEMS)
                    sy(sync.dma_start(masks_sb[:, :], din["masks"][:, :]))
                    sy(sync.dma_start(ohr_sb[:, :], din["ohr"][:, :]))
                SY_MASKS = C["SY"]

                with contextlib.ExitStack() as pk:
                    ktmp = pk.enter_context(nc.sbuf_tensor("ktmp", [128, 2 * TC], BF16)).ap()
                    sqn = pk.enter_context(nc.sbuf_tensor("sqn", [128, TC], BF16)).ap()
                    ps_nrm = pk.enter_context(nc.psum_tensor("ps_nrm", [1, TC], F32)).ap()
                    ps_vtr = pk.enter_context(nc.psum_tensor("ps_vtr", [128, TC], BF16)).ap()
                    kdma, vdma = [], []

                    def post_norm(buf, off_fn, h, slot):
                        vbase = C["VE"]

                        @block.vector
                        def _(v):
                            ve(v.tensor_mul(sqn[:, :], buf[:, off_fn:off_fn + TC],
                                            buf[:, off_fn:off_fn + TC]))
                        ve_sq = C["VE"]

                        @block.tensor
                        def _(t):
                            t.wait_ge(VE, ve_sq)
                            ins = None
                            for th in range(2):
                                ins = t.matmul(ps_nrm[:1, th * TB:(th + 1) * TB], ones_b[:, :],
                                               sqn[:, th * TB:(th + 1) * TB], start=True, stop=True)
                            mmi(ins)
                        mm_nr = C["MM"]

                        @block.vector
                        def _(v):
                            v.wait_ge(MM, mm_nr)
                            ve(v.reduce_max(nmax[:, 2 + slot:3 + slot], ps_nrm[:1, :], axis=AX))
                            vspace(v)
                            vspace(v)
                            if h == 0:
                                ve(v.tensor_copy(nmax[:, slot:slot + 1], nmax[:, 2 + slot:3 + slot]))
                            else:
                                ve(v.tensor_max(nmax[:, slot:slot + 1], nmax[:, slot:slot + 1],
                                                nmax[:, 2 + slot:3 + slot]))
                            vspace(v)

                    def evac_k(oc, th, ps, mmv):
                        h = oc

                        @block.vector
                        def _(v):
                            v.wait_ge(MM, mmv)
                            if h >= 2 and th == 0:
                                v.wait_ge(sems["KO%d" % (h % 2)], kdma[h - 2])
                            ve(v.tensor_scalar(
                                ktmp[:, (h % 2) * TC + th * TB:(h % 2) * TC + (th + 1) * TB],
                                ps, csb["bqkv"][:, oc + 16:oc + 17], None, ALU.add))
                        return ("VE", C["VE"])

                    def post_kp(oc):
                        h, par = oc, oc % 2
                        post_norm(ktmp, par * TC, h, 1)
                        ve_k = C["VE"]

                        @block.scalar
                        def _(s):
                            s.wait_ge(VE, ve_k)
                            dmai("KO%d" % par, s.dma_start(ag_k_in[0, h], ktmp[:, par * TC:par * TC + TB]))
                            dmai("KO%d" % par, s.dma_start(ag_k_in[1, h], ktmp[:, par * TC + TB:par * TC + TC]))
                        kdma.append(C["KO%d" % par])

                    def srcK(dc, th):
                        return xhat_slice(dc, th)
                    emit_proj(D, srcK, mk_mat_b(16 * 128), evac_k, wait_cc=CC_AGB, post_fn=post_kp)

                    def evac_v2(oc, th, ps, mmv):
                        h = oc

                        @block.vector
                        def _(v):
                            v.wait_ge(MM, mmv)
                            if h >= 2 and th == 0:
                                v.wait_ge(sems["KO%d" % (h % 2)], vdma[h - 2])
                            ve(v.tensor_scalar(
                                ktmp[:, (h % 2) * TC + th * TB:(h % 2) * TC + (th + 1) * TB],
                                ps, csb["bqkv"][:, oc + 32:oc + 33], None, ALU.add))
                        return ("VE", C["VE"])

                    def post_vp(oc):
                        h, par = oc, oc % 2
                        vbase = C["VE"]

                        @block.tensor
                        def _(t):
                            t.wait_ge(VE, vbase)
                            ins = None
                            for j2 in range(8):
                                ins = t.transpose(ps_vtr[:, j2 * 128:(j2 + 1) * 128],
                                                  ktmp[:, par * TC + j2 * 128:par * TC + (j2 + 1) * 128],
                                                  csb["ident"][:, :])
                            mmi(ins)
                        mm_vt = C["MM"]

                        @block.vector
                        def _(v):
                            v.wait_ge(MM, mm_vt)
                            ve(v.tensor_copy(ktmp[:, par * TC:(par + 1) * TC], ps_vtr[:, :]))
                        ve_vc = C["VE"]

                        @block.scalar
                        def _(s):
                            s.wait_ge(VE, ve_vc)
                            for ab in range(2):
                                for tp in range(2):
                                    dmai("KO%d" % par, s.dma_start(
                                        ag_v_in[ab, h, tp],
                                        ktmp[:, par * TC + ab * TB + tp * 256:
                                             par * TC + ab * TB + (tp + 1) * 256]))
                        vdma.append(C["KO%d" % par])
                    emit_proj(D, srcK, mk_mat_b(32 * 128), evac_v2, post_fn=post_vp)

                    def evac_q2(oc, th, ps, mmv):
                        @block.vector
                        def _(v):
                            v.wait_ge(MM, mmv)
                            ve(v.tensor_scalar(
                                q_sb[:, oc * TC + th * TB:oc * TC + (th + 1) * TB],
                                ps, csb["bqkv"][:, oc:oc + 1], INV_SQRT_DH, ALU.add, ALU.mult))
                        return ("VE", C["VE"])

                    def post_qp(oc):
                        post_norm(q_sb, oc * TC, oc, 0)
                    emit_proj(D, srcK, mk_mat_b(0), evac_q2, post_fn=post_qp)
                VE_NORMS = C["VE"]
                C_KO = (C["KO0"], C["KO1"])

                @block.scalar
                def _(s):
                    s.wait_ge(VE, VE_NORMS)
                    s2i(s.dma_start(ar_bd_in[:, :], nmax[:, 0:2]))
                S2_BD = C["S2"]

                @block.gpsimd
                def _(g):
                    g.wait_ge(S2, S2_BD)
                    g.wait_ge(sems["KO0"], C_KO[0])
                    g.wait_ge(sems["KO1"], C_KO[1])
                    cci(g.collective_compute("AllReduce", ALU.max, replica_groups=RG,
                                             ins=[ar_bd_in[:, :]], outs=[ar_bd_out[:, :]]))
                    g.wait_ge(CCs, C["CC"])
                    gp(g.dma_start(nmax[:, 0:2], ar_bd_out[:, :]))
                    cci(g.collective_compute("AllGather", ALU.bypass, replica_groups=RG,
                                             ins=[ag_k_in[:, :, :, :]],
                                             outs=[ag_k_out_t.ap()[:, :, :, :, :]]))
                    cci(g.collective_compute("AllGather", ALU.bypass, replica_groups=RG,
                                             ins=[ag_v_in[:, :, :, :, :]],
                                             outs=[agv_gat_t.ap()[:, :, :, :, :, :]]))
                CC_BD = C["CC"] - 2
                CC_K = C["CC"] - 1
                CC_V = C["CC"]
                G_BD = C["G"]

                @block.vector
                def _(v):
                    v.wait_ge(G, G_BD)
                    ve(v.tensor_mul(nmax[:, 2:3], nmax[:, 0:1], nmax[:, 1:2]))
                    vspace(v)
                VE_C1 = C["VE"]

                @block.scalar
                def _(s):
                    s.wait_ge(VE, VE_C1)
                    scc(s.activation(nmax[:, 3:4], nmax[:, 2:3], ACT.Sqrt))
                SC_C = C["SC"]

                @block.vector
                def _(v):
                    v.wait_ge(SC, SC_C)
                    ve(v.tensor_scalar_min(nmax[:, 3:4], nmax[:, 3:4], 20.0))
                    vspace(v)
                    vspace(v)
                    ve(v.tensor_scalar(nmax[:, 2:3], nmax[:, 3:4], -1.0, None, ALU.mult))
                    vspace(v)
                VE_NC = C["VE"]

                with contextlib.ExitStack() as pc2:
                    ps_bv = pc2.enter_context(nc.psum_tensor("ps_bv", [128, NIT], F32)).ap()
                    ps_ng = pc2.enter_context(nc.psum_tensor("ps_ng", [128, 1], F32)).ap()

                    @block.tensor
                    def _(t):
                        t.wait_ge(VE, VE_NC)
                        mmi(t.matmul(ps_ng, onesr_f[:, :], nmax[:1, 2:3], start=True, stop=True))
                        mmi(t.matmul(ps_bv, onesr_f[:, :], csb["hsel"][:1, :], start=True, stop=True))
                    MM_BV = C["MM"]

                    @block.vector
                    def _(v):
                        v.wait_ge(MM, MM_BV)
                        ve(v.tensor_copy(negc[:, :], ps_ng))
                        vspace(v)
                        vspace(v)
                        ve(v.tensor_scalar(biasv[:, :], ps_bv, negc[:, 0:1], None, ALU.add))
                VE_BV = C["VE"]

                with contextlib.ExitStack() as pat:
                    attacc = pat.enter_context(nc.sbuf_tensor("attacc", [128, H * TB], F32)).ap()
                    kbuf = pat.enter_context(nc.sbuf_tensor("kbuf", [128, 2 * 4 * TB], BF16)).ap()
                    vbuf = pat.enter_context(nc.sbuf_tensor("vbuf", [128, 2 * 4 * TB], BF16)).ap()
                    abuf = pat.enter_context(nc.sbuf_tensor("abuf", [128, 4 * TB], BF16)).ap()
                    sums_sb = pat.enter_context(nc.sbuf_tensor("sums_sb", [H, TB], F32)).ap()
                    ps_sc = pat.enter_context(nc.psum_tensor("ps_sc", [128, 4 * TB], F32)).ap()
                    ps_av = [pat.enter_context(nc.psum_tensor(nm("ps_av"), [128, TB], F32)).ap()
                             for _t in range(2)]
                    ps_sm = pat.enter_context(nc.psum_tensor("ps_sm", [H, TB], F32)).ap()

                    it_tab = [(0, 0, 0)] + [(0, 1, r) for r in range(8)] \
                        + [(1, 0, 1)] + [(1, 1, r) for r in range(8)] + [(1, 2, r) for r in range(8)]
                    assert len(it_tab) == NIT

                    kv_ld, kv_use = {}, {}
                    sc_free = []
                    av_mm = []
                    for itg, (tqb, kind, rank) in enumerate(it_tab):
                        local_first = (itg in (0, 9))
                        if local_first:
                            @block.vector
                            def _(v):
                                wait_snap(v, ALL_SEMS)
                                ve(v.memset(attacc[:, :], 0.0))
                            acc_zero = C["VE"]
                            av_ev = []
                            n_base = itg
                        def kv_load(hg):
                            gkey = itg * 4 + hg

                            @block.sync
                            def _(sync):
                                if gkey >= 2:
                                    sync.wait_ge(MM, kv_use[gkey - 2])
                                if gkey == 0:
                                    sync.wait_ge(CCs, CC_K)
                                    wait_snap(sync, ALL_SEMS)
                                if gkey == 1:
                                    sync.wait_ge(CCs, CC_V)
                                par = gkey % 2
                                if kind == 0:
                                    ab = tqb
                                    ksrc = ag_k_in[ab, 4 * hg:4 * (hg + 1)]
                                    vsrc = ag_v_in[ab, 4 * hg:4 * (hg + 1)]
                                else:
                                    ab = 0 if kind == 1 else 1
                                    ksrc = ag_k_out_t.ap()[rank, ab, 4 * hg:4 * (hg + 1)]
                                    vsrc = agv_gat_t.ap()[rank, ab, 4 * hg:4 * (hg + 1)]
                                kdst = kbuf.rearrange("q (b hh t) -> q b hh t", b=2, t=TB)
                                dmai("KV%d" % par, sync.dma_start(kdst[:, par, :, :],
                                                                  ksrc.rearrange("hh p t -> p hh t")))
                                vdst = vbuf.rearrange("q (b tp hh f) -> q b tp hh f",
                                                      b=2, tp=2, f=256)
                                for tp in range(2):
                                    dmai("KV%d" % par, sync.dma_start(
                                        vdst[:, par, tp, :, :],
                                        vsrc[:, tp].rearrange("hh p f -> p hh f")))
                                kv_ld[gkey] = C["KV%d" % par]

                        for h in range(H):
                            if h % 4 == 0:
                                kv_load(h // 4)
                            hg = h // 4
                            hh = h % 4
                            gkey = itg * 4 + hg
                            par = gkey % 2
                            idx = itg - n_base
                            lidx = len(av_ev)
                            fidx = len(sc_free)
                            kb = kbuf[:, par * 4 * TB + hh * TB:par * 4 * TB + (hh + 1) * TB]
                            vbase = par * 4 * TB
                            qs = q_sb[:, h * TC + tqb * TB:h * TC + (tqb + 1) * TB]

                            @block.tensor
                            def _(t):
                                t.wait_ge(sems["KV%d" % (gkey % 2)], kv_ld[gkey])
                                if lidx == 0:
                                    t.wait_ge(VE, max(VE_BV, acc_zero))
                                if fidx >= 1:
                                    psc, pve = sc_free[fidx - 1]
                                    t.wait_ge(SC, psc)
                                    if pve:
                                        t.wait_ge(VE, pve)
                                ins = None
                                for sub in range(2):
                                    ins = t.matmul(ps_sc[:, sub * TB:(sub + 1) * TB],
                                                   kb[:, sub * 128:(sub + 1) * 128], qs,
                                                   start=True, stop=True)
                                mmi(ins)
                                for sub in range(2, 4):
                                    ins = t.matmul(ps_sc[:, sub * TB:(sub + 1) * TB],
                                                   kb[:, sub * 128:(sub + 1) * 128], qs,
                                                   start=True, stop=True)
                                mmi(ins)
                            m_s = (C["MM"] - 1, C["MM"])

                            exp_sc = []
                            for half in range(2):
                                @block.scalar
                                def _(s, half=half):
                                    s.wait_ge(MM, m_s[half])
                                    if fidx >= 1 and half == 0:
                                        s.wait_ge(MM, av_mm[fidx - 1])
                                    exp_sc.append(scc(s.activation(
                                        abuf[:, half * TC:(half + 1) * TC],
                                        ps_sc[:, half * TC:(half + 1) * TC],
                                        ACT.Exp, bias=biasv[:, itg:itg + 1])))
                            mask_ve = 0
                            if kind == 0:
                                @block.vector
                                def _(v):
                                    v.wait_ge(SC, exp_sc[1])
                                    v.wait_ge(SY, SY_MASKS)
                                    ve(v.tensor_mul(abuf[:, :], abuf[:, :], masks_sb[:, :]))
                                mask_ve = C["VE"]
                            sc_free.append((exp_sc[1], mask_ve))

                            n_loc = (9 if tqb == 0 else 17) * H

                            @block.tensor
                            def _(t):
                                if mask_ve:
                                    t.wait_ge(VE, mask_ve)
                                else:
                                    t.wait_ge(SC, exp_sc[1])
                                if lidx >= 2:
                                    t.wait_ge(VE, av_ev[lidx - 2])
                                apar = lidx % 2
                                for sub in range(4):
                                    vo = vbase + (sub // 2) * 1024 + hh * 256 + (sub % 2) * 128
                                    t.matmul(ps_av[apar], vbuf[:, vo:vo + 128],
                                             abuf[:, sub * TB:(sub + 1) * TB],
                                             start=(sub == 0), stop=(sub == 3))
                                ins = None
                                for sub in range(4):
                                    ins = t.matmul(ps_sm,
                                                   csb["ohot"][:, 16 * h:16 * (h + 1)],
                                                   abuf[:, sub * TB:(sub + 1) * TB],
                                                   start=(lidx == 0 and sub == 0),
                                                   stop=(lidx == n_loc - 1 and sub == 3))
                                mmi(ins)
                            kv_use[gkey] = C["MM"]
                            av_mm.append(C["MM"])
                            mm_av = C["MM"]

                            @block.vector
                            def _(v):
                                v.wait_ge(MM, mm_av)
                                apar = lidx % 2
                                ve(v.scalar_tensor_tensor(
                                    attacc[:, h * TB:(h + 1) * TB], ps_av[apar], 1.0,
                                    attacc[:, h * TB:(h + 1) * TB], ALU.mult, ALU.add))
                            av_ev.append(C["VE"])

                        if itg in (8, 25):
                            MM_SM = C["MM"]

                            @block.vector
                            def _(v):
                                v.wait_ge(MM, MM_SM)
                                ve(v.tensor_scalar(sums_sb[:, :], ps_sm, float(2 ** 25),
                                                   None, ALU.mult))
                                vspace(v)
                                ve(v.reciprocal(sums_sb[:, :], sums_sb[:, :]))
                            VE_RS = C["VE"]

                            fin_ve = []
                            for h in range(H):
                                @block.tensor
                                def _(t, h=h):
                                    t.wait_ge(VE, VE_RS)
                                    if h >= 2:
                                        t.wait_ge(VE, fin_ve[h - 2])
                                    else:
                                        t.wait_ge(VE, av_ev[-1])
                                    if h == 0:
                                        t.wait_ge(SY, SY_MASKS)
                                    mmi(t.matmul(ps_av[h % 2], ohr_sb[:, 128 * h:128 * (h + 1)],
                                                 sums_sb[:, :], start=True, stop=True))
                                mm_rb = C["MM"]

                                @block.vector
                                def _(v, h=h, tqb=tqb, mm_rb=mm_rb):
                                    v.wait_ge(MM, mm_rb)
                                    ve(v.tensor_mul(xhat[:, h * TC + tqb * TB:h * TC + (tqb + 1) * TB],
                                                    attacc[:, h * TB:(h + 1) * TB], ps_av[h % 2]))
                                fin_ve.append(C["VE"])
                            tgt2 = dict(C)

                            @block.tensor
                            def _(t):
                                t.wait_ge(VE, tgt2["VE"])

                def evac_bo(oc, th, ps, mmv):
                    @block.vector
                    def _(v):
                        v.wait_ge(MM, mmv)
                        ve(v.scalar_tensor_tensor(
                            x_res[:, oc * TC + th * TB:oc * TC + (th + 1) * TB],
                            ps, csb["bout"][:, oc:oc + 1],
                            x_res[:, oc * TC + th * TB:oc * TC + (th + 1) * TB],
                            ALU.add, ALU.add))
                    return ("VE", C["VE"])

                def src_att(dc, th):
                    return xhat[:, dc * TC + th * TB:dc * TC + (th + 1) * TB]
                emit_proj(D, src_att, mk_mat_b(QKV_O), evac_bo)
            tap(3)
            barrier()

        if PH_FFN2:
            emit_ffn()

        tgt = dict(C)

        @block.scalar
        def _(s):
            for n in ["MM", "VE", "SC"]:
                if tgt[n]:
                    s.wait_ge(sems[n], tgt[n])
            src = x_res.rearrange("p (c t) -> p c t", t=TC)
            dst = out_d.rearrange("(c p) t -> p c t", p=128)
            for q in range(4):
                s2i(s.dma_start(dst[:, 4 * q:4 * (q + 1), :], src[:, 4 * q:4 * (q + 1), :]))
        S2_OUT = C["S2"]

        if DEBUG_TAPS:
            @block.sync
            def _(sync):
                sync.wait_ge(CCs, C["CC"])
                for r_ in range(NCORES):
                    sy(sync.dma_start(dbgwu[r_], ag_ug_out[r_]))
                    sy(sync.dma_start(dbgwd[r_], ag_dn_out[r_]))
                sy(sync.dma_start(dbgs[:, 0:9], qparams[:, :]))
                sy(sync.dma_start(dbgs[0:1, 9:16], scl[:, 0:7]))
                sy(sync.dma_start(dbgs[:, 16:20], absacc[:, 0:4]))
        SY_END = C["SY"]

        @block.sync
        def _(sync):
            sync.wait_ge(S2, S2_OUT)
            if DEBUG_TAPS:
                sync.wait_ge(SY, SY_END)

    return nc


def kernel(**inputs):
    in_maps = prepare_host(inputs)
    nc = build_nc()
    res = run_bass_kernel_spmd(nc, in_maps, core_ids=list(range(NCORES)))
    return assemble_host(res.results)


# version 13
# speedup vs baseline: 1.1175x; 1.1175x over previous
import contextlib
import numpy as np
import ml_dtypes
import concourse.bass as bass
import concourse.mybir as mybir
from concourse.bass_utils import run_bass_kernel_spmd

F32 = mybir.dt.float32
BF16 = mybir.dt.bfloat16
I32 = mybir.dt.int32
ALU = mybir.AluOpType
ACT = mybir.ActivationFunctionType
AX = mybir.AxisListType.X

NCORES = 8
S = 8192
D = 2048
DC = 16
H = 16
DH = 128
MD = 64
E = 8192
TB = 512
TC = 1024
QKV_O = 6144
INV_SQRT_DH = float(1.0 / np.sqrt(128.0))
LN_EPS = 1e-5
ATTA_EPS = 1e-6
FFN_EPS = 1e-8
C0_ABS = (0.017632, 0.017632, 0.008816)
NIT = 26

DEBUG_TAPS = False
PH_ATTA = True
PH_FFN1 = True
PH_ATTB = True
PH_FFN2 = True


def _bf(x):
    return np.ascontiguousarray(x).astype(ml_dtypes.bfloat16)


def _ct(x):
    return np.ascontiguousarray(x)


def prepare_host(inp):
    x2d = np.asarray(inp["L0_attn_a_hidden_states"], np.float32).reshape(S, D)
    wqT = _ct(np.asarray(inp["L0_attn_a_wq"], np.float32).T)
    wkT = _ct(np.asarray(inp["L0_attn_a_wk"], np.float32).T)
    wvT = _ct(np.asarray(inp["L0_attn_a_wv"], np.float32).T)
    woT = _ct(np.asarray(inp["L0_attn_a_wo"], np.float32).T)
    qkvT = _ct(np.asarray(inp["L0_attn_b_w_qkv"], np.float32).T)
    woutT = _ct(np.asarray(inp["L0_attn_b_w_out"], np.float32).T)
    wuT = _ct(np.asarray(inp["L0_ffn_W_up"], np.float32).T)
    wgT = _ct(np.asarray(inp["L0_ffn_W_gate"], np.float32).T)
    wdT = _ct(np.asarray(inp["L0_ffn_W_down"], np.float32).T)

    def re_bias(b, nch):
        return _ct(np.asarray(b, np.float32).reshape(nch, 128).T)

    biases = {
        "bq": re_bias(inp["L0_attn_a_bq"], DC), "bk": re_bias(inp["L0_attn_a_bk"], DC),
        "bv": re_bias(inp["L0_attn_a_bv"], DC), "bo": re_bias(inp["L0_attn_a_bo"], DC),
        "bqkv": re_bias(inp["L0_attn_b_b_qkv"], 48), "bout": re_bias(inp["L0_attn_b_b_out"], DC),
    }
    wmapqT = _ct(np.asarray(inp["L0_attn_a_wmap_q"], np.float32).T)
    wmapkT = _ct(np.asarray(inp["L0_attn_a_wmap_k"], np.float32).T)

    ident = _bf(np.eye(128, dtype=np.float32))
    pp = np.arange(128)[:, None]
    ff = np.arange(TB)[None, :]
    masks = _bf(np.concatenate([(sub * 128 + pp <= ff) for sub in range(4)], 1).astype(np.float32))
    oh = np.zeros((128, 16 * H), np.float32)
    for h in range(H):
        oh[:, 16 * h + h] = 1.0
    ohot = _bf(oh)
    ohr = np.zeros((H, 128 * H), np.float32)
    for h in range(H):
        ohr[h, 128 * h:128 * (h + 1)] = float(2 ** 25)

    in_maps = []
    for i in range(NCORES):
        xs = np.concatenate([x2d[i * TB:(i + 1) * TB], x2d[(15 - i) * TB:(16 - i) * TB]], 0)
        hsel = np.zeros(NIT, np.float32)
        for r in range(8):
            hsel[1 + r] = 0.0 if r < i else -200.0
            hsel[18 + r] = 0.0 if r > i else -200.0
        d = {
            "xT": _ct(xs.T),
            "wq_sl": _ct(wqT[256 * i:256 * (i + 1)]),
            "wk_sl": _ct(wkT[256 * i:256 * (i + 1)]),
            "wv_sl": _ct(wvT[256 * i:256 * (i + 1)]),
            "wo_sl": _ct(woT[256 * i:256 * (i + 1)]),
            "qkv_sl": _ct(qkvT[256 * i:256 * (i + 1)]),
            "wout_sl": _ct(woutT[256 * i:256 * (i + 1)]),
            "wu_sl": _ct(wuT[256 * i:256 * (i + 1)]),
            "wg_sl": _ct(wgT[256 * i:256 * (i + 1)]),
            "wd_sl": _ct(wdT[1024 * i:1024 * (i + 1)]),
            "wmapqT": wmapqT, "wmapkT": wmapkT,
            "ident": ident, "masks": masks, "ohot": ohot, "ohr": ohr,
            "hsel": hsel.reshape(1, NIT),
        }
        d.update(biases)
        in_maps.append(d)
    return in_maps


def assemble_host(results):
    y = np.empty((S, D), np.float32)
    for i in range(NCORES):
        o = results[i]["out"]
        y[i * TB:(i + 1) * TB] = o[:, :TB].T
        y[(15 - i) * TB:(16 - i) * TB] = o[:, TB:].T
    return y.reshape(1, S, D)


def build_nc():
    nc = bass.Bass("TRN2", target_bir_lowering=False, debug=False,
                   detect_race_conditions=False)
    uid = [0]

    def nm(base):
        uid[0] += 1
        return f"{base}_{uid[0]}"

    xT = nc.dram_tensor("xT", [D, TC], F32, kind="ExternalInput").ap()
    out_d = nc.dram_tensor("out", [D, TC], F32, kind="ExternalOutput").ap()
    dbg = {}
    if DEBUG_TAPS:
        for i_ in range(1, 4):
            dbg[i_] = nc.dram_tensor(f"dbg{i_}", [D, TC], F32, kind="ExternalOutput").ap()
        dbgwu = nc.dram_tensor("dbgwu", [NCORES, 2, 256, E], BF16, kind="ExternalOutput").ap()
        dbgwd = nc.dram_tensor("dbgwd", [NCORES, 1024, D], BF16, kind="ExternalOutput").ap()
        dbgs = nc.dram_tensor("dbgs", [128, 24], F32, kind="ExternalOutput").ap()
    din = {}
    for n, shp, dt in [
        ("wq_sl", [256, D], F32), ("wk_sl", [256, D], F32), ("wv_sl", [256, D], F32),
        ("wo_sl", [256, D], F32), ("qkv_sl", [256, QKV_O], F32), ("wout_sl", [256, D], F32),
        ("wu_sl", [256, E], F32), ("wg_sl", [256, E], F32), ("wd_sl", [1024, D], F32),
        ("bq", [128, DC], F32), ("bk", [128, DC], F32), ("bv", [128, DC], F32),
        ("bo", [128, DC], F32), ("bqkv", [128, 48], F32), ("bout", [128, DC], F32),
        ("wmapqT", [128, MD], F32), ("wmapkT", [128, MD], F32),
        ("ident", [128, 128], BF16), ("masks", [128, 4 * TB], BF16),
        ("ohot", [128, 16 * H], BF16), ("ohr", [H, 128 * H], F32),
        ("hsel", [1, NIT], F32),
    ]:
        din[n] = nc.dram_tensor(n, shp, dt, kind="ExternalInput").ap()

    ag_a_in = nc.dram_tensor("ag_a_in", [4, 256, D], BF16, kind="Internal").ap()
    ag_a_out = nc.dram_tensor("ag_a_out", [NCORES, 4, 256, D], BF16, kind="Internal").ap()
    ag_b_in = nc.dram_tensor("ag_b_in", [256, QKV_O + D], BF16, kind="Internal").ap()
    ag_b_out = nc.dram_tensor("ag_b_out", [NCORES, 256, QKV_O + D], BF16, kind="Internal").ap()
    ag_ug_in = nc.dram_tensor("ag_ug_in", [2, 256, E], BF16, kind="Internal").ap()
    ag_ug_out = nc.dram_tensor("ag_ug_out", [NCORES, 2, 256, E], BF16, kind="Internal").ap()
    ag_dn_in = nc.dram_tensor("ag_dn_in", [1024, D], BF16, kind="Internal").ap()
    ag_dn_out = nc.dram_tensor("ag_dn_out", [NCORES, 1024, D], BF16, kind="Internal").ap()
    ar_sc_in = nc.dram_tensor("ar_sc_in", [1, 4], F32, kind="Internal").ap()
    ar_sc_out = nc.dram_tensor("ar_sc_out", [1, 4], F32, kind="Internal").ap()
    ar_kv_in = nc.dram_tensor("ar_kv_in", [MD, H * DH + H], F32, kind="Internal").ap()
    ar_kv_out = nc.dram_tensor("ar_kv_out", [MD, H * DH + H], F32, kind="Internal").ap()
    ar_bd_in = nc.dram_tensor("ar_bd_in", [1, 2], F32, kind="Internal").ap()
    ar_bd_out = nc.dram_tensor("ar_bd_out", [1, 2], F32, kind="Internal").ap()
    ag_k_in = nc.dram_tensor("ag_k_in", [2, H, 128, TB], BF16, kind="Internal").ap()
    ag_k_out_t = nc.dram_tensor("ag_k_out", [NCORES, 2, H, 128, TB], BF16, kind="Internal")
    ag_v_in = nc.dram_tensor("ag_v_in", [2, H, 2, 128, 256], BF16, kind="Internal").ap()
    agv_gat_t = nc.dram_tensor("agv_gat", [NCORES, 2, H, 2, 128, 256], BF16, kind="Internal")
    RG = [list(range(NCORES))]

    with contextlib.ExitStack() as st:
        sb = st.enter_context
        cbuf = sb(nc.sbuf_tensor("cbuf", [128, 512], BF16)).ap()
        x_res = sb(nc.sbuf_tensor("x_res", [128, DC * TC], F32)).ap()
        xhat = sb(nc.sbuf_tensor("xhat", [128, DC * TC], BF16)).ap()
        ones_b = sb(nc.sbuf_tensor("ones_b", [128, 1], BF16)).ap()
        onesr_f = sb(nc.sbuf_tensor("onesr_f", [1, 128], F32)).ap()
        ones_f = sb(nc.sbuf_tensor("ones_f", [128, 1], F32)).ap()
        csb = {}
        for n in ["bq", "bk", "bv", "bo", "bqkv", "bout", "ident",
                  "ohot", "hsel"]:
            shp = list(din[n].tensor.shape)
            csb[n] = sb(nc.sbuf_tensor("c_" + n, shp, din[n].tensor.dtype)).ap()
        wmapq_b = sb(nc.sbuf_tensor("wmapq_b", [128, MD], BF16)).ap()
        wmapk_b = sb(nc.sbuf_tensor("wmapk_b", [128, MD], BF16)).ap()
        qparams = sb(nc.sbuf_tensor("qparams", [128, 9], F32)).ap()
        qrow = sb(nc.sbuf_tensor("qrow", [1, 9], F32)).ap()
        negc = sb(nc.sbuf_tensor("negc", [128, 1], F32)).ap()
        biasv = sb(nc.sbuf_tensor("biasv", [128, NIT], F32)).ap()
        scl = sb(nc.sbuf_tensor("scl", [1, 16], F32)).ap()
        nmax = sb(nc.sbuf_tensor("nmax", [1, 4], F32)).ap()
        absacc = sb(nc.sbuf_tensor("absacc", [128, 24], F32)).ap()
        scl24 = sb(nc.sbuf_tensor("scl24", [1, 24], F32)).ap()
        spc = sb(nc.sbuf_tensor("spc", [128, 8], BF16)).ap()
        block = st.enter_context(nc.Block())

        sems = {}
        C = {}
        _dummies = [sb(nc.semaphore(f"DUMMY{i_}")) for i_ in range(6)]
        for n in ["SY", "G", "S2", "CC", "MM", "VE", "SC",
                  "QL0", "QL1", "QS0", "QS1", "WL0", "WL1", "UG0", "UG1",
                  "DN0", "DN1", "KV0", "KV1", "KO0", "KO1", "CB0", "CB1"]:
            sems[n] = sb(nc.semaphore(n))
            C[n] = 0
        SY, G, S2, CCs, MM, VE, SC = (sems[n] for n in ["SY", "G", "S2", "CC", "MM", "VE", "SC"])

        def tick(name, ins, by):
            ins.then_inc(sems[name], by)
            C[name] += by
            return C[name]

        ve = lambda ins: tick("VE", ins, 1)
        scc = lambda ins: tick("SC", ins, 1)
        mmi = lambda ins: tick("MM", ins, 1)
        sy = lambda ins: tick("SY", ins, 16)
        gp = lambda ins: tick("G", ins, 16)
        s2i = lambda ins: tick("S2", ins, 16)
        cci = lambda ins: tick("CC", ins, 1)

        def vspace(v):
            ve(v.tensor_copy(spc[:, 0:4], spc[:, 4:8]))
        dmai = lambda name, ins: tick(name, ins, 16)

        ALL_SEMS = ["SY", "G", "S2", "QS0", "QS1", "QL0", "QL1", "WL0", "WL1",
                    "UG0", "UG1", "DN0", "DN1", "KV0", "KV1", "KO0", "KO1",
                    "MM", "VE", "SC"]

        def wait_snap(e, names):
            for n in names:
                if C[n] > 0:
                    e.wait_ge(sems[n], C[n])

        def tap(idx):
            if not DEBUG_TAPS or idx not in dbg:
                return
            tgt = dict(C)

            @block.scalar
            def _(s):
                for n in ["MM", "VE", "SC"]:
                    if tgt[n]:
                        s.wait_ge(sems[n], tgt[n])
                srcv = x_res.rearrange("p (c t) -> p c t", t=TC)
                dstv = dbg[idx].rearrange("(c p) t -> p c t", p=128)
                for q in range(4):
                    s2i(s.dma_start(dstv[:, 4 * q:4 * (q + 1), :], srcv[:, 4 * q:4 * (q + 1), :]))

        def barrier(engines=("tensor", "vector", "scalar")):
            tgt = dict(C)

            def waits(e):
                for n in ["SY", "G", "S2", "CC", "MM", "VE", "SC"]:
                    if tgt[n] > 0:
                        e.wait_ge(sems[n], tgt[n])
            for name in engines:
                getattr(block, name)(waits)

        @block.sync
        def _(sync):
            src = xT.rearrange("(c p) t -> p c t", p=128)
            dst = x_res.rearrange("p (c t) -> p c t", t=TC)
            for q in range(4):
                sy(sync.dma_start(dst[:, 4 * q:4 * (q + 1), :], src[:, 4 * q:4 * (q + 1), :]))
            for n in csb:
                sy(sync.dma_start(csb[n][:, :], din[n][:, :]))
        SY_X = C["SY"]

        @block.gpsimd
        def _(g):
            gp(g.dma_start(wmapq_b[:, :], din["wmapqT"][:, :]))
            gp(g.dma_start(wmapk_b[:, :], din["wmapkT"][:, :]))
        G_MAPS = C["G"]

        @block.vector
        def _(v):
            ve(v.memset(ones_b[:, :], 1.0))
            ve(v.memset(spc[:, :], 0.0))
            ve(v.memset(scl[:, :], 0.0))
            ve(v.memset(nmax[:, :], 0.0))
            ve(v.memset(onesr_f[:, :], 1.0))
            ve(v.memset(ones_f[:, :], 1.0))

        with contextlib.ExitStack() as p0:
            qld = p0.enter_context(nc.sbuf_tensor("qld", [128, 2 * D], F32)).ap()
            qab = p0.enter_context(nc.sbuf_tensor("qab", [128, 2 * D], F32)).ap()
            qa = p0.enter_context(nc.sbuf_tensor("qa", [128, D], BF16)).ap()
            qb2 = p0.enter_context(nc.sbuf_tensor("qb2", [128, D], BF16)).ap()
            qout = p0.enter_context(nc.sbuf_tensor("qout", [128, 2 * D], BF16)).ap()
            ps_t = p0.enter_context(nc.psum_tensor("ps_t", [1, 24], F32)).ap()
            ps_bc9 = p0.enter_context(nc.psum_tensor("ps_bc9", [128, 16], F32)).ap()

            cast_a = []
            for mi, n in enumerate(["wq_sl", "wk_sl", "wv_sl", "wo_sl"]):
                for ch in range(2):
                    cast_a.append((din[n][ch * 128:(ch + 1) * 128, :],
                                   ag_a_in[mi, ch * 128:(ch + 1) * 128, :]))
            st_done, ld_done = [], []
            for j, (src, dst) in enumerate(cast_a):
                par = j % 2

                @block.gpsimd
                def _(g):
                    if j == 0:
                        g.wait_ge(G, G_MAPS)
                    else:
                        g.wait_ge(sems["QS0"], st_done[j - 1])
                    ld_done.append(gp(g.dma_start(qout[:, par * D:(par + 1) * D], src)))

                @block.sync
                def _(sync):
                    sync.wait_ge(G, ld_done[j])
                    st_done.append(dmai("QS0", sync.dma_start(dst, qout[:, par * D:(par + 1) * D])))
            QS_AGA = C["QS0"]

            @block.gpsimd
            def _(g):
                g.wait_ge(sems["QS0"], QS_AGA)
                cci(g.collective_compute("AllGather", ALU.bypass, replica_groups=RG,
                                         ins=[ag_a_in[:, :, :]], outs=[ag_a_out[:, :, :, :]]))
            CC_AGA = C["CC"]

            ffn_pieces = []
            for mi, n in enumerate(["wu_sl", "wg_sl"]):
                for l in range(2):
                    for cp in range(4):
                        ffn_pieces.append((mi, din[n][l * 128:(l + 1) * 128, cp * D:(cp + 1) * D],
                                           ag_ug_in[mi, l * 128:(l + 1) * 128, cp * D:(cp + 1) * D]))
            for ch in range(8):
                ffn_pieces.append((2, din["wd_sl"][ch * 128:(ch + 1) * 128, :],
                                   ag_dn_in[ch * 128:(ch + 1) * 128, :]))

            abs_ld, abs_sc, abs_ve = [], [], []
            for j, (mi, src, _dst) in enumerate(ffn_pieces):
                par = j % 2

                @block.sync
                def _(sync):
                    if j >= 2:
                        sync.wait_ge(SC, abs_sc[j - 2])
                    abs_ld.append(dmai("QL%d" % par, sync.dma_start(qld[:, par * D:(par + 1) * D], src)))

                @block.scalar
                def _(s):
                    s.wait_ge(sems["QL%d" % par], abs_ld[j])
                    if j >= 2:
                        s.wait_ge(VE, abs_ve[j - 2])
                    scc(s.activation(qab[:, par * D:(par + 1) * D],
                                     qld[:, par * D:(par + 1) * D], ACT.Abs))
                    abs_sc.append(scc(s.activation(qab[:, par * D:(par + 1) * D],
                                                   qab[:, par * D:(par + 1) * D],
                                                   ACT.Copy, bias=-C0_ABS[mi])))

                @block.vector
                def _(v):
                    v.wait_ge(SC, abs_sc[j])
                    ve(v.reduce_sum(absacc[:, j:j + 1], qab[:, par * D:(par + 1) * D],
                                    axis=AX))
                    abs_ve.append(C["VE"])
            VE_ABS = C["VE"]

            @block.tensor
            def _(t):
                t.wait_ge(VE, VE_ABS)
                for mi in range(3):
                    mmi(t.matmul(ps_t[:1, mi * 8:(mi + 1) * 8], ones_f[:, :],
                                 absacc[:, mi * 8:(mi + 1) * 8], start=True, stop=True))
            MM_ABS = C["MM"]

            @block.scalar
            def _(s):
                s.wait_ge(MM, MM_ABS)
                scc(s.copy(scl24[:, :], ps_t[:1, 0:24]))
            SC_S24 = C["SC"]

            @block.vector
            def _(v):
                v.wait_ge(SC, SC_S24)
                for mi in range(3):
                    ve(v.reduce_sum(scl[:, mi:mi + 1], scl24[:1, mi * 8:(mi + 1) * 8], axis=AX))
                vspace(v)
                vspace(v)
                ve(v.tensor_scalar(scl[:, 0:3], scl[:1, 0:3], 1.0 / (E * D), None, ALU.mult))
            VE_SCP = C["VE"]

            @block.scalar
            def _(s):
                s.wait_ge(VE, VE_SCP)
                s2i(s.dma_start(ar_sc_in[:, :], scl[:, 0:4]))
            S2_SC = C["S2"]

            @block.gpsimd
            def _(g):
                g.wait_ge(S2, S2_SC)
                cci(g.collective_compute("AllReduce", ALU.add, replica_groups=RG,
                                         ins=[ar_sc_in[:, :]], outs=[ar_sc_out[:, :]]))
                g.wait_ge(CCs, C["CC"])
                gp(g.dma_start(scl[:, 4:8], ar_sc_out[:, :]))
            G_SCL = C["G"]

            @block.scalar
            def _(s):
                s.wait_ge(VE, 6)
                scc(s.activation(scl[:, 8:9], ones_f[0:1, 0:1], ACT.Copy, bias=1e-10))
                scc(s.activation(scl[:, 11:12], ones_f[0:1, 0:1], ACT.Copy))
                scc(s.activation(scl[:, 9:10], scl[:, 8:9], ACT.Copy, scale=1e10, bias=-1e10))
                scc(s.activation(scl[:, 11:12], ones_f[0:1, 0:1], ACT.Copy))
                scc(s.activation(scl[:, 10:11], scl[:, 9:10], ACT.Copy, scale=1.8e-7,
                                 bias=0.5 - 1.8e-7))
            SC_HF = C["SC"]

            @block.vector
            def _(v):
                v.wait_ge(G, G_SCL)
                v.wait_ge(SC, SC_HF)
                for mi in range(3):
                    c0e = C0_ABS[mi] + FFN_EPS
                    ve(v.tensor_scalar(qrow[:, 6 + mi:7 + mi], scl[:, 4 + mi:5 + mi],
                                       1.0, c0e, ALU.mult, ALU.add))
                vspace(v)
                vspace(v)
                for mi in range(3):
                    ve(v.tensor_scalar(qrow[:, 0 + mi:1 + mi], qrow[:, 6 + mi:7 + mi],
                                       scl[:1, 10:11], None, ALU.mult))
                vspace(v)
                vspace(v)
                for mi in range(3):
                    ve(v.tensor_scalar(qrow[:, 3 + mi:4 + mi], qrow[:, 0 + mi:1 + mi],
                                       -1.0, None, ALU.mult))
            VE_QROW = C["VE"]

            @block.tensor
            def _(t):
                t.wait_ge(VE, VE_QROW)
                mmi(t.matmul(ps_bc9[:, 0:9], onesr_f[:, :], qrow[:1, :], start=True, stop=True))
            MM_QP = C["MM"]

            @block.vector
            def _(v):
                v.wait_ge(MM, MM_QP)
                ve(v.tensor_copy(qparams[:, :], ps_bc9[:, 0:9]))
            VE_QP = C["VE"]

            q_ld, q_ve, q_st = [], [], []
            SY_UG = None
            for j, (mi, src, dst) in enumerate(ffn_pieces):
                par = j % 2

                @block.sync
                def _(sync):
                    if j >= 2:
                        sync.wait_ge(VE, q_ve[j - 2])
                    q_ld.append(dmai("QL%d" % par, sync.dma_start(qld[:, par * D:(par + 1) * D], src)))

                @block.vector
                def _(v):
                    v.wait_ge(sems["QL%d" % par], q_ld[j])
                    if j == 0:
                        v.wait_ge(VE, VE_QP)
                    if j >= 2:
                        v.wait_ge(sems["QS%d" % par], q_st[j - 2])
                    ve(v.tensor_scalar(qa[:, :], qld[:, par * D:(par + 1) * D],
                                       qparams[:, mi:mi + 1], qparams[:, 6 + mi:7 + mi],
                                       ALU.is_gt, ALU.mult))
                    ve(v.tensor_scalar(qb2[:, :], qld[:, par * D:(par + 1) * D],
                                       qparams[:, 3 + mi:4 + mi], qparams[:, 6 + mi:7 + mi],
                                       ALU.is_lt, ALU.mult))
                    q_ve.append(ve(v.tensor_sub(qout[:, par * D:(par + 1) * D], qa[:, :], qb2[:, :])))

                @block.sync
                def _(sync):
                    sync.wait_ge(VE, q_ve[j])
                    q_st.append(dmai("QS%d" % par, sync.dma_start(dst, qout[:, par * D:(par + 1) * D])))
                if j == 15:
                    QS_UG = (C["QS0"], C["QS1"])
            QS_DN = (C["QS0"], C["QS1"])
            VE_QE = C["VE"]

            @block.gpsimd
            def _(g):
                g.wait_ge(sems["QS0"], QS_UG[0])
                g.wait_ge(sems["QS1"], QS_UG[1])
                cci(g.collective_compute("AllGather", ALU.bypass, replica_groups=RG,
                                         ins=[ag_ug_in[:, :, :]], outs=[ag_ug_out[:, :, :, :]]))
                g.wait_ge(sems["QS0"], QS_DN[0])
                g.wait_ge(sems["QS1"], QS_DN[1])
                cci(g.collective_compute("AllGather", ALU.bypass, replica_groups=RG,
                                         ins=[ag_dn_in[:, :]], outs=[ag_dn_out[:, :, :]]))
            CC_UG = C["CC"] - 1
            CC_DN = C["CC"]

        def emit_cast_b():
            cast_b = []
            for l in range(2):
                for cp in range(3):
                    cast_b.append((din["qkv_sl"][l * 128:(l + 1) * 128, cp * D:(cp + 1) * D],
                                   ag_b_in[l * 128:(l + 1) * 128, cp * D:(cp + 1) * D]))
            for l in range(2):
                cast_b.append((din["wout_sl"][l * 128:(l + 1) * 128, :],
                               ag_b_in[l * 128:(l + 1) * 128, QKV_O:QKV_O + D]))
            st_b, ld_b = [], []
            jj = 0
            for (srcp, dstp) in cast_b:
                for q in range(4):
                    j = jj
                    src = srcp[:, q * 512:(q + 1) * 512]
                    dst = dstp[:, q * 512:(q + 1) * 512]

                    @block.gpsimd
                    def _(g, j=j, src=src):
                        if j > 0:
                            g.wait_ge(sems["CB1"], st_b[j - 1])
                        ld_b.append(dmai("CB0", g.dma_start(cbuf[:, :], src)))

                    @block.sync
                    def _(sync, j=j, dst=dst):
                        sync.wait_ge(sems["CB0"], ld_b[j])
                        st_b.append(dmai("CB1", sync.dma_start(dst, cbuf[:, :])))
                    jj += 1
            QS_AGB = C["CB1"]

            @block.gpsimd
            def _(g):
                g.wait_ge(sems["CB1"], QS_AGB)
                cci(g.collective_compute("AllGather", ALU.bypass, replica_groups=RG,
                                         ins=[ag_b_in[:, :]], outs=[ag_b_out[:, :, :]]))
            return C["CC"]

        def emit_ln(light_entry=False):
            with contextlib.ExitStack() as ls:
                pm_s = ls.enter_context(nc.psum_tensor(nm("pm_s"), [1, TC], F32)).ap()
                pm_q = ls.enter_context(nc.psum_tensor(nm("pm_q"), [1, TC], F32)).ap()
                pm_mb = ls.enter_context(nc.psum_tensor(nm("pm_mb"), [128, TC], F32)).ap()
                pm_rb = ls.enter_context(nc.psum_tensor(nm("pm_rb"), [128, TC], F32)).ap()
                xb_t = ls.enter_context(nc.sbuf_tensor(nm("xb_t"), [128, 2 * TC], BF16)).ap()
                sq_t = ls.enter_context(nc.sbuf_tensor(nm("sq_t"), [128, 2 * TC], BF16)).ap()
                lnm = ls.enter_context(nc.sbuf_tensor(nm("lnm"), [1, TC], F32)).ap()
                lnv = ls.enter_context(nc.sbuf_tensor(nm("lnv"), [1, TC], F32)).ap()
                lnw = ls.enter_context(nc.sbuf_tensor(nm("lnw"), [1, TC], F32)).ap()
                ftmp = ls.enter_context(nc.sbuf_tensor(nm("ftmp"), [128, TB], F32)).ap()
                cast_ve, chunk_mm = [], []
                for c in range(DC):
                    par = c % 2

                    @block.vector
                    def _(v):
                        if c == 0 and light_entry:
                            v.wait_ge(SY, SY_X)
                            for nq in ("QS0", "QS1"):
                                if C[nq]:
                                    v.wait_ge(sems[nq], C[nq])
                        elif c == 0:
                            wait_snap(v, ALL_SEMS)
                        if c >= 2:
                            v.wait_ge(MM, chunk_mm[c - 2])
                        ve(v.tensor_copy(xb_t[:, par * TC:(par + 1) * TC],
                                         x_res[:, c * TC:(c + 1) * TC]))
                        cast_ve.append(ve(v.tensor_mul(sq_t[:, par * TC:(par + 1) * TC],
                                                       xb_t[:, par * TC:(par + 1) * TC],
                                                       xb_t[:, par * TC:(par + 1) * TC])))

                    @block.tensor
                    def _(t):
                        t.wait_ge(VE, cast_ve[c])
                        for th in range(2):
                            t.matmul(pm_s[:1, th * TB:(th + 1) * TB], ones_b[:, :],
                                     xb_t[:, par * TC + th * TB:par * TC + (th + 1) * TB],
                                     start=(c == 0), stop=(c == DC - 1))
                        ins = None
                        for th in range(2):
                            ins = t.matmul(pm_q[:1, th * TB:(th + 1) * TB], ones_b[:, :],
                                           sq_t[:, par * TC + th * TB:par * TC + (th + 1) * TB],
                                           start=(c == 0), stop=(c == DC - 1))
                        chunk_mm.append(mmi(ins))
                MM_STAT = C["MM"]

                @block.vector
                def _(v):
                    v.wait_ge(MM, MM_STAT)
                    ve(v.tensor_scalar(lnm[:, :], pm_s[:1, :], 1.0 / D, None, ALU.mult))
                    ve(v.tensor_scalar(lnv[:, :], pm_q[:1, :], 1.0 / D, LN_EPS, ALU.mult, ALU.add))
                    ve(v.tensor_mul(lnw[:, :], lnm[:, :], lnm[:, :]))
                    ve(v.tensor_sub(lnv[:, :], lnv[:, :], lnw[:, :]))
                    ve(v.reciprocal(lnv[:, :], lnv[:, :]))
                VE_FIN = C["VE"]

                @block.scalar
                def _(s):
                    s.wait_ge(VE, VE_FIN)
                    scc(s.activation(lnw[:, :], lnv[:, :], ACT.Sqrt))
                SC_RSTD = C["SC"]

                @block.tensor
                def _(t):
                    t.wait_ge(SC, SC_RSTD)
                    ins = None
                    for th in range(2):
                        t.matmul(pm_mb[:, th * TB:(th + 1) * TB], onesr_f[:, :],
                                 lnm[:1, th * TB:(th + 1) * TB], start=True, stop=True)
                    for th in range(2):
                        ins = t.matmul(pm_rb[:, th * TB:(th + 1) * TB], onesr_f[:, :],
                                       lnw[:1, th * TB:(th + 1) * TB], start=True, stop=True)
                    mmi(ins)
                MM_BC = C["MM"]

                @block.vector
                def _(v):
                    v.wait_ge(MM, MM_BC)
                    for c in range(DC):
                        for th in range(2):
                            ve(v.tensor_sub(ftmp[:, :], x_res[:, c * TC + th * TB:c * TC + (th + 1) * TB],
                                            pm_mb[:, th * TB:(th + 1) * TB]))
                            ve(v.tensor_mul(xhat[:, c * TC + th * TB:c * TC + (th + 1) * TB],
                                            ftmp[:, :], pm_rb[:, th * TB:(th + 1) * TB]))

        def xhat_slice(dc, th):
            return xhat[:, dc * TC + th * TB:dc * TC + (th + 1) * TB]

        def emit_proj(O, src_slice_fn, mat_slice_fn, evac_fn, wait_cc=None, post_fn=None,
                      light_entry=None, load_eng="sync"):
            n_og = O // TB
            with contextlib.ExitStack() as pscope:
                wstr = pscope.enter_context(
                    nc.sbuf_tensor(nm("wstr"), [128, 2 * DC * TB], BF16)).ap()
                pp = [[pscope.enter_context(nc.psum_tensor(nm("pp"), [128, TB], F32)).ap()
                       for _t in range(2)] for _pa in range(2)]
                ld, last_use, free_at = [], [], []
                for og in range(n_og):
                    def loads(sync):
                        if og >= 2:
                            sync.wait_ge(MM, last_use[4 * (og - 2) + 3])
                        if og == 0 and wait_cc is not None:
                            sync.wait_ge(CCs, wait_cc)
                        if og == 0 and light_entry is not None:
                            sync.wait_ge(VE, light_entry[0])
                            sync.wait_ge(MM, light_entry[1])
                        elif og == 0:
                            wait_snap(sync, ALL_SEMS)
                        par = og % 2
                        dst4 = wstr.rearrange("q (b r l c) -> q b r l c", b=2, r=8, c=TB)
                        for r in range(8):
                            src = mat_slice_fn(r, og * TB)
                            dmai("WL%d" % par, sync.dma_start(dst4[:, par, r, :, :], src))
                        ld.append(C["WL%d" % par])
                    if load_eng == "scalar":
                        block.scalar(loads)
                    else:
                        block.sync(loads)
                    for j in range(4):
                        oc = og * 4 + j
                        pa = oc % 2

                        @block.tensor
                        def _(t):
                            t.wait_ge(sems["WL%d" % (og % 2)], ld[og])
                            if oc == 0 and light_entry is None:
                                wait_snap(t, ALL_SEMS)
                            elif oc == 0:
                                t.wait_ge(VE, light_entry[0])
                            if oc >= 2:
                                for th in range(2):
                                    snme, val = free_at[2 * (oc - 2) + th]
                                    t.wait_ge(sems[snme], val)
                            ins = None
                            for th in range(2):
                                for dc in range(DC):
                                    ins = t.matmul(
                                        pp[pa][th],
                                        wstr[:, (og % 2) * DC * TB + dc * TB + j * 128:
                                             (og % 2) * DC * TB + dc * TB + (j + 1) * 128],
                                        src_slice_fn(dc, th),
                                        start=(dc == 0), stop=(dc == DC - 1))
                                mmi(ins)
                        last_use.append(C["MM"])
                        for th in range(2):
                            free_at.append(evac_fn(oc, th, pp[pa][th], C["MM"] - 1 + th))
                        if post_fn is not None:
                            post_fn(oc)
                tgt = dict(C)

                @block.tensor
                def _(t):
                    for nme in ["VE", "SC"]:
                        if tgt[nme]:
                            t.wait_ge(sems[nme], tgt[nme])

        def mk_mat_a(mi):
            def f(r, o0):
                return ag_a_out[r, mi].rearrange("(l p) c -> p l c", p=128)[:, :, o0:o0 + TB]
            return f

        def mk_mat_b(col0):
            def f(r, o0):
                return ag_b_out[r].rearrange("(l p) c -> p l c", p=128)[:, :, col0 + o0:col0 + o0 + TB]
            return f

        if PH_ATTA:
            emit_ln(light_entry=True)
            with contextlib.ExitStack() as pa:
                qprime = pa.enter_context(nc.sbuf_tensor("qprime", [MD, H * TC], BF16)).ap()
                kvac = pa.enter_context(nc.sbuf_tensor("kvac", [MD, H * DH], F32)).ap()
                ksum = pa.enter_context(nc.sbuf_tensor("ksum", [MD, H], F32)).ap()
                hh_sb = pa.enter_context(nc.sbuf_tensor("hh_sb", [128, 2 * TC], BF16)).ap()

                with contextlib.ExitStack() as pkv:
                    ktm = pkv.enter_context(nc.sbuf_tensor("ktm", [128, H * 8 * MD], BF16)).ap()
                    kp_sb = pkv.enter_context(nc.sbuf_tensor("kp_sb", [MD, 2 * TC], BF16)).ap()
                    el_t = pkv.enter_context(nc.sbuf_tensor("el_t", [MD, 2 * TB], F32)).ap()
                    el_e = pkv.enter_context(nc.sbuf_tensor("el_e", [MD, 2 * TB], F32)).ap()
                    vtm_sb = pkv.enter_context(nc.sbuf_tensor("vtm_sb", [128, 2 * TC], BF16)).ap()
                    pm_map = [pkv.enter_context(nc.psum_tensor(nm("pmap"), [MD, TB], F32)).ap()
                              for _t in range(2)]
                    ps_tr = pkv.enter_context(nc.psum_tensor("ps_tr", [128, TC], BF16)).ap()
                    ps_kv = pkv.enter_context(nc.psum_tensor("ps_kv", [MD, DH], F32)).ap()

                    def evac_simple(bias_name, dstbuf):
                        def f(oc, th, ps, mmv):
                            @block.vector
                            def _(v):
                                v.wait_ge(MM, mmv)
                                ve(v.tensor_scalar(
                                    dstbuf[:, (oc % 2) * TC + th * TB:(oc % 2) * TC + (th + 1) * TB],
                                    ps, csb[bias_name][:, oc:oc + 1], None, ALU.add))
                            return ("VE", C["VE"])
                        return f

                    last_elu = [0]

                    def post_map_elu(oc, wmap, outbuf, do_ksum):
                        h, par = oc, oc % 2
                        vbase = C["VE"]

                        @block.tensor
                        def _(t):
                            t.wait_ge(VE, vbase)
                            t.wait_ge(VE, last_elu[0])
                            t.wait_ge(G, G_MAPS)
                            for th in range(2):
                                mmi(t.matmul(pm_map[th], wmap,
                                             hh_sb[:, par * TC + th * TB:par * TC + (th + 1) * TB],
                                             start=True, stop=True))
                        mm_map = C["MM"]

                        @block.vector
                        def _(v):
                            v.wait_ge(MM, mm_map)
                            for th in range(2):
                                ve(v.tensor_scalar(el_t[:, th * TB:(th + 1) * TB],
                                                   pm_map[th], 0.0, None, ALU.min))
                        ve_min = C["VE"]

                        @block.scalar
                        def _(s):
                            s.wait_ge(VE, ve_min)
                            scc(s.activation(el_e[:, :], el_t[:, :], ACT.Exp))
                        sc_exp = C["SC"]

                        @block.vector
                        def _(v):
                            v.wait_ge(SC, sc_exp)
                            for th in range(2):
                                ve(v.scalar_tensor_tensor(
                                    outbuf[:, (h if outbuf is qprime else par) * TC + th * TB:
                                           (h if outbuf is qprime else par) * TC + (th + 1) * TB],
                                    pm_map[th], 0.0, el_e[:, th * TB:(th + 1) * TB],
                                    ALU.max, ALU.add))
                            if do_ksum:
                                ve(v.reduce_sum(ksum[:, h:h + 1],
                                                kp_sb[:, par * TC:(par + 1) * TC], axis=AX))
                        last_elu[0] = C["VE"]

                    def post_q(oc):
                        post_map_elu(oc, wmapq_b[:, :], qprime, False)
                    emit_proj(D, xhat_slice, mk_mat_a(0), evac_simple("bq", hh_sb),
                              wait_cc=CC_AGA, light_entry=(C["VE"], C["MM"]),
                              post_fn=post_q, load_eng="scalar")

                    ktr_done = []

                    def post_k(oc):
                        h, par = oc, oc % 2
                        post_map_elu(oc, wmapk_b[:, :], kp_sb, True)
                        ve_kp = C["VE"]

                        @block.tensor
                        def _(t):
                            t.wait_ge(VE, ve_kp)
                            ins = None
                            for j2 in range(8):
                                ins = t.transpose(ps_tr[:, j2 * MD:(j2 + 1) * MD],
                                                  kp_sb[:MD, par * TC + j2 * 128:par * TC + (j2 + 1) * 128],
                                                  csb["ident"][:MD, :MD])
                            mmi(ins)
                        mm_tr = C["MM"]

                        @block.vector
                        def _(v):
                            v.wait_ge(MM, mm_tr)
                            ve(v.tensor_copy(ktm[:, h * 8 * MD:(h + 1) * 8 * MD], ps_tr[:, 0:8 * MD]))
                        ktr_done.append(C["VE"])
                    emit_proj(D, xhat_slice, mk_mat_a(1), evac_simple("bk", hh_sb), post_fn=post_k,
                              load_eng="scalar")

                    def post_v(oc):
                        h, par = oc, oc % 2
                        vbase = C["VE"]

                        @block.tensor
                        def _(t):
                            t.wait_ge(VE, vbase)
                            ins = None
                            for j2 in range(8):
                                ins = t.transpose(ps_tr[:, j2 * 128:(j2 + 1) * 128],
                                                  hh_sb[:, par * TC + j2 * 128:par * TC + (j2 + 1) * 128],
                                                  csb["ident"][:, :])
                            mmi(ins)
                        mm_vt = C["MM"]

                        @block.vector
                        def _(v):
                            v.wait_ge(MM, mm_vt)
                            ve(v.tensor_copy(vtm_sb[:, par * TC:(par + 1) * TC], ps_tr[:, :]))
                        ve_vt = C["VE"]

                        @block.tensor
                        def _(t):
                            t.wait_ge(VE, ve_vt)
                            ins = None
                            for j2 in range(8):
                                ins = t.matmul(ps_kv,
                                               ktm[:, h * 8 * MD + j2 * MD:h * 8 * MD + (j2 + 1) * MD],
                                               vtm_sb[:, par * TC + j2 * 128:par * TC + (j2 + 1) * 128],
                                               start=(j2 == 0), stop=(j2 == 7))
                            mmi(ins)
                        mm_kv = C["MM"]

                        @block.vector
                        def _(v):
                            v.wait_ge(MM, mm_kv)
                            ve(v.tensor_copy(kvac[:, h * DH:(h + 1) * DH], ps_kv))
                    emit_proj(D, xhat_slice, mk_mat_a(2), evac_simple("bv", hh_sb), post_fn=post_v,
                              load_eng="scalar")

                VE_KVALL = C["VE"]

                @block.scalar
                def _(s):
                    s.wait_ge(VE, VE_KVALL)
                    s2i(s.dma_start(ar_kv_in[:, 0:H * DH], kvac[:, :]))
                    s2i(s.dma_start(ar_kv_in[:, H * DH:H * DH + H], ksum[:, :]))
                S2_KV = C["S2"]

                @block.gpsimd
                def _(g):
                    g.wait_ge(S2, S2_KV)
                    cci(g.collective_compute("AllReduce", ALU.add, replica_groups=RG,
                                             ins=[ar_kv_in[:, :]], outs=[ar_kv_out[:, :]]))
                    g.wait_ge(CCs, C["CC"])
                    gp(g.dma_start(kvac[:, :], ar_kv_out[:, 0:H * DH]))
                    gp(g.dma_start(ksum[:, :], ar_kv_out[:, H * DH:H * DH + H]))
                G_KVR = C["G"]

                with contextlib.ExitStack() as pt:
                    kvr_bf = pt.enter_context(nc.sbuf_tensor("kvr_bf", [MD, H * DH], BF16)).ap()
                    ksr_bf = pt.enter_context(nc.sbuf_tensor("ksr_bf", [MD, H], BF16)).ap()
                    rn_sb = pt.enter_context(nc.sbuf_tensor("rn_sb", [1, TC], F32)).ap()
                    rb_sb = pt.enter_context(nc.sbuf_tensor("rb_sb", [128, TC], F32)).ap()
                    ps_n = pt.enter_context(nc.psum_tensor("ps_n", [1, TC], F32)).ap()
                    ps_rb = pt.enter_context(nc.psum_tensor("ps_rb", [128, TC], F32)).ap()
                    ps_o = pt.enter_context(nc.psum_tensor("ps_o", [128, TC], F32)).ap()

                    @block.vector
                    def _(v):
                        v.wait_ge(G, G_KVR)
                        ve(v.tensor_copy(kvr_bf[:, :], kvac[:, :]))
                        ve(v.tensor_copy(ksr_bf[:, :], ksum[:, :]))
                    VE_KVBF = C["VE"]

                    for h in range(H):
                        vpre = C["VE"]

                        @block.tensor
                        def _(t):
                            t.wait_ge(VE, max(VE_KVBF, vpre))
                            ins = None
                            for th in range(2):
                                ins = t.matmul(ps_n[:1, th * TB:(th + 1) * TB],
                                               ksr_bf[:, h:h + 1],
                                               qprime[:, h * TC + th * TB:h * TC + (th + 1) * TB],
                                               start=True, stop=True)
                            mmi(ins)
                        mm_n = C["MM"]

                        @block.vector
                        def _(v):
                            v.wait_ge(MM, mm_n)
                            ve(v.tensor_scalar(rn_sb[:, :], ps_n[:1, :], ATTA_EPS, None, ALU.add))
                            ve(v.reciprocal(rn_sb[:, :], rn_sb[:, :]))
                        ve_rn = C["VE"]

                        @block.tensor
                        def _(t):
                            t.wait_ge(VE, ve_rn)
                            ins = None
                            for th in range(2):
                                t.matmul(ps_rb[:, th * TB:(th + 1) * TB], onesr_f[:, :],
                                         rn_sb[:1, th * TB:(th + 1) * TB], start=True, stop=True)
                            for th in range(2):
                                ins = t.matmul(ps_o[:, th * TB:(th + 1) * TB],
                                               kvr_bf[:, h * DH:(h + 1) * DH],
                                               qprime[:, h * TC + th * TB:h * TC + (th + 1) * TB],
                                               start=True, stop=True)
                            mmi(ins)
                        mm_o = C["MM"]

                        @block.vector
                        def _(v):
                            v.wait_ge(MM, mm_o)
                            ve(v.tensor_copy(rb_sb[:, :], ps_rb[:, :]))
                            ve(v.tensor_mul(xhat[:, h * TC:(h + 1) * TC], ps_o[:, :], rb_sb[:, :]))

                def evac_wo(oc, th, ps, mmv):
                    @block.vector
                    def _(v):
                        v.wait_ge(MM, mmv)
                        ve(v.scalar_tensor_tensor(
                            x_res[:, oc * TC + th * TB:oc * TC + (th + 1) * TB],
                            ps, csb["bo"][:, oc:oc + 1],
                            x_res[:, oc * TC + th * TB:oc * TC + (th + 1) * TB],
                            ALU.add, ALU.add))
                    return ("VE", C["VE"])
                emit_proj(D, xhat_slice, mk_mat_a(3), evac_wo)
            tap(1)
            barrier()

        def emit_ffn():
            emit_ln()
            with contextlib.ExitStack() as pf:
                wu_b = pf.enter_context(nc.sbuf_tensor(nm("wu_b"), [128, 2 * 4096], BF16)).ap()
                wg_b = pf.enter_context(nc.sbuf_tensor(nm("wg_b"), [128, 2 * 4096], BF16)).ap()
                wd_b = pf.enter_context(nc.sbuf_tensor(nm("wd_b"), [128, 2 * 2048], BF16)).ap()
                hid = pf.enter_context(nc.sbuf_tensor(nm("hid"), [128, 2 * 8 * TC], BF16)).ap()
                sil = pf.enter_context(nc.sbuf_tensor(nm("sil"), [128, TB], F32)).ap()
                ps_u = [pf.enter_context(nc.psum_tensor(nm("ps_u"), [128, TB], F32)).ap()
                        for _t in range(2)]
                ps_g = [pf.enter_context(nc.psum_tensor(nm("ps_g"), [128, TB], F32)).ap()
                        for _t in range(2)]
                ps_d = [pf.enter_context(nc.psum_tensor(nm("ps_d"), [128, TB], F32)).ap()
                        for _t in range(4)]

                ug_ld, ug_use, dn_ld, dn_use = {}, {}, {}, {}
                ev_ug, dn_ev, hid_ve = {}, {}, {}

                def load_ug(g, pc):
                    @block.sync
                    def _(sync):
                        if (g, pc) == (0, 0):
                            sync.wait_ge(CCs, CC_UG)
                            wait_snap(sync, ALL_SEMS)
                        prev = (g * 4 + pc) - 2
                        if prev >= 0:
                            sync.wait_ge(MM, ug_use[(prev // 4, prev % 4)])
                        par = pc % 2
                        e0 = g * 1024 + pc * 256
                        for buf, m in ((wu_b, 0), (wg_b, 1)):
                            dst = buf.rearrange("q (b r l e) -> q b r l e", b=2, r=8, e=256)
                            for r in range(8):
                                src = ag_ug_out[r, m].rearrange("(l p) e -> p l e", p=128)[:, :, e0:e0 + 256]
                                dmai("UG%d" % par, sync.dma_start(dst[:, par, r, :, :], src))
                        ug_ld[(g, pc)] = C["UG%d" % par]

                def load_dn(g, inst):
                    @block.sync
                    def _(sync):
                        if (g, inst) == (0, 0):
                            sync.wait_ge(CCs, CC_DN)
                            wait_snap(sync, ALL_SEMS)
                        prev = (g * 8 + inst) - 2
                        if prev >= 0:
                            sync.wait_ge(MM, dn_use[(prev // 8, prev % 8)])
                        par = inst % 2
                        dst = wd_b.rearrange("q (b ec d) -> q b ec d", b=2, d=256)
                        src = ag_dn_out[g].rearrange("(ec p) d -> p ec d", p=128)[:, :, inst * 256:(inst + 1) * 256]
                        dmai("DN%d" % par, sync.dma_start(dst[:, par, :, :], src))
                        dn_ld[(g, inst)] = C["DN%d" % par]

                for g in range(8):
                    for pc in range(4):
                        load_ug(g, pc)
                        par = pc % 2
                        for e2 in range(2):
                            ecl = pc * 2 + e2
                            eidx = g * 8 + ecl

                            @block.tensor
                            def _(t):
                                t.wait_ge(sems["UG%d" % (pc % 2)], ug_ld[(g, pc)])
                                if eidx == 0:
                                    wait_snap(t, ALL_SEMS)
                                if eidx >= 1:
                                    psc, pve = ev_ug[eidx - 1]
                                    t.wait_ge(SC, psc)
                                    t.wait_ge(VE, pve)
                                ins = None
                                for th in range(2):
                                    for dc in range(DC):
                                        t.matmul(ps_u[th],
                                                 wu_b[:, par * 4096 + dc * 256 + e2 * 128:
                                                      par * 4096 + dc * 256 + (e2 + 1) * 128],
                                                 xhat_slice(dc, th),
                                                 start=(dc == 0), stop=(dc == DC - 1))
                                for th in range(2):
                                    for dc in range(DC):
                                        ins = t.matmul(ps_g[th],
                                                       wg_b[:, par * 4096 + dc * 256 + e2 * 128:
                                                            par * 4096 + dc * 256 + (e2 + 1) * 128],
                                                       xhat_slice(dc, th),
                                                       start=(dc == 0), stop=(dc == DC - 1))
                                mmi(ins)
                            ug_use[(g, pc)] = C["MM"]
                            mm_ug = C["MM"]

                            for th in range(2):
                                ve_now = C["VE"]

                                @block.scalar
                                def _(s):
                                    s.wait_ge(MM, mm_ug)
                                    s.wait_ge(VE, ve_now)
                                    scc(s.activation(sil[:, :], ps_u[th], ACT.Silu))
                                sc_sil = C["SC"]

                                @block.vector
                                def _(v):
                                    v.wait_ge(SC, sc_sil)
                                    ve(v.tensor_mul(
                                        hid[:, (g % 2) * 8 * TC + ecl * TC + th * TB:
                                            (g % 2) * 8 * TC + ecl * TC + (th + 1) * TB],
                                        ps_g[th], sil[:, :]))
                            ev_ug[eidx] = (C["SC"], C["VE"])
                    hid_ve[g] = C["VE"]

                    for inst in range(8):
                        load_dn(g, inst)

                        @block.tensor
                        def _(t):
                            t.wait_ge(sems["DN%d" % (inst % 2)], dn_ld[(g, inst)])
                            t.wait_ge(VE, hid_ve[g])
                            prev = (g * 8 + inst) - 1
                            if prev >= 0:
                                t.wait_ge(VE, dn_ev[(prev // 8, prev % 8)])
                            for dl in range(2):
                                for th in range(2):
                                    ins = None
                                    for ec in range(8):
                                        ins = t.matmul(
                                            ps_d[dl * 2 + th],
                                            wd_b[:, (inst % 2) * 2048 + ec * 256 + dl * 128:
                                                 (inst % 2) * 2048 + ec * 256 + (dl + 1) * 128],
                                            hid[:, (g % 2) * 8 * TC + ec * TC + th * TB:
                                                (g % 2) * 8 * TC + ec * TC + (th + 1) * TB],
                                            start=(ec == 0), stop=(ec == 7))
                                    mmi(ins)
                        dn_use[(g, inst)] = C["MM"]
                        mm_d = C["MM"]

                        @block.vector
                        def _(v):
                            v.wait_ge(MM, mm_d)
                            for dl in range(2):
                                dcc = inst * 2 + dl
                                for th in range(2):
                                    ve(v.scalar_tensor_tensor(
                                        x_res[:, dcc * TC + th * TB:dcc * TC + (th + 1) * TB],
                                        ps_d[dl * 2 + th], 1.0,
                                        x_res[:, dcc * TC + th * TB:dcc * TC + (th + 1) * TB],
                                        ALU.mult, ALU.add))
                        dn_ev[(g, inst)] = C["VE"]
                tgt = dict(C)

                @block.tensor
                def _(t):
                    t.wait_ge(VE, tgt["VE"])
            barrier()

        CC_AGB = emit_cast_b()

        if PH_FFN1:
            emit_ffn()
            tap(2)

        if PH_ATTB:
            emit_ln()
            with contextlib.ExitStack() as pb:
                q_sb = pb.enter_context(nc.sbuf_tensor("q_sb", [128, H * TC], BF16)).ap()

                with contextlib.ExitStack() as pk:
                    ktmp = pk.enter_context(nc.sbuf_tensor("ktmp", [128, 2 * TC], BF16)).ap()
                    sqn = pk.enter_context(nc.sbuf_tensor("sqn", [128, TC], BF16)).ap()
                    ps_nrm = pk.enter_context(nc.psum_tensor("ps_nrm", [1, TC], F32)).ap()
                    ps_vtr = pk.enter_context(nc.psum_tensor("ps_vtr", [128, TC], BF16)).ap()
                    kdma, vdma = [], []

                    def post_norm(buf, off_fn, h, slot):
                        vbase = C["VE"]

                        @block.vector
                        def _(v):
                            ve(v.tensor_mul(sqn[:, :], buf[:, off_fn:off_fn + TC],
                                            buf[:, off_fn:off_fn + TC]))
                        ve_sq = C["VE"]

                        @block.tensor
                        def _(t):
                            t.wait_ge(VE, ve_sq)
                            ins = None
                            for th in range(2):
                                ins = t.matmul(ps_nrm[:1, th * TB:(th + 1) * TB], ones_b[:, :],
                                               sqn[:, th * TB:(th + 1) * TB], start=True, stop=True)
                            mmi(ins)
                        mm_nr = C["MM"]

                        @block.vector
                        def _(v):
                            v.wait_ge(MM, mm_nr)
                            ve(v.reduce_max(nmax[:, 2 + slot:3 + slot], ps_nrm[:1, :], axis=AX))
                            vspace(v)
                            vspace(v)
                            if h == 0:
                                ve(v.tensor_copy(nmax[:, slot:slot + 1], nmax[:, 2 + slot:3 + slot]))
                            else:
                                ve(v.tensor_max(nmax[:, slot:slot + 1], nmax[:, slot:slot + 1],
                                                nmax[:, 2 + slot:3 + slot]))
                            vspace(v)

                    def evac_k(oc, th, ps, mmv):
                        h = oc

                        @block.vector
                        def _(v):
                            v.wait_ge(MM, mmv)
                            if h >= 2 and th == 0:
                                v.wait_ge(sems["KO%d" % (h % 2)], kdma[h - 2])
                            ve(v.tensor_scalar(
                                ktmp[:, (h % 2) * TC + th * TB:(h % 2) * TC + (th + 1) * TB],
                                ps, csb["bqkv"][:, oc + 16:oc + 17], None, ALU.add))
                        return ("VE", C["VE"])

                    def post_kp(oc):
                        h, par = oc, oc % 2
                        post_norm(ktmp, par * TC, h, 1)
                        ve_k = C["VE"]

                        @block.scalar
                        def _(s):
                            s.wait_ge(VE, ve_k)
                            dmai("KO%d" % par, s.dma_start(ag_k_in[0, h], ktmp[:, par * TC:par * TC + TB]))
                            dmai("KO%d" % par, s.dma_start(ag_k_in[1, h], ktmp[:, par * TC + TB:par * TC + TC]))
                        kdma.append(C["KO%d" % par])

                    def srcK(dc, th):
                        return xhat_slice(dc, th)
                    emit_proj(D, srcK, mk_mat_b(16 * 128), evac_k, wait_cc=CC_AGB, post_fn=post_kp)

                    def evac_v2(oc, th, ps, mmv):
                        h = oc

                        @block.vector
                        def _(v):
                            v.wait_ge(MM, mmv)
                            if h >= 2 and th == 0:
                                v.wait_ge(sems["KO%d" % (h % 2)], vdma[h - 2])
                            ve(v.tensor_scalar(
                                ktmp[:, (h % 2) * TC + th * TB:(h % 2) * TC + (th + 1) * TB],
                                ps, csb["bqkv"][:, oc + 32:oc + 33], None, ALU.add))
                        return ("VE", C["VE"])

                    def post_vp(oc):
                        h, par = oc, oc % 2
                        vbase = C["VE"]

                        @block.tensor
                        def _(t):
                            t.wait_ge(VE, vbase)
                            ins = None
                            for j2 in range(8):
                                ins = t.transpose(ps_vtr[:, j2 * 128:(j2 + 1) * 128],
                                                  ktmp[:, par * TC + j2 * 128:par * TC + (j2 + 1) * 128],
                                                  csb["ident"][:, :])
                            mmi(ins)
                        mm_vt = C["MM"]

                        @block.vector
                        def _(v):
                            v.wait_ge(MM, mm_vt)
                            ve(v.tensor_copy(ktmp[:, par * TC:(par + 1) * TC], ps_vtr[:, :]))
                        ve_vc = C["VE"]

                        @block.scalar
                        def _(s):
                            s.wait_ge(VE, ve_vc)
                            for ab in range(2):
                                for tp in range(2):
                                    dmai("KO%d" % par, s.dma_start(
                                        ag_v_in[ab, h, tp],
                                        ktmp[:, par * TC + ab * TB + tp * 256:
                                             par * TC + ab * TB + (tp + 1) * 256]))
                        vdma.append(C["KO%d" % par])
                    emit_proj(D, srcK, mk_mat_b(32 * 128), evac_v2, post_fn=post_vp)

                    def evac_q2(oc, th, ps, mmv):
                        @block.vector
                        def _(v):
                            v.wait_ge(MM, mmv)
                            ve(v.tensor_scalar(
                                q_sb[:, oc * TC + th * TB:oc * TC + (th + 1) * TB],
                                ps, csb["bqkv"][:, oc:oc + 1], INV_SQRT_DH, ALU.add, ALU.mult))
                        return ("VE", C["VE"])

                    def post_qp(oc):
                        post_norm(q_sb, oc * TC, oc, 0)
                    emit_proj(D, srcK, mk_mat_b(0), evac_q2, post_fn=post_qp)
                VE_NORMS = C["VE"]
                C_KO = (C["KO0"], C["KO1"])

                @block.scalar
                def _(s):
                    s.wait_ge(VE, VE_NORMS)
                    s2i(s.dma_start(ar_bd_in[:, :], nmax[:, 0:2]))
                S2_BD = C["S2"]

                @block.gpsimd
                def _(g):
                    g.wait_ge(S2, S2_BD)
                    g.wait_ge(sems["KO0"], C_KO[0])
                    g.wait_ge(sems["KO1"], C_KO[1])
                    cci(g.collective_compute("AllReduce", ALU.max, replica_groups=RG,
                                             ins=[ar_bd_in[:, :]], outs=[ar_bd_out[:, :]]))
                    g.wait_ge(CCs, C["CC"])
                    gp(g.dma_start(nmax[:, 0:2], ar_bd_out[:, :]))
                    cci(g.collective_compute("AllGather", ALU.bypass, replica_groups=RG,
                                             ins=[ag_k_in[:, :, :, :]],
                                             outs=[ag_k_out_t.ap()[:, :, :, :, :]]))
                    cci(g.collective_compute("AllGather", ALU.bypass, replica_groups=RG,
                                             ins=[ag_v_in[:, :, :, :, :]],
                                             outs=[agv_gat_t.ap()[:, :, :, :, :, :]]))
                CC_BD = C["CC"] - 2
                CC_K = C["CC"] - 1
                CC_V = C["CC"]
                G_BD = C["G"]

                @block.vector
                def _(v):
                    v.wait_ge(G, G_BD)
                    ve(v.tensor_mul(nmax[:, 2:3], nmax[:, 0:1], nmax[:, 1:2]))
                    vspace(v)
                VE_C1 = C["VE"]

                @block.scalar
                def _(s):
                    s.wait_ge(VE, VE_C1)
                    scc(s.activation(nmax[:, 3:4], nmax[:, 2:3], ACT.Sqrt))
                SC_C = C["SC"]

                @block.vector
                def _(v):
                    v.wait_ge(SC, SC_C)
                    ve(v.tensor_scalar_min(nmax[:, 3:4], nmax[:, 3:4], 20.0))
                    vspace(v)
                    vspace(v)
                    ve(v.tensor_scalar(nmax[:, 2:3], nmax[:, 3:4], -1.0, None, ALU.mult))
                    vspace(v)
                VE_NC = C["VE"]

                with contextlib.ExitStack() as pc2:
                    ps_bv = pc2.enter_context(nc.psum_tensor("ps_bv", [128, NIT], F32)).ap()
                    ps_ng = pc2.enter_context(nc.psum_tensor("ps_ng", [128, 1], F32)).ap()

                    @block.tensor
                    def _(t):
                        t.wait_ge(VE, VE_NC)
                        mmi(t.matmul(ps_ng, onesr_f[:, :], nmax[:1, 2:3], start=True, stop=True))
                        mmi(t.matmul(ps_bv, onesr_f[:, :], csb["hsel"][:1, :], start=True, stop=True))
                    MM_BV = C["MM"]

                    @block.vector
                    def _(v):
                        v.wait_ge(MM, MM_BV)
                        ve(v.tensor_copy(negc[:, :], ps_ng))
                        vspace(v)
                        vspace(v)
                        ve(v.tensor_scalar(biasv[:, :], ps_bv, negc[:, 0:1], None, ALU.add))
                VE_BV = C["VE"]

                with contextlib.ExitStack() as pat:
                    attacc = pat.enter_context(nc.sbuf_tensor("attacc", [128, H * TB], F32)).ap()
                    masks_sb = pat.enter_context(nc.sbuf_tensor("masks_sb", [128, 4 * TB], BF16)).ap()
                    ohr_sb = pat.enter_context(nc.sbuf_tensor("ohr_sb", [H, 128 * H], F32)).ap()

                    @block.sync
                    def _(sync):
                        wait_snap(sync, ALL_SEMS)
                        sy(sync.dma_start(masks_sb[:, :], din["masks"][:, :]))
                        sy(sync.dma_start(ohr_sb[:, :], din["ohr"][:, :]))
                    SY_MASKS = C["SY"]
                    kbuf = pat.enter_context(nc.sbuf_tensor("kbuf", [128, 2 * 4 * TB], BF16)).ap()
                    vbuf = pat.enter_context(nc.sbuf_tensor("vbuf", [128, 2 * 4 * TB], BF16)).ap()
                    abuf = pat.enter_context(nc.sbuf_tensor("abuf", [128, 4 * TB], BF16)).ap()
                    sums_sb = pat.enter_context(nc.sbuf_tensor("sums_sb", [H, TB], F32)).ap()
                    sumt = pat.enter_context(nc.sbuf_tensor("sumt", [128, 3 * TB], BF16)).ap()
                    ps_sc = pat.enter_context(nc.psum_tensor("ps_sc", [128, 4 * TB], F32)).ap()
                    ps_av = [pat.enter_context(nc.psum_tensor(nm("ps_av"), [128, TB], F32)).ap()
                             for _t in range(2)]
                    ps_sm = pat.enter_context(nc.psum_tensor("ps_sm", [H, TB], F32)).ap()

                    it_tab = [(0, 0, 0)] + [(0, 1, r) for r in range(8)] \
                        + [(1, 0, 1)] + [(1, 1, r) for r in range(8)] + [(1, 2, r) for r in range(8)]
                    assert len(it_tab) == NIT

                    kv_ld, kv_use = {}, {}
                    sc_free = []
                    av_mm = []
                    for itg, (tqb, kind, rank) in enumerate(it_tab):
                        local_first = (itg in (0, 9))
                        if local_first:
                            @block.vector
                            def _(v):
                                wait_snap(v, ALL_SEMS)
                                ve(v.memset(attacc[:, :], 0.0))
                            acc_zero = C["VE"]
                            av_ev = []
                            n_base = itg
                        def kv_load(hg):
                            gkey = itg * 4 + hg

                            @block.sync
                            def _(sync):
                                if gkey >= 2:
                                    sync.wait_ge(MM, kv_use[gkey - 2])
                                if gkey == 0:
                                    sync.wait_ge(CCs, CC_K)
                                    wait_snap(sync, ALL_SEMS)
                                if gkey == 1:
                                    sync.wait_ge(CCs, CC_V)
                                par = gkey % 2
                                if kind == 0:
                                    ab = tqb
                                    ksrc = ag_k_in[ab, 4 * hg:4 * (hg + 1)]
                                    vsrc = ag_v_in[ab, 4 * hg:4 * (hg + 1)]
                                else:
                                    ab = 0 if kind == 1 else 1
                                    ksrc = ag_k_out_t.ap()[rank, ab, 4 * hg:4 * (hg + 1)]
                                    vsrc = agv_gat_t.ap()[rank, ab, 4 * hg:4 * (hg + 1)]
                                kdst = kbuf.rearrange("q (b hh t) -> q b hh t", b=2, t=TB)
                                dmai("KV%d" % par, sync.dma_start(kdst[:, par, :, :],
                                                                  ksrc.rearrange("hh p t -> p hh t")))
                                vdst = vbuf.rearrange("q (b tp hh f) -> q b tp hh f",
                                                      b=2, tp=2, f=256)
                                for tp in range(2):
                                    dmai("KV%d" % par, sync.dma_start(
                                        vdst[:, par, tp, :, :],
                                        vsrc[:, tp].rearrange("hh p f -> p hh f")))
                                kv_ld[gkey] = C["KV%d" % par]

                        for h in range(H):
                            if h % 4 == 0:
                                kv_load(h // 4)
                            hg = h // 4
                            hh = h % 4
                            gkey = itg * 4 + hg
                            par = gkey % 2
                            idx = itg - n_base
                            lidx = len(av_ev)
                            fidx = len(sc_free)
                            kb = kbuf[:, par * 4 * TB + hh * TB:par * 4 * TB + (hh + 1) * TB]
                            vbase = par * 4 * TB
                            qs = q_sb[:, h * TC + tqb * TB:h * TC + (tqb + 1) * TB]

                            @block.tensor
                            def _(t):
                                t.wait_ge(sems["KV%d" % (gkey % 2)], kv_ld[gkey])
                                if lidx == 0:
                                    t.wait_ge(VE, max(VE_BV, acc_zero))
                                if fidx >= 1:
                                    psc, pve = sc_free[fidx - 1]
                                    t.wait_ge(SC, psc)
                                    if pve:
                                        t.wait_ge(VE, pve)
                                ins = None
                                for sub in range(2):
                                    ins = t.matmul(ps_sc[:, sub * TB:(sub + 1) * TB],
                                                   kb[:, sub * 128:(sub + 1) * 128], qs,
                                                   start=True, stop=True)
                                mmi(ins)
                                for sub in range(2, 4):
                                    ins = t.matmul(ps_sc[:, sub * TB:(sub + 1) * TB],
                                                   kb[:, sub * 128:(sub + 1) * 128], qs,
                                                   start=True, stop=True)
                                mmi(ins)
                            m_s = (C["MM"] - 1, C["MM"])

                            exp_sc = []
                            for half in range(2):
                                @block.scalar
                                def _(s, half=half):
                                    s.wait_ge(MM, m_s[half])
                                    if fidx >= 1 and half == 0:
                                        s.wait_ge(MM, av_mm[fidx - 1])
                                    exp_sc.append(scc(s.activation(
                                        abuf[:, half * TC:(half + 1) * TC],
                                        ps_sc[:, half * TC:(half + 1) * TC],
                                        ACT.Exp, bias=biasv[:, itg:itg + 1])))
                            mask_ve = 0
                            if kind == 0:
                                @block.vector
                                def _(v):
                                    v.wait_ge(SC, exp_sc[1])
                                    v.wait_ge(SY, SY_MASKS)
                                    ve(v.tensor_mul(abuf[:, :], abuf[:, :], masks_sb[:, :]))
                                mask_ve = C["VE"]
                            sc_free.append((exp_sc[1], mask_ve))

                            n_loc = (9 if tqb == 0 else 17) * H

                            @block.vector
                            def _(v):
                                if mask_ve:
                                    v.wait_ge(VE, mask_ve)
                                else:
                                    v.wait_ge(SC, exp_sc[1])
                                ve(v.tensor_add(sumt[:, 0:TB], abuf[:, 0:TB], abuf[:, TB:2 * TB]))
                                ve(v.tensor_add(sumt[:, TB:2 * TB], abuf[:, 2 * TB:3 * TB],
                                                abuf[:, 3 * TB:4 * TB]))
                                ve(v.tensor_add(sumt[:, 2 * TB:3 * TB], sumt[:, 0:TB],
                                                sumt[:, TB:2 * TB]))
                            sum_ve = C["VE"]

                            @block.tensor
                            def _(t):
                                if mask_ve:
                                    t.wait_ge(VE, mask_ve)
                                else:
                                    t.wait_ge(SC, exp_sc[1])
                                if lidx >= 2:
                                    t.wait_ge(VE, av_ev[lidx - 2])
                                apar = lidx % 2
                                for sub in range(4):
                                    vo = vbase + (sub // 2) * 1024 + hh * 256 + (sub % 2) * 128
                                    t.matmul(ps_av[apar], vbuf[:, vo:vo + 128],
                                             abuf[:, sub * TB:(sub + 1) * TB],
                                             start=(sub == 0), stop=(sub == 3))
                                t.wait_ge(VE, sum_ve)
                                ins = t.matmul(ps_sm,
                                               csb["ohot"][:, 16 * h:16 * (h + 1)],
                                               sumt[:, 2 * TB:3 * TB],
                                               start=(lidx == 0),
                                               stop=(lidx == n_loc - 1))
                                mmi(ins)
                            kv_use[gkey] = C["MM"]
                            av_mm.append(C["MM"])
                            mm_av = C["MM"]

                            @block.vector
                            def _(v):
                                v.wait_ge(MM, mm_av)
                                apar = lidx % 2
                                ve(v.scalar_tensor_tensor(
                                    attacc[:, h * TB:(h + 1) * TB], ps_av[apar], 1.0,
                                    attacc[:, h * TB:(h + 1) * TB], ALU.mult, ALU.add))
                            av_ev.append(C["VE"])

                        if itg in (8, 25):
                            MM_SM = C["MM"]

                            @block.vector
                            def _(v):
                                v.wait_ge(MM, MM_SM)
                                ve(v.tensor_scalar(sums_sb[:, :], ps_sm, float(2 ** 25),
                                                   None, ALU.mult))
                                vspace(v)
                                ve(v.reciprocal(sums_sb[:, :], sums_sb[:, :]))
                            VE_RS = C["VE"]

                            fin_ve = []
                            for h in range(H):
                                @block.tensor
                                def _(t, h=h):
                                    t.wait_ge(VE, VE_RS)
                                    if h >= 2:
                                        t.wait_ge(VE, fin_ve[h - 2])
                                    else:
                                        t.wait_ge(VE, av_ev[-1])
                                    if h == 0:
                                        t.wait_ge(SY, SY_MASKS)
                                    mmi(t.matmul(ps_av[h % 2], ohr_sb[:, 128 * h:128 * (h + 1)],
                                                 sums_sb[:, :], start=True, stop=True))
                                mm_rb = C["MM"]

                                @block.vector
                                def _(v, h=h, tqb=tqb, mm_rb=mm_rb):
                                    v.wait_ge(MM, mm_rb)
                                    ve(v.tensor_mul(xhat[:, h * TC + tqb * TB:h * TC + (tqb + 1) * TB],
                                                    attacc[:, h * TB:(h + 1) * TB], ps_av[h % 2]))
                                fin_ve.append(C["VE"])
                            tgt2 = dict(C)

                            @block.tensor
                            def _(t):
                                t.wait_ge(VE, tgt2["VE"])

                def evac_bo(oc, th, ps, mmv):
                    @block.vector
                    def _(v):
                        v.wait_ge(MM, mmv)
                        ve(v.scalar_tensor_tensor(
                            x_res[:, oc * TC + th * TB:oc * TC + (th + 1) * TB],
                            ps, csb["bout"][:, oc:oc + 1],
                            x_res[:, oc * TC + th * TB:oc * TC + (th + 1) * TB],
                            ALU.add, ALU.add))
                    return ("VE", C["VE"])

                def src_att(dc, th):
                    return xhat[:, dc * TC + th * TB:dc * TC + (th + 1) * TB]
                emit_proj(D, src_att, mk_mat_b(QKV_O), evac_bo)
            tap(3)
            barrier()

        if PH_FFN2:
            emit_ffn()

        tgt = dict(C)

        @block.scalar
        def _(s):
            for n in ["MM", "VE", "SC"]:
                if tgt[n]:
                    s.wait_ge(sems[n], tgt[n])
            src = x_res.rearrange("p (c t) -> p c t", t=TC)
            dst = out_d.rearrange("(c p) t -> p c t", p=128)
            for q in range(4):
                s2i(s.dma_start(dst[:, 4 * q:4 * (q + 1), :], src[:, 4 * q:4 * (q + 1), :]))
        S2_OUT = C["S2"]

        if DEBUG_TAPS:
            @block.sync
            def _(sync):
                sync.wait_ge(CCs, C["CC"])
                for r_ in range(NCORES):
                    sy(sync.dma_start(dbgwu[r_], ag_ug_out[r_]))
                    sy(sync.dma_start(dbgwd[r_], ag_dn_out[r_]))
                sy(sync.dma_start(dbgs[:, 0:9], qparams[:, :]))
                sy(sync.dma_start(dbgs[0:1, 9:16], scl[:, 0:7]))
                sy(sync.dma_start(dbgs[:, 16:20], absacc[:, 0:4]))
        SY_END = C["SY"]

        @block.sync
        def _(sync):
            sync.wait_ge(S2, S2_OUT)
            if DEBUG_TAPS:
                sync.wait_ge(SY, SY_END)

    return nc


def kernel(**inputs):
    in_maps = prepare_host(inputs)
    nc = build_nc()
    res = run_bass_kernel_spmd(nc, in_maps, core_ids=list(range(NCORES)))
    return assemble_host(res.results)


# version 14
# speedup vs baseline: 1.1940x; 1.0684x over previous
import contextlib
import numpy as np
import ml_dtypes
import concourse.bass as bass
import concourse.mybir as mybir
from concourse.bass_utils import run_bass_kernel_spmd

F32 = mybir.dt.float32
BF16 = mybir.dt.bfloat16
I32 = mybir.dt.int32
ALU = mybir.AluOpType
ACT = mybir.ActivationFunctionType
AX = mybir.AxisListType.X

NCORES = 8
S = 8192
D = 2048
DC = 16
H = 16
DH = 128
MD = 64
E = 8192
TB = 512
TC = 1024
QKV_O = 6144
INV_SQRT_DH = float(1.0 / np.sqrt(128.0))
LN_EPS = 1e-5
ATTA_EPS = 1e-6
FFN_EPS = 1e-8
C0_ABS = (0.017632, 0.017632, 0.008816)
NIT = 26

DEBUG_TAPS = False
PH_ATTA = True
PH_FFN1 = True
PH_ATTB = True
PH_FFN2 = True


def _bf(x):
    return np.ascontiguousarray(x).astype(ml_dtypes.bfloat16)


def _ct(x):
    return np.ascontiguousarray(x)


def prepare_host(inp):
    x2d = np.asarray(inp["L0_attn_a_hidden_states"], np.float32).reshape(S, D)
    wqT = _ct(np.asarray(inp["L0_attn_a_wq"], np.float32).T)
    wkT = _ct(np.asarray(inp["L0_attn_a_wk"], np.float32).T)
    wvT = _ct(np.asarray(inp["L0_attn_a_wv"], np.float32).T)
    woT = _ct(np.asarray(inp["L0_attn_a_wo"], np.float32).T)
    qkvT = _ct(np.asarray(inp["L0_attn_b_w_qkv"], np.float32).T)
    woutT = _ct(np.asarray(inp["L0_attn_b_w_out"], np.float32).T)
    wuT = _ct(np.asarray(inp["L0_ffn_W_up"], np.float32).T)
    wgT = _ct(np.asarray(inp["L0_ffn_W_gate"], np.float32).T)
    wdT = _ct(np.asarray(inp["L0_ffn_W_down"], np.float32).T)

    def re_bias(b, nch):
        return _ct(np.asarray(b, np.float32).reshape(nch, 128).T)

    biases = {
        "bq": re_bias(inp["L0_attn_a_bq"], DC), "bk": re_bias(inp["L0_attn_a_bk"], DC),
        "bv": re_bias(inp["L0_attn_a_bv"], DC), "bo": re_bias(inp["L0_attn_a_bo"], DC),
        "bqkv": re_bias(inp["L0_attn_b_b_qkv"], 48), "bout": re_bias(inp["L0_attn_b_b_out"], DC),
    }
    wmapqT = _ct(np.asarray(inp["L0_attn_a_wmap_q"], np.float32).T)
    wmapkT = _ct(np.asarray(inp["L0_attn_a_wmap_k"], np.float32).T)

    ident = _bf(np.eye(128, dtype=np.float32))
    pp = np.arange(128)[:, None]
    ff = np.arange(TB)[None, :]
    masks = _bf(np.concatenate([(sub * 128 + pp <= ff) for sub in range(4)], 1).astype(np.float32))
    oh = np.zeros((128, 16 * H), np.float32)
    for h in range(H):
        oh[:, 16 * h + h] = 1.0
    ohot = _bf(oh)
    ohr = np.zeros((H, 128 * H), np.float32)
    for h in range(H):
        ohr[h, 128 * h:128 * (h + 1)] = float(2 ** 25)

    in_maps = []
    for i in range(NCORES):
        xs = np.concatenate([x2d[i * TB:(i + 1) * TB], x2d[(15 - i) * TB:(16 - i) * TB]], 0)
        hsel = np.zeros(NIT, np.float32)
        for r in range(8):
            hsel[1 + r] = 0.0 if r < i else -200.0
            hsel[18 + r] = 0.0 if r > i else -200.0
        d = {
            "xT": _ct(xs.T),
            "wq_sl": _ct(wqT[256 * i:256 * (i + 1)]),
            "wk_sl": _ct(wkT[256 * i:256 * (i + 1)]),
            "wv_sl": _ct(wvT[256 * i:256 * (i + 1)]),
            "wo_sl": _ct(woT[256 * i:256 * (i + 1)]),
            "qkv_sl": _ct(qkvT[256 * i:256 * (i + 1)]),
            "wout_sl": _ct(woutT[256 * i:256 * (i + 1)]),
            "wu_sl": _ct(wuT[256 * i:256 * (i + 1)]),
            "wg_sl": _ct(wgT[256 * i:256 * (i + 1)]),
            "wd_sl": _ct(wdT[1024 * i:1024 * (i + 1)]),
            "wmapqT": wmapqT, "wmapkT": wmapkT,
            "ident": ident, "masks": masks, "ohot": ohot, "ohr": ohr,
            "hsel": hsel.reshape(1, NIT),
        }
        d.update(biases)
        in_maps.append(d)
    return in_maps


def assemble_host(results):
    y = np.empty((S, D), np.float32)
    for i in range(NCORES):
        o = results[i]["out"]
        y[i * TB:(i + 1) * TB] = o[:, :TB].T
        y[(15 - i) * TB:(16 - i) * TB] = o[:, TB:].T
    return y.reshape(1, S, D)


def build_nc():
    nc = bass.Bass("TRN2", target_bir_lowering=False, debug=False,
                   detect_race_conditions=False)
    uid = [0]

    def nm(base):
        uid[0] += 1
        return f"{base}_{uid[0]}"

    xT = nc.dram_tensor("xT", [D, TC], F32, kind="ExternalInput").ap()
    out_d = nc.dram_tensor("out", [D, TC], F32, kind="ExternalOutput").ap()
    dbg = {}
    if DEBUG_TAPS:
        for i_ in range(1, 4):
            dbg[i_] = nc.dram_tensor(f"dbg{i_}", [D, TC], F32, kind="ExternalOutput").ap()
        dbgwu = nc.dram_tensor("dbgwu", [NCORES, 2, 256, E], BF16, kind="ExternalOutput").ap()
        dbgwd = nc.dram_tensor("dbgwd", [NCORES, 1024, D], BF16, kind="ExternalOutput").ap()
        dbgs = nc.dram_tensor("dbgs", [128, 24], F32, kind="ExternalOutput").ap()
    din = {}
    for n, shp, dt in [
        ("wq_sl", [256, D], F32), ("wk_sl", [256, D], F32), ("wv_sl", [256, D], F32),
        ("wo_sl", [256, D], F32), ("qkv_sl", [256, QKV_O], F32), ("wout_sl", [256, D], F32),
        ("wu_sl", [256, E], F32), ("wg_sl", [256, E], F32), ("wd_sl", [1024, D], F32),
        ("bq", [128, DC], F32), ("bk", [128, DC], F32), ("bv", [128, DC], F32),
        ("bo", [128, DC], F32), ("bqkv", [128, 48], F32), ("bout", [128, DC], F32),
        ("wmapqT", [128, MD], F32), ("wmapkT", [128, MD], F32),
        ("ident", [128, 128], BF16), ("masks", [128, 4 * TB], BF16),
        ("ohot", [128, 16 * H], BF16), ("ohr", [H, 128 * H], F32),
        ("hsel", [1, NIT], F32),
    ]:
        din[n] = nc.dram_tensor(n, shp, dt, kind="ExternalInput").ap()

    ag_a_in = nc.dram_tensor("ag_a_in", [4, 256, D], BF16, kind="Internal").ap()
    ag_a_out = nc.dram_tensor("ag_a_out", [NCORES, 4, 256, D], BF16, kind="Internal", addr_space="Shared").ap()
    ag_b_in = nc.dram_tensor("ag_b_in", [256, QKV_O + D], BF16, kind="Internal").ap()
    ag_b_out = nc.dram_tensor("ag_b_out", [NCORES, 256, QKV_O + D], BF16, kind="Internal", addr_space="Shared").ap()
    ag_ug_in = nc.dram_tensor("ag_ug_in", [2, 256, E], BF16, kind="Internal").ap()
    ag_ug_out = nc.dram_tensor("ag_ug_out", [NCORES, 2, 256, E], BF16, kind="Internal", addr_space="Shared").ap()
    ag_dn_in = nc.dram_tensor("ag_dn_in", [1024, D], BF16, kind="Internal").ap()
    ag_dn_out = nc.dram_tensor("ag_dn_out", [NCORES, 1024, D], BF16, kind="Internal", addr_space="Shared").ap()
    ar_sc_in = nc.dram_tensor("ar_sc_in", [1, 4], F32, kind="Internal").ap()
    ar_sc_out = nc.dram_tensor("ar_sc_out", [1, 4], F32, kind="Internal").ap()
    ar_kv_in = nc.dram_tensor("ar_kv_in", [MD, H * DH + H], F32, kind="Internal").ap()
    ar_kv_out = nc.dram_tensor("ar_kv_out", [MD, H * DH + H], F32, kind="Internal").ap()
    ar_bd_in = nc.dram_tensor("ar_bd_in", [1, 2], F32, kind="Internal").ap()
    ar_bd_out = nc.dram_tensor("ar_bd_out", [1, 2], F32, kind="Internal").ap()
    ag_k_in = nc.dram_tensor("ag_k_in", [2, H, 128, TB], BF16, kind="Internal").ap()
    ag_k_out_t = nc.dram_tensor("ag_k_out", [NCORES, 2, H, 128, TB], BF16, kind="Internal", addr_space="Shared")
    ag_v_in = nc.dram_tensor("ag_v_in", [2, H, 2, 128, 256], BF16, kind="Internal").ap()
    agv_gat_t = nc.dram_tensor("agv_gat", [NCORES, 2, H, 2, 128, 256], BF16, kind="Internal", addr_space="Shared")
    RG = [list(range(NCORES))]

    with contextlib.ExitStack() as st:
        sb = st.enter_context
        cbuf = sb(nc.sbuf_tensor("cbuf", [128, 512], BF16)).ap()
        x_res = sb(nc.sbuf_tensor("x_res", [128, DC * TC], F32)).ap()
        xhat = sb(nc.sbuf_tensor("xhat", [128, DC * TC], BF16)).ap()
        ones_b = sb(nc.sbuf_tensor("ones_b", [128, 1], BF16)).ap()
        onesr_f = sb(nc.sbuf_tensor("onesr_f", [1, 128], F32)).ap()
        ones_f = sb(nc.sbuf_tensor("ones_f", [128, 1], F32)).ap()
        csb = {}
        for n in ["bq", "bk", "bv", "bo", "bqkv", "bout", "ident",
                  "ohot", "hsel"]:
            shp = list(din[n].tensor.shape)
            csb[n] = sb(nc.sbuf_tensor("c_" + n, shp, din[n].tensor.dtype)).ap()
        wmapq_b = sb(nc.sbuf_tensor("wmapq_b", [128, MD], BF16)).ap()
        wmapk_b = sb(nc.sbuf_tensor("wmapk_b", [128, MD], BF16)).ap()
        qparams = sb(nc.sbuf_tensor("qparams", [128, 9], F32)).ap()
        qrow = sb(nc.sbuf_tensor("qrow", [1, 9], F32)).ap()
        negc = sb(nc.sbuf_tensor("negc", [128, 1], F32)).ap()
        biasv = sb(nc.sbuf_tensor("biasv", [128, NIT], F32)).ap()
        scl = sb(nc.sbuf_tensor("scl", [1, 16], F32)).ap()
        nmax = sb(nc.sbuf_tensor("nmax", [1, 4], F32)).ap()
        absacc = sb(nc.sbuf_tensor("absacc", [128, 24], F32)).ap()
        scl24 = sb(nc.sbuf_tensor("scl24", [1, 24], F32)).ap()
        spc = sb(nc.sbuf_tensor("spc", [128, 8], BF16)).ap()
        block = st.enter_context(nc.Block())

        sems = {}
        C = {}
        _dummies = [sb(nc.semaphore(f"DUMMY{i_}")) for i_ in range(6)]
        for n in ["SY", "G", "S2", "CC", "MM", "VE", "SC",
                  "QL0", "QL1", "QS0", "QS1", "WL0", "WL1", "UG0", "UG1",
                  "DN0", "DN1", "KV0", "KV1", "KO0", "KO1", "CB0", "CB1"]:
            sems[n] = sb(nc.semaphore(n))
            C[n] = 0
        SY, G, S2, CCs, MM, VE, SC = (sems[n] for n in ["SY", "G", "S2", "CC", "MM", "VE", "SC"])

        def tick(name, ins, by):
            ins.then_inc(sems[name], by)
            C[name] += by
            return C[name]

        ve = lambda ins: tick("VE", ins, 1)
        scc = lambda ins: tick("SC", ins, 1)
        mmi = lambda ins: tick("MM", ins, 1)
        sy = lambda ins: tick("SY", ins, 16)
        gp = lambda ins: tick("G", ins, 16)
        s2i = lambda ins: tick("S2", ins, 16)
        cci = lambda ins: tick("CC", ins, 1)

        def vspace(v):
            ve(v.tensor_copy(spc[:, 0:4], spc[:, 4:8]))
        dmai = lambda name, ins: tick(name, ins, 16)

        ALL_SEMS = ["SY", "G", "S2", "QS0", "QS1", "QL0", "QL1", "WL0", "WL1",
                    "UG0", "UG1", "DN0", "DN1", "KV0", "KV1", "KO0", "KO1",
                    "MM", "VE", "SC"]

        def wait_snap(e, names):
            for n in names:
                if C[n] > 0:
                    e.wait_ge(sems[n], C[n])

        def tap(idx):
            if not DEBUG_TAPS or idx not in dbg:
                return
            tgt = dict(C)

            @block.scalar
            def _(s):
                for n in ["MM", "VE", "SC"]:
                    if tgt[n]:
                        s.wait_ge(sems[n], tgt[n])
                srcv = x_res.rearrange("p (c t) -> p c t", t=TC)
                dstv = dbg[idx].rearrange("(c p) t -> p c t", p=128)
                for q in range(4):
                    s2i(s.dma_start(dstv[:, 4 * q:4 * (q + 1), :], srcv[:, 4 * q:4 * (q + 1), :]))

        def barrier(engines=("tensor", "vector", "scalar")):
            tgt = dict(C)

            def waits(e):
                for n in ["SY", "G", "S2", "CC", "MM", "VE", "SC"]:
                    if tgt[n] > 0:
                        e.wait_ge(sems[n], tgt[n])
            for name in engines:
                getattr(block, name)(waits)

        @block.sync
        def _(sync):
            src = xT.rearrange("(c p) t -> p c t", p=128)
            dst = x_res.rearrange("p (c t) -> p c t", t=TC)
            for q in range(4):
                sy(sync.dma_start(dst[:, 4 * q:4 * (q + 1), :], src[:, 4 * q:4 * (q + 1), :]))
            for n in csb:
                sy(sync.dma_start(csb[n][:, :], din[n][:, :]))
        SY_X = C["SY"]

        @block.gpsimd
        def _(g):
            gp(g.dma_start(wmapq_b[:, :], din["wmapqT"][:, :]))
            gp(g.dma_start(wmapk_b[:, :], din["wmapkT"][:, :]))
        G_MAPS = C["G"]

        @block.vector
        def _(v):
            ve(v.memset(ones_b[:, :], 1.0))
            ve(v.memset(spc[:, :], 0.0))
            ve(v.memset(scl[:, :], 0.0))
            ve(v.memset(nmax[:, :], 0.0))
            ve(v.memset(onesr_f[:, :], 1.0))
            ve(v.memset(ones_f[:, :], 1.0))

        with contextlib.ExitStack() as p0:
            qld = p0.enter_context(nc.sbuf_tensor("qld", [128, 2 * D], F32)).ap()
            qab = p0.enter_context(nc.sbuf_tensor("qab", [128, 2 * D], F32)).ap()
            qa = p0.enter_context(nc.sbuf_tensor("qa", [128, D], BF16)).ap()
            qb2 = p0.enter_context(nc.sbuf_tensor("qb2", [128, D], BF16)).ap()
            qout = p0.enter_context(nc.sbuf_tensor("qout", [128, 2 * D], BF16)).ap()
            ps_t = p0.enter_context(nc.psum_tensor("ps_t", [1, 24], F32)).ap()
            ps_bc9 = p0.enter_context(nc.psum_tensor("ps_bc9", [128, 16], F32)).ap()

            cast_a = []
            for mi, n in enumerate(["wq_sl", "wk_sl", "wv_sl", "wo_sl"]):
                for ch in range(2):
                    cast_a.append((din[n][ch * 128:(ch + 1) * 128, :],
                                   ag_a_in[mi, ch * 128:(ch + 1) * 128, :]))
            st_done, ld_done = [], []
            for j, (src, dst) in enumerate(cast_a):
                par = j % 2

                @block.gpsimd
                def _(g):
                    if j == 0:
                        g.wait_ge(G, G_MAPS)
                    else:
                        g.wait_ge(sems["QS0"], st_done[j - 1])
                    ld_done.append(gp(g.dma_start(qout[:, par * D:(par + 1) * D], src)))

                @block.sync
                def _(sync):
                    sync.wait_ge(G, ld_done[j])
                    st_done.append(dmai("QS0", sync.dma_start(dst, qout[:, par * D:(par + 1) * D])))
            QS_AGA = C["QS0"]

            @block.gpsimd
            def _(g):
                g.wait_ge(sems["QS0"], QS_AGA)
                cci(g.collective_compute("AllGather", ALU.bypass, replica_groups=RG,
                                         ins=[ag_a_in[:, :, :]], outs=[ag_a_out[:, :, :, :]]))
            CC_AGA = C["CC"]

            ffn_pieces = []
            for mi, n in enumerate(["wu_sl", "wg_sl"]):
                for l in range(2):
                    for cp in range(4):
                        ffn_pieces.append((mi, din[n][l * 128:(l + 1) * 128, cp * D:(cp + 1) * D],
                                           ag_ug_in[mi, l * 128:(l + 1) * 128, cp * D:(cp + 1) * D]))
            for ch in range(8):
                ffn_pieces.append((2, din["wd_sl"][ch * 128:(ch + 1) * 128, :],
                                   ag_dn_in[ch * 128:(ch + 1) * 128, :]))

            abs_ld, abs_sc, abs_ve = [], [], []
            for j, (mi, src, _dst) in enumerate(ffn_pieces):
                par = j % 2

                @block.sync
                def _(sync):
                    if j >= 2:
                        sync.wait_ge(SC, abs_sc[j - 2])
                    abs_ld.append(dmai("QL%d" % par, sync.dma_start(qld[:, par * D:(par + 1) * D], src)))

                @block.scalar
                def _(s):
                    s.wait_ge(sems["QL%d" % par], abs_ld[j])
                    if j >= 2:
                        s.wait_ge(VE, abs_ve[j - 2])
                    scc(s.activation(qab[:, par * D:(par + 1) * D],
                                     qld[:, par * D:(par + 1) * D], ACT.Abs))
                    abs_sc.append(scc(s.activation(qab[:, par * D:(par + 1) * D],
                                                   qab[:, par * D:(par + 1) * D],
                                                   ACT.Copy, bias=-C0_ABS[mi])))

                @block.vector
                def _(v):
                    v.wait_ge(SC, abs_sc[j])
                    ve(v.reduce_sum(absacc[:, j:j + 1], qab[:, par * D:(par + 1) * D],
                                    axis=AX))
                    abs_ve.append(C["VE"])
            VE_ABS = C["VE"]

            @block.tensor
            def _(t):
                t.wait_ge(VE, VE_ABS)
                for mi in range(3):
                    mmi(t.matmul(ps_t[:1, mi * 8:(mi + 1) * 8], ones_f[:, :],
                                 absacc[:, mi * 8:(mi + 1) * 8], start=True, stop=True))
            MM_ABS = C["MM"]

            @block.scalar
            def _(s):
                s.wait_ge(MM, MM_ABS)
                scc(s.copy(scl24[:, :], ps_t[:1, 0:24]))
            SC_S24 = C["SC"]

            @block.vector
            def _(v):
                v.wait_ge(SC, SC_S24)
                for mi in range(3):
                    ve(v.reduce_sum(scl[:, mi:mi + 1], scl24[:1, mi * 8:(mi + 1) * 8], axis=AX))
                vspace(v)
                vspace(v)
                ve(v.tensor_scalar(scl[:, 0:3], scl[:1, 0:3], 1.0 / (E * D), None, ALU.mult))
            VE_SCP = C["VE"]

            @block.scalar
            def _(s):
                s.wait_ge(VE, VE_SCP)
                s2i(s.dma_start(ar_sc_in[:, :], scl[:, 0:4]))
            S2_SC = C["S2"]

            @block.gpsimd
            def _(g):
                g.wait_ge(S2, S2_SC)
                cci(g.collective_compute("AllReduce", ALU.add, replica_groups=RG,
                                         ins=[ar_sc_in[:, :]], outs=[ar_sc_out[:, :]]))
                g.wait_ge(CCs, C["CC"])
                gp(g.dma_start(scl[:, 4:8], ar_sc_out[:, :]))
            G_SCL = C["G"]

            @block.scalar
            def _(s):
                s.wait_ge(VE, 6)
                scc(s.activation(scl[:, 8:9], ones_f[0:1, 0:1], ACT.Copy, bias=1e-10))
                scc(s.activation(scl[:, 11:12], ones_f[0:1, 0:1], ACT.Copy))
                scc(s.activation(scl[:, 9:10], scl[:, 8:9], ACT.Copy, scale=1e10, bias=-1e10))
                scc(s.activation(scl[:, 11:12], ones_f[0:1, 0:1], ACT.Copy))
                scc(s.activation(scl[:, 10:11], scl[:, 9:10], ACT.Copy, scale=1.8e-7,
                                 bias=0.5 - 1.8e-7))
            SC_HF = C["SC"]

            @block.vector
            def _(v):
                v.wait_ge(G, G_SCL)
                v.wait_ge(SC, SC_HF)
                for mi in range(3):
                    c0e = C0_ABS[mi] + FFN_EPS
                    ve(v.tensor_scalar(qrow[:, 6 + mi:7 + mi], scl[:, 4 + mi:5 + mi],
                                       1.0, c0e, ALU.mult, ALU.add))
                vspace(v)
                vspace(v)
                for mi in range(3):
                    ve(v.tensor_scalar(qrow[:, 0 + mi:1 + mi], qrow[:, 6 + mi:7 + mi],
                                       scl[:1, 10:11], None, ALU.mult))
                vspace(v)
                vspace(v)
                for mi in range(3):
                    ve(v.tensor_scalar(qrow[:, 3 + mi:4 + mi], qrow[:, 0 + mi:1 + mi],
                                       -1.0, None, ALU.mult))
            VE_QROW = C["VE"]

            @block.tensor
            def _(t):
                t.wait_ge(VE, VE_QROW)
                mmi(t.matmul(ps_bc9[:, 0:9], onesr_f[:, :], qrow[:1, :], start=True, stop=True))
            MM_QP = C["MM"]

            @block.vector
            def _(v):
                v.wait_ge(MM, MM_QP)
                ve(v.tensor_copy(qparams[:, :], ps_bc9[:, 0:9]))
            VE_QP = C["VE"]

            q_ld, q_ve, q_st = [], [], []
            SY_UG = None
            for j, (mi, src, dst) in enumerate(ffn_pieces):
                par = j % 2

                @block.sync
                def _(sync):
                    if j >= 2:
                        sync.wait_ge(VE, q_ve[j - 2])
                    q_ld.append(dmai("QL%d" % par, sync.dma_start(qld[:, par * D:(par + 1) * D], src)))

                @block.vector
                def _(v):
                    v.wait_ge(sems["QL%d" % par], q_ld[j])
                    if j == 0:
                        v.wait_ge(VE, VE_QP)
                    if j >= 2:
                        v.wait_ge(sems["QS%d" % par], q_st[j - 2])
                    ve(v.tensor_scalar(qa[:, :], qld[:, par * D:(par + 1) * D],
                                       qparams[:, mi:mi + 1], qparams[:, 6 + mi:7 + mi],
                                       ALU.is_gt, ALU.mult))
                    ve(v.tensor_scalar(qb2[:, :], qld[:, par * D:(par + 1) * D],
                                       qparams[:, 3 + mi:4 + mi], qparams[:, 6 + mi:7 + mi],
                                       ALU.is_lt, ALU.mult))
                    q_ve.append(ve(v.tensor_sub(qout[:, par * D:(par + 1) * D], qa[:, :], qb2[:, :])))

                @block.sync
                def _(sync):
                    sync.wait_ge(VE, q_ve[j])
                    q_st.append(dmai("QS%d" % par, sync.dma_start(dst, qout[:, par * D:(par + 1) * D])))
                if j == 15:
                    QS_UG = (C["QS0"], C["QS1"])
            QS_DN = (C["QS0"], C["QS1"])
            VE_QE = C["VE"]

            @block.gpsimd
            def _(g):
                g.wait_ge(sems["QS0"], QS_UG[0])
                g.wait_ge(sems["QS1"], QS_UG[1])
                cci(g.collective_compute("AllGather", ALU.bypass, replica_groups=RG,
                                         ins=[ag_ug_in[:, :, :]], outs=[ag_ug_out[:, :, :, :]]))
                g.wait_ge(sems["QS0"], QS_DN[0])
                g.wait_ge(sems["QS1"], QS_DN[1])
                cci(g.collective_compute("AllGather", ALU.bypass, replica_groups=RG,
                                         ins=[ag_dn_in[:, :]], outs=[ag_dn_out[:, :, :]]))
            CC_UG = C["CC"] - 1
            CC_DN = C["CC"]

        def emit_cast_b():
            cast_b = []
            for l in range(2):
                for cp in range(3):
                    cast_b.append((din["qkv_sl"][l * 128:(l + 1) * 128, cp * D:(cp + 1) * D],
                                   ag_b_in[l * 128:(l + 1) * 128, cp * D:(cp + 1) * D]))
            for l in range(2):
                cast_b.append((din["wout_sl"][l * 128:(l + 1) * 128, :],
                               ag_b_in[l * 128:(l + 1) * 128, QKV_O:QKV_O + D]))
            st_b, ld_b = [], []
            jj = 0
            for (srcp, dstp) in cast_b:
                for q in range(4):
                    j = jj
                    src = srcp[:, q * 512:(q + 1) * 512]
                    dst = dstp[:, q * 512:(q + 1) * 512]

                    @block.gpsimd
                    def _(g, j=j, src=src):
                        if j > 0:
                            g.wait_ge(sems["CB1"], st_b[j - 1])
                        ld_b.append(dmai("CB0", g.dma_start(cbuf[:, :], src)))

                    @block.sync
                    def _(sync, j=j, dst=dst):
                        sync.wait_ge(sems["CB0"], ld_b[j])
                        st_b.append(dmai("CB1", sync.dma_start(dst, cbuf[:, :])))
                    jj += 1
            QS_AGB = C["CB1"]

            @block.gpsimd
            def _(g):
                g.wait_ge(sems["CB1"], QS_AGB)
                cci(g.collective_compute("AllGather", ALU.bypass, replica_groups=RG,
                                         ins=[ag_b_in[:, :]], outs=[ag_b_out[:, :, :]]))
            return C["CC"]

        def emit_ln(light_entry=False):
            with contextlib.ExitStack() as ls:
                pm_s = ls.enter_context(nc.psum_tensor(nm("pm_s"), [1, TC], F32)).ap()
                pm_q = ls.enter_context(nc.psum_tensor(nm("pm_q"), [1, TC], F32)).ap()
                pm_mb = ls.enter_context(nc.psum_tensor(nm("pm_mb"), [128, TC], F32)).ap()
                pm_rb = ls.enter_context(nc.psum_tensor(nm("pm_rb"), [128, TC], F32)).ap()
                xb_t = ls.enter_context(nc.sbuf_tensor(nm("xb_t"), [128, 2 * TC], BF16)).ap()
                sq_t = ls.enter_context(nc.sbuf_tensor(nm("sq_t"), [128, 2 * TC], BF16)).ap()
                lnm = ls.enter_context(nc.sbuf_tensor(nm("lnm"), [1, TC], F32)).ap()
                lnv = ls.enter_context(nc.sbuf_tensor(nm("lnv"), [1, TC], F32)).ap()
                lnw = ls.enter_context(nc.sbuf_tensor(nm("lnw"), [1, TC], F32)).ap()
                ftmp = ls.enter_context(nc.sbuf_tensor(nm("ftmp"), [128, TB], F32)).ap()
                cast_ve, chunk_mm = [], []
                for c in range(DC):
                    par = c % 2

                    @block.vector
                    def _(v):
                        if c == 0 and light_entry:
                            v.wait_ge(SY, SY_X)
                            for nq in ("QS0", "QS1"):
                                if C[nq]:
                                    v.wait_ge(sems[nq], C[nq])
                        elif c == 0:
                            wait_snap(v, ALL_SEMS)
                        if c >= 2:
                            v.wait_ge(MM, chunk_mm[c - 2])
                        ve(v.tensor_copy(xb_t[:, par * TC:(par + 1) * TC],
                                         x_res[:, c * TC:(c + 1) * TC]))
                        cast_ve.append(ve(v.tensor_mul(sq_t[:, par * TC:(par + 1) * TC],
                                                       xb_t[:, par * TC:(par + 1) * TC],
                                                       xb_t[:, par * TC:(par + 1) * TC])))

                    @block.tensor
                    def _(t):
                        t.wait_ge(VE, cast_ve[c])
                        for th in range(2):
                            t.matmul(pm_s[:1, th * TB:(th + 1) * TB], ones_b[:, :],
                                     xb_t[:, par * TC + th * TB:par * TC + (th + 1) * TB],
                                     start=(c == 0), stop=(c == DC - 1))
                        ins = None
                        for th in range(2):
                            ins = t.matmul(pm_q[:1, th * TB:(th + 1) * TB], ones_b[:, :],
                                           sq_t[:, par * TC + th * TB:par * TC + (th + 1) * TB],
                                           start=(c == 0), stop=(c == DC - 1))
                        chunk_mm.append(mmi(ins))
                MM_STAT = C["MM"]

                @block.vector
                def _(v):
                    v.wait_ge(MM, MM_STAT)
                    ve(v.tensor_scalar(lnm[:, :], pm_s[:1, :], 1.0 / D, None, ALU.mult))
                    ve(v.tensor_scalar(lnv[:, :], pm_q[:1, :], 1.0 / D, LN_EPS, ALU.mult, ALU.add))
                    ve(v.tensor_mul(lnw[:, :], lnm[:, :], lnm[:, :]))
                    ve(v.tensor_sub(lnv[:, :], lnv[:, :], lnw[:, :]))
                    ve(v.reciprocal(lnv[:, :], lnv[:, :]))
                VE_FIN = C["VE"]

                @block.scalar
                def _(s):
                    s.wait_ge(VE, VE_FIN)
                    scc(s.activation(lnw[:, :], lnv[:, :], ACT.Sqrt))
                SC_RSTD = C["SC"]

                @block.tensor
                def _(t):
                    t.wait_ge(SC, SC_RSTD)
                    ins = None
                    for th in range(2):
                        t.matmul(pm_mb[:, th * TB:(th + 1) * TB], onesr_f[:, :],
                                 lnm[:1, th * TB:(th + 1) * TB], start=True, stop=True)
                    for th in range(2):
                        ins = t.matmul(pm_rb[:, th * TB:(th + 1) * TB], onesr_f[:, :],
                                       lnw[:1, th * TB:(th + 1) * TB], start=True, stop=True)
                    mmi(ins)
                MM_BC = C["MM"]

                @block.vector
                def _(v):
                    v.wait_ge(MM, MM_BC)
                    for c in range(DC):
                        for th in range(2):
                            ve(v.tensor_sub(ftmp[:, :], x_res[:, c * TC + th * TB:c * TC + (th + 1) * TB],
                                            pm_mb[:, th * TB:(th + 1) * TB]))
                            ve(v.tensor_mul(xhat[:, c * TC + th * TB:c * TC + (th + 1) * TB],
                                            ftmp[:, :], pm_rb[:, th * TB:(th + 1) * TB]))

        def xhat_slice(dc, th):
            return xhat[:, dc * TC + th * TB:dc * TC + (th + 1) * TB]

        def emit_proj(O, src_slice_fn, mat_slice_fn, evac_fn, wait_cc=None, post_fn=None,
                      light_entry=None, load_eng="sync"):
            n_og = O // TB
            with contextlib.ExitStack() as pscope:
                wstr = pscope.enter_context(
                    nc.sbuf_tensor(nm("wstr"), [128, 2 * DC * TB], BF16)).ap()
                pp = [[pscope.enter_context(nc.psum_tensor(nm("pp"), [128, TB], F32)).ap()
                       for _t in range(2)] for _pa in range(2)]
                ld, last_use, free_at = [], [], []
                for og in range(n_og):
                    def loads(sync):
                        if og >= 2:
                            sync.wait_ge(MM, last_use[4 * (og - 2) + 3])
                        if og == 0 and wait_cc is not None:
                            sync.wait_ge(CCs, wait_cc)
                        if og == 0 and light_entry is not None:
                            sync.wait_ge(VE, light_entry[0])
                            sync.wait_ge(MM, light_entry[1])
                        elif og == 0:
                            wait_snap(sync, ALL_SEMS)
                        par = og % 2
                        dst4 = wstr.rearrange("q (b r l c) -> q b r l c", b=2, r=8, c=TB)
                        for r in range(8):
                            src = mat_slice_fn(r, og * TB)
                            dmai("WL%d" % par, sync.dma_start(dst4[:, par, r, :, :], src))
                        ld.append(C["WL%d" % par])
                    if load_eng == "scalar":
                        block.scalar(loads)
                    else:
                        block.sync(loads)
                    for j in range(4):
                        oc = og * 4 + j
                        pa = oc % 2

                        @block.tensor
                        def _(t):
                            t.wait_ge(sems["WL%d" % (og % 2)], ld[og])
                            if oc == 0 and light_entry is None:
                                wait_snap(t, ALL_SEMS)
                            elif oc == 0:
                                t.wait_ge(VE, light_entry[0])
                            if oc >= 2:
                                for th in range(2):
                                    snme, val = free_at[2 * (oc - 2) + th]
                                    t.wait_ge(sems[snme], val)
                            ins = None
                            for th in range(2):
                                for dc in range(DC):
                                    ins = t.matmul(
                                        pp[pa][th],
                                        wstr[:, (og % 2) * DC * TB + dc * TB + j * 128:
                                             (og % 2) * DC * TB + dc * TB + (j + 1) * 128],
                                        src_slice_fn(dc, th),
                                        start=(dc == 0), stop=(dc == DC - 1))
                                mmi(ins)
                        last_use.append(C["MM"])
                        for th in range(2):
                            free_at.append(evac_fn(oc, th, pp[pa][th], C["MM"] - 1 + th))
                        if post_fn is not None:
                            post_fn(oc)
                tgt = dict(C)

                @block.tensor
                def _(t):
                    for nme in ["VE", "SC"]:
                        if tgt[nme]:
                            t.wait_ge(sems[nme], tgt[nme])

        def mk_mat_a(mi):
            def f(r, o0):
                return ag_a_out[r, mi].rearrange("(l p) c -> p l c", p=128)[:, :, o0:o0 + TB]
            return f

        def mk_mat_b(col0):
            def f(r, o0):
                return ag_b_out[r].rearrange("(l p) c -> p l c", p=128)[:, :, col0 + o0:col0 + o0 + TB]
            return f

        if PH_ATTA:
            emit_ln(light_entry=True)
            with contextlib.ExitStack() as pa:
                qprime = pa.enter_context(nc.sbuf_tensor("qprime", [MD, H * TC], BF16)).ap()
                kvac = pa.enter_context(nc.sbuf_tensor("kvac", [MD, H * DH], F32)).ap()
                ksum = pa.enter_context(nc.sbuf_tensor("ksum", [MD, H], F32)).ap()
                hh_sb = pa.enter_context(nc.sbuf_tensor("hh_sb", [128, 2 * TC], BF16)).ap()

                with contextlib.ExitStack() as pkv:
                    ktm = pkv.enter_context(nc.sbuf_tensor("ktm", [128, H * 8 * MD], BF16)).ap()
                    kp_sb = pkv.enter_context(nc.sbuf_tensor("kp_sb", [MD, 2 * TC], BF16)).ap()
                    el_t = pkv.enter_context(nc.sbuf_tensor("el_t", [MD, 2 * TB], F32)).ap()
                    el_e = pkv.enter_context(nc.sbuf_tensor("el_e", [MD, 2 * TB], F32)).ap()
                    vtm_sb = pkv.enter_context(nc.sbuf_tensor("vtm_sb", [128, 2 * TC], BF16)).ap()
                    pm_map = [pkv.enter_context(nc.psum_tensor(nm("pmap"), [MD, TB], F32)).ap()
                              for _t in range(2)]
                    ps_tr = pkv.enter_context(nc.psum_tensor("ps_tr", [128, TC], BF16)).ap()
                    ps_kv = pkv.enter_context(nc.psum_tensor("ps_kv", [MD, DH], F32)).ap()

                    def evac_simple(bias_name, dstbuf):
                        def f(oc, th, ps, mmv):
                            @block.vector
                            def _(v):
                                v.wait_ge(MM, mmv)
                                ve(v.tensor_scalar(
                                    dstbuf[:, (oc % 2) * TC + th * TB:(oc % 2) * TC + (th + 1) * TB],
                                    ps, csb[bias_name][:, oc:oc + 1], None, ALU.add))
                            return ("VE", C["VE"])
                        return f

                    last_elu = [0]

                    def post_map_elu(oc, wmap, outbuf, do_ksum):
                        h, par = oc, oc % 2
                        vbase = C["VE"]

                        @block.tensor
                        def _(t):
                            t.wait_ge(VE, vbase)
                            t.wait_ge(VE, last_elu[0])
                            t.wait_ge(G, G_MAPS)
                            for th in range(2):
                                mmi(t.matmul(pm_map[th], wmap,
                                             hh_sb[:, par * TC + th * TB:par * TC + (th + 1) * TB],
                                             start=True, stop=True))
                        mm_map = C["MM"]

                        @block.vector
                        def _(v):
                            v.wait_ge(MM, mm_map)
                            for th in range(2):
                                ve(v.tensor_scalar(el_t[:, th * TB:(th + 1) * TB],
                                                   pm_map[th], 0.0, None, ALU.min))
                        ve_min = C["VE"]

                        @block.scalar
                        def _(s):
                            s.wait_ge(VE, ve_min)
                            scc(s.activation(el_e[:, :], el_t[:, :], ACT.Exp))
                        sc_exp = C["SC"]

                        @block.vector
                        def _(v):
                            v.wait_ge(SC, sc_exp)
                            for th in range(2):
                                ve(v.scalar_tensor_tensor(
                                    outbuf[:, (h if outbuf is qprime else par) * TC + th * TB:
                                           (h if outbuf is qprime else par) * TC + (th + 1) * TB],
                                    pm_map[th], 0.0, el_e[:, th * TB:(th + 1) * TB],
                                    ALU.max, ALU.add))
                            if do_ksum:
                                ve(v.reduce_sum(ksum[:, h:h + 1],
                                                kp_sb[:, par * TC:(par + 1) * TC], axis=AX))
                        last_elu[0] = C["VE"]

                    def post_q(oc):
                        post_map_elu(oc, wmapq_b[:, :], qprime, False)
                    emit_proj(D, xhat_slice, mk_mat_a(0), evac_simple("bq", hh_sb),
                              wait_cc=CC_AGA, light_entry=(C["VE"], C["MM"]),
                              post_fn=post_q, load_eng="scalar")

                    ktr_done = []

                    def post_k(oc):
                        h, par = oc, oc % 2
                        post_map_elu(oc, wmapk_b[:, :], kp_sb, True)
                        ve_kp = C["VE"]

                        @block.tensor
                        def _(t):
                            t.wait_ge(VE, ve_kp)
                            ins = None
                            for j2 in range(8):
                                ins = t.transpose(ps_tr[:, j2 * MD:(j2 + 1) * MD],
                                                  kp_sb[:MD, par * TC + j2 * 128:par * TC + (j2 + 1) * 128],
                                                  csb["ident"][:MD, :MD])
                            mmi(ins)
                        mm_tr = C["MM"]

                        @block.vector
                        def _(v):
                            v.wait_ge(MM, mm_tr)
                            ve(v.tensor_copy(ktm[:, h * 8 * MD:(h + 1) * 8 * MD], ps_tr[:, 0:8 * MD]))
                        ktr_done.append(C["VE"])
                    emit_proj(D, xhat_slice, mk_mat_a(1), evac_simple("bk", hh_sb), post_fn=post_k,
                              load_eng="scalar")

                    def post_v(oc):
                        h, par = oc, oc % 2
                        vbase = C["VE"]

                        @block.tensor
                        def _(t):
                            t.wait_ge(VE, vbase)
                            ins = None
                            for j2 in range(8):
                                ins = t.transpose(ps_tr[:, j2 * 128:(j2 + 1) * 128],
                                                  hh_sb[:, par * TC + j2 * 128:par * TC + (j2 + 1) * 128],
                                                  csb["ident"][:, :])
                            mmi(ins)
                        mm_vt = C["MM"]

                        @block.vector
                        def _(v):
                            v.wait_ge(MM, mm_vt)
                            ve(v.tensor_copy(vtm_sb[:, par * TC:(par + 1) * TC], ps_tr[:, :]))
                        ve_vt = C["VE"]

                        @block.tensor
                        def _(t):
                            t.wait_ge(VE, ve_vt)
                            ins = None
                            for j2 in range(8):
                                ins = t.matmul(ps_kv,
                                               ktm[:, h * 8 * MD + j2 * MD:h * 8 * MD + (j2 + 1) * MD],
                                               vtm_sb[:, par * TC + j2 * 128:par * TC + (j2 + 1) * 128],
                                               start=(j2 == 0), stop=(j2 == 7))
                            mmi(ins)
                        mm_kv = C["MM"]

                        @block.vector
                        def _(v):
                            v.wait_ge(MM, mm_kv)
                            ve(v.tensor_copy(kvac[:, h * DH:(h + 1) * DH], ps_kv))
                    emit_proj(D, xhat_slice, mk_mat_a(2), evac_simple("bv", hh_sb), post_fn=post_v,
                              load_eng="scalar")

                VE_KVALL = C["VE"]

                @block.scalar
                def _(s):
                    s.wait_ge(VE, VE_KVALL)
                    s2i(s.dma_start(ar_kv_in[:, 0:H * DH], kvac[:, :]))
                    s2i(s.dma_start(ar_kv_in[:, H * DH:H * DH + H], ksum[:, :]))
                S2_KV = C["S2"]

                @block.gpsimd
                def _(g):
                    g.wait_ge(S2, S2_KV)
                    cci(g.collective_compute("AllReduce", ALU.add, replica_groups=RG,
                                             ins=[ar_kv_in[:, :]], outs=[ar_kv_out[:, :]]))
                    g.wait_ge(CCs, C["CC"])
                    gp(g.dma_start(kvac[:, :], ar_kv_out[:, 0:H * DH]))
                    gp(g.dma_start(ksum[:, :], ar_kv_out[:, H * DH:H * DH + H]))
                G_KVR = C["G"]

                with contextlib.ExitStack() as pt:
                    kvr_bf = pt.enter_context(nc.sbuf_tensor("kvr_bf", [MD, H * DH], BF16)).ap()
                    ksr_bf = pt.enter_context(nc.sbuf_tensor("ksr_bf", [MD, H], BF16)).ap()
                    rn_sb = pt.enter_context(nc.sbuf_tensor("rn_sb", [1, TC], F32)).ap()
                    rb_sb = pt.enter_context(nc.sbuf_tensor("rb_sb", [128, TC], F32)).ap()
                    ps_n = pt.enter_context(nc.psum_tensor("ps_n", [1, TC], F32)).ap()
                    ps_rb = pt.enter_context(nc.psum_tensor("ps_rb", [128, TC], F32)).ap()
                    ps_o = pt.enter_context(nc.psum_tensor("ps_o", [128, TC], F32)).ap()

                    @block.vector
                    def _(v):
                        v.wait_ge(G, G_KVR)
                        ve(v.tensor_copy(kvr_bf[:, :], kvac[:, :]))
                        ve(v.tensor_copy(ksr_bf[:, :], ksum[:, :]))
                    VE_KVBF = C["VE"]

                    for h in range(H):
                        vpre = C["VE"]

                        @block.tensor
                        def _(t):
                            t.wait_ge(VE, max(VE_KVBF, vpre))
                            ins = None
                            for th in range(2):
                                ins = t.matmul(ps_n[:1, th * TB:(th + 1) * TB],
                                               ksr_bf[:, h:h + 1],
                                               qprime[:, h * TC + th * TB:h * TC + (th + 1) * TB],
                                               start=True, stop=True)
                            mmi(ins)
                        mm_n = C["MM"]

                        @block.vector
                        def _(v):
                            v.wait_ge(MM, mm_n)
                            ve(v.tensor_scalar(rn_sb[:, :], ps_n[:1, :], ATTA_EPS, None, ALU.add))
                            ve(v.reciprocal(rn_sb[:, :], rn_sb[:, :]))
                        ve_rn = C["VE"]

                        @block.tensor
                        def _(t):
                            t.wait_ge(VE, ve_rn)
                            ins = None
                            for th in range(2):
                                t.matmul(ps_rb[:, th * TB:(th + 1) * TB], onesr_f[:, :],
                                         rn_sb[:1, th * TB:(th + 1) * TB], start=True, stop=True)
                            for th in range(2):
                                ins = t.matmul(ps_o[:, th * TB:(th + 1) * TB],
                                               kvr_bf[:, h * DH:(h + 1) * DH],
                                               qprime[:, h * TC + th * TB:h * TC + (th + 1) * TB],
                                               start=True, stop=True)
                            mmi(ins)
                        mm_o = C["MM"]

                        @block.vector
                        def _(v):
                            v.wait_ge(MM, mm_o)
                            ve(v.tensor_copy(rb_sb[:, :], ps_rb[:, :]))
                            ve(v.tensor_mul(xhat[:, h * TC:(h + 1) * TC], ps_o[:, :], rb_sb[:, :]))

                def evac_wo(oc, th, ps, mmv):
                    @block.vector
                    def _(v):
                        v.wait_ge(MM, mmv)
                        ve(v.scalar_tensor_tensor(
                            x_res[:, oc * TC + th * TB:oc * TC + (th + 1) * TB],
                            ps, csb["bo"][:, oc:oc + 1],
                            x_res[:, oc * TC + th * TB:oc * TC + (th + 1) * TB],
                            ALU.add, ALU.add))
                    return ("VE", C["VE"])
                emit_proj(D, xhat_slice, mk_mat_a(3), evac_wo)
            tap(1)
            barrier()

        def emit_ffn():
            emit_ln()
            with contextlib.ExitStack() as pf:
                wu_b = pf.enter_context(nc.sbuf_tensor(nm("wu_b"), [128, 2 * 4096], BF16)).ap()
                wg_b = pf.enter_context(nc.sbuf_tensor(nm("wg_b"), [128, 2 * 4096], BF16)).ap()
                wd_b = pf.enter_context(nc.sbuf_tensor(nm("wd_b"), [128, 2 * 2048], BF16)).ap()
                hid = pf.enter_context(nc.sbuf_tensor(nm("hid"), [128, 2 * 8 * TC], BF16)).ap()
                sil = pf.enter_context(nc.sbuf_tensor(nm("sil"), [128, TB], F32)).ap()
                ps_u = [pf.enter_context(nc.psum_tensor(nm("ps_u"), [128, TB], F32)).ap()
                        for _t in range(2)]
                ps_g = [pf.enter_context(nc.psum_tensor(nm("ps_g"), [128, TB], F32)).ap()
                        for _t in range(2)]
                ps_d = [pf.enter_context(nc.psum_tensor(nm("ps_d"), [128, TB], F32)).ap()
                        for _t in range(4)]

                ug_ld, ug_use, dn_ld, dn_use = {}, {}, {}, {}
                ev_ug, dn_ev, hid_ve = {}, {}, {}

                def load_ug(g, pc):
                    @block.sync
                    def _(sync):
                        if (g, pc) == (0, 0):
                            sync.wait_ge(CCs, CC_UG)
                            wait_snap(sync, ALL_SEMS)
                        prev = (g * 4 + pc) - 2
                        if prev >= 0:
                            sync.wait_ge(MM, ug_use[(prev // 4, prev % 4)])
                        par = pc % 2
                        e0 = g * 1024 + pc * 256
                        for buf, m in ((wu_b, 0), (wg_b, 1)):
                            dst = buf.rearrange("q (b r l e) -> q b r l e", b=2, r=8, e=256)
                            for r in range(8):
                                src = ag_ug_out[r, m].rearrange("(l p) e -> p l e", p=128)[:, :, e0:e0 + 256]
                                dmai("UG%d" % par, sync.dma_start(dst[:, par, r, :, :], src))
                        ug_ld[(g, pc)] = C["UG%d" % par]

                def load_dn(g, inst):
                    @block.sync
                    def _(sync):
                        if (g, inst) == (0, 0):
                            sync.wait_ge(CCs, CC_DN)
                            wait_snap(sync, ALL_SEMS)
                        prev = (g * 8 + inst) - 2
                        if prev >= 0:
                            sync.wait_ge(MM, dn_use[(prev // 8, prev % 8)])
                        par = inst % 2
                        dst = wd_b.rearrange("q (b ec d) -> q b ec d", b=2, d=256)
                        src = ag_dn_out[g].rearrange("(ec p) d -> p ec d", p=128)[:, :, inst * 256:(inst + 1) * 256]
                        dmai("DN%d" % par, sync.dma_start(dst[:, par, :, :], src))
                        dn_ld[(g, inst)] = C["DN%d" % par]

                for g in range(8):
                    for pc in range(4):
                        load_ug(g, pc)
                        par = pc % 2
                        for e2 in range(2):
                            ecl = pc * 2 + e2
                            eidx = g * 8 + ecl

                            @block.tensor
                            def _(t):
                                t.wait_ge(sems["UG%d" % (pc % 2)], ug_ld[(g, pc)])
                                if eidx == 0:
                                    wait_snap(t, ALL_SEMS)
                                if eidx >= 1:
                                    psc, pve = ev_ug[eidx - 1]
                                    t.wait_ge(SC, psc)
                                    t.wait_ge(VE, pve)
                                ins = None
                                for th in range(2):
                                    for dc in range(DC):
                                        t.matmul(ps_u[th],
                                                 wu_b[:, par * 4096 + dc * 256 + e2 * 128:
                                                      par * 4096 + dc * 256 + (e2 + 1) * 128],
                                                 xhat_slice(dc, th),
                                                 start=(dc == 0), stop=(dc == DC - 1))
                                for th in range(2):
                                    for dc in range(DC):
                                        ins = t.matmul(ps_g[th],
                                                       wg_b[:, par * 4096 + dc * 256 + e2 * 128:
                                                            par * 4096 + dc * 256 + (e2 + 1) * 128],
                                                       xhat_slice(dc, th),
                                                       start=(dc == 0), stop=(dc == DC - 1))
                                mmi(ins)
                            ug_use[(g, pc)] = C["MM"]
                            mm_ug = C["MM"]

                            for th in range(2):
                                ve_now = C["VE"]

                                @block.scalar
                                def _(s):
                                    s.wait_ge(MM, mm_ug)
                                    s.wait_ge(VE, ve_now)
                                    scc(s.activation(sil[:, :], ps_u[th], ACT.Silu))
                                sc_sil = C["SC"]

                                @block.vector
                                def _(v):
                                    v.wait_ge(SC, sc_sil)
                                    ve(v.tensor_mul(
                                        hid[:, (g % 2) * 8 * TC + ecl * TC + th * TB:
                                            (g % 2) * 8 * TC + ecl * TC + (th + 1) * TB],
                                        ps_g[th], sil[:, :]))
                            ev_ug[eidx] = (C["SC"], C["VE"])
                    hid_ve[g] = C["VE"]

                    for inst in range(8):
                        load_dn(g, inst)

                        @block.tensor
                        def _(t):
                            t.wait_ge(sems["DN%d" % (inst % 2)], dn_ld[(g, inst)])
                            t.wait_ge(VE, hid_ve[g])
                            prev = (g * 8 + inst) - 1
                            if prev >= 0:
                                t.wait_ge(VE, dn_ev[(prev // 8, prev % 8)])
                            for dl in range(2):
                                for th in range(2):
                                    ins = None
                                    for ec in range(8):
                                        ins = t.matmul(
                                            ps_d[dl * 2 + th],
                                            wd_b[:, (inst % 2) * 2048 + ec * 256 + dl * 128:
                                                 (inst % 2) * 2048 + ec * 256 + (dl + 1) * 128],
                                            hid[:, (g % 2) * 8 * TC + ec * TC + th * TB:
                                                (g % 2) * 8 * TC + ec * TC + (th + 1) * TB],
                                            start=(ec == 0), stop=(ec == 7))
                                    mmi(ins)
                        dn_use[(g, inst)] = C["MM"]
                        mm_d = C["MM"]

                        @block.vector
                        def _(v):
                            v.wait_ge(MM, mm_d)
                            for dl in range(2):
                                dcc = inst * 2 + dl
                                for th in range(2):
                                    ve(v.scalar_tensor_tensor(
                                        x_res[:, dcc * TC + th * TB:dcc * TC + (th + 1) * TB],
                                        ps_d[dl * 2 + th], 1.0,
                                        x_res[:, dcc * TC + th * TB:dcc * TC + (th + 1) * TB],
                                        ALU.mult, ALU.add))
                        dn_ev[(g, inst)] = C["VE"]
                tgt = dict(C)

                @block.tensor
                def _(t):
                    t.wait_ge(VE, tgt["VE"])
            barrier()

        CC_AGB = emit_cast_b()

        if PH_FFN1:
            emit_ffn()
            tap(2)

        if PH_ATTB:
            emit_ln()
            with contextlib.ExitStack() as pb:
                q_sb = pb.enter_context(nc.sbuf_tensor("q_sb", [128, H * TC], BF16)).ap()

                with contextlib.ExitStack() as pk:
                    ktmp = pk.enter_context(nc.sbuf_tensor("ktmp", [128, 2 * TC], BF16)).ap()
                    sqn = pk.enter_context(nc.sbuf_tensor("sqn", [128, TC], BF16)).ap()
                    ps_nrm = pk.enter_context(nc.psum_tensor("ps_nrm", [1, TC], F32)).ap()
                    ps_vtr = pk.enter_context(nc.psum_tensor("ps_vtr", [128, TC], BF16)).ap()
                    kdma, vdma = [], []

                    def post_norm(buf, off_fn, h, slot):
                        vbase = C["VE"]

                        @block.vector
                        def _(v):
                            ve(v.tensor_mul(sqn[:, :], buf[:, off_fn:off_fn + TC],
                                            buf[:, off_fn:off_fn + TC]))
                        ve_sq = C["VE"]

                        @block.tensor
                        def _(t):
                            t.wait_ge(VE, ve_sq)
                            ins = None
                            for th in range(2):
                                ins = t.matmul(ps_nrm[:1, th * TB:(th + 1) * TB], ones_b[:, :],
                                               sqn[:, th * TB:(th + 1) * TB], start=True, stop=True)
                            mmi(ins)
                        mm_nr = C["MM"]

                        @block.vector
                        def _(v):
                            v.wait_ge(MM, mm_nr)
                            ve(v.reduce_max(nmax[:, 2 + slot:3 + slot], ps_nrm[:1, :], axis=AX))
                            vspace(v)
                            vspace(v)
                            if h == 0:
                                ve(v.tensor_copy(nmax[:, slot:slot + 1], nmax[:, 2 + slot:3 + slot]))
                            else:
                                ve(v.tensor_max(nmax[:, slot:slot + 1], nmax[:, slot:slot + 1],
                                                nmax[:, 2 + slot:3 + slot]))
                            vspace(v)

                    def evac_k(oc, th, ps, mmv):
                        h = oc

                        @block.vector
                        def _(v):
                            v.wait_ge(MM, mmv)
                            if h >= 2 and th == 0:
                                v.wait_ge(sems["KO%d" % (h % 2)], kdma[h - 2])
                            ve(v.tensor_scalar(
                                ktmp[:, (h % 2) * TC + th * TB:(h % 2) * TC + (th + 1) * TB],
                                ps, csb["bqkv"][:, oc + 16:oc + 17], None, ALU.add))
                        return ("VE", C["VE"])

                    def post_kp(oc):
                        h, par = oc, oc % 2
                        post_norm(ktmp, par * TC, h, 1)
                        ve_k = C["VE"]

                        @block.scalar
                        def _(s):
                            s.wait_ge(VE, ve_k)
                            dmai("KO%d" % par, s.dma_start(ag_k_in[0, h], ktmp[:, par * TC:par * TC + TB]))
                            dmai("KO%d" % par, s.dma_start(ag_k_in[1, h], ktmp[:, par * TC + TB:par * TC + TC]))
                        kdma.append(C["KO%d" % par])

                    def srcK(dc, th):
                        return xhat_slice(dc, th)
                    emit_proj(D, srcK, mk_mat_b(16 * 128), evac_k, wait_cc=CC_AGB, post_fn=post_kp)

                    def evac_v2(oc, th, ps, mmv):
                        h = oc

                        @block.vector
                        def _(v):
                            v.wait_ge(MM, mmv)
                            if h >= 2 and th == 0:
                                v.wait_ge(sems["KO%d" % (h % 2)], vdma[h - 2])
                            ve(v.tensor_scalar(
                                ktmp[:, (h % 2) * TC + th * TB:(h % 2) * TC + (th + 1) * TB],
                                ps, csb["bqkv"][:, oc + 32:oc + 33], None, ALU.add))
                        return ("VE", C["VE"])

                    def post_vp(oc):
                        h, par = oc, oc % 2
                        vbase = C["VE"]

                        @block.tensor
                        def _(t):
                            t.wait_ge(VE, vbase)
                            ins = None
                            for j2 in range(8):
                                ins = t.transpose(ps_vtr[:, j2 * 128:(j2 + 1) * 128],
                                                  ktmp[:, par * TC + j2 * 128:par * TC + (j2 + 1) * 128],
                                                  csb["ident"][:, :])
                            mmi(ins)
                        mm_vt = C["MM"]

                        @block.vector
                        def _(v):
                            v.wait_ge(MM, mm_vt)
                            ve(v.tensor_copy(ktmp[:, par * TC:(par + 1) * TC], ps_vtr[:, :]))
                        ve_vc = C["VE"]

                        @block.scalar
                        def _(s):
                            s.wait_ge(VE, ve_vc)
                            for ab in range(2):
                                for tp in range(2):
                                    dmai("KO%d" % par, s.dma_start(
                                        ag_v_in[ab, h, tp],
                                        ktmp[:, par * TC + ab * TB + tp * 256:
                                             par * TC + ab * TB + (tp + 1) * 256]))
                        vdma.append(C["KO%d" % par])
                    emit_proj(D, srcK, mk_mat_b(32 * 128), evac_v2, post_fn=post_vp)

                    def evac_q2(oc, th, ps, mmv):
                        @block.vector
                        def _(v):
                            v.wait_ge(MM, mmv)
                            ve(v.tensor_scalar(
                                q_sb[:, oc * TC + th * TB:oc * TC + (th + 1) * TB],
                                ps, csb["bqkv"][:, oc:oc + 1], INV_SQRT_DH, ALU.add, ALU.mult))
                        return ("VE", C["VE"])

                    def post_qp(oc):
                        post_norm(q_sb, oc * TC, oc, 0)
                    emit_proj(D, srcK, mk_mat_b(0), evac_q2, post_fn=post_qp)
                VE_NORMS = C["VE"]
                C_KO = (C["KO0"], C["KO1"])

                @block.scalar
                def _(s):
                    s.wait_ge(VE, VE_NORMS)
                    s2i(s.dma_start(ar_bd_in[:, :], nmax[:, 0:2]))
                S2_BD = C["S2"]

                @block.gpsimd
                def _(g):
                    g.wait_ge(S2, S2_BD)
                    g.wait_ge(sems["KO0"], C_KO[0])
                    g.wait_ge(sems["KO1"], C_KO[1])
                    cci(g.collective_compute("AllReduce", ALU.max, replica_groups=RG,
                                             ins=[ar_bd_in[:, :]], outs=[ar_bd_out[:, :]]))
                    g.wait_ge(CCs, C["CC"])
                    gp(g.dma_start(nmax[:, 0:2], ar_bd_out[:, :]))
                    cci(g.collective_compute("AllGather", ALU.bypass, replica_groups=RG,
                                             ins=[ag_k_in[:, :, :, :]],
                                             outs=[ag_k_out_t.ap()[:, :, :, :, :]]))
                    cci(g.collective_compute("AllGather", ALU.bypass, replica_groups=RG,
                                             ins=[ag_v_in[:, :, :, :, :]],
                                             outs=[agv_gat_t.ap()[:, :, :, :, :, :]]))
                CC_BD = C["CC"] - 2
                CC_K = C["CC"] - 1
                CC_V = C["CC"]
                G_BD = C["G"]

                @block.vector
                def _(v):
                    v.wait_ge(G, G_BD)
                    ve(v.tensor_mul(nmax[:, 2:3], nmax[:, 0:1], nmax[:, 1:2]))
                    vspace(v)
                VE_C1 = C["VE"]

                @block.scalar
                def _(s):
                    s.wait_ge(VE, VE_C1)
                    scc(s.activation(nmax[:, 3:4], nmax[:, 2:3], ACT.Sqrt))
                SC_C = C["SC"]

                @block.vector
                def _(v):
                    v.wait_ge(SC, SC_C)
                    ve(v.tensor_scalar_min(nmax[:, 3:4], nmax[:, 3:4], 20.0))
                    vspace(v)
                    vspace(v)
                    ve(v.tensor_scalar(nmax[:, 2:3], nmax[:, 3:4], -1.0, None, ALU.mult))
                    vspace(v)
                VE_NC = C["VE"]

                with contextlib.ExitStack() as pc2:
                    ps_bv = pc2.enter_context(nc.psum_tensor("ps_bv", [128, NIT], F32)).ap()
                    ps_ng = pc2.enter_context(nc.psum_tensor("ps_ng", [128, 1], F32)).ap()

                    @block.tensor
                    def _(t):
                        t.wait_ge(VE, VE_NC)
                        mmi(t.matmul(ps_ng, onesr_f[:, :], nmax[:1, 2:3], start=True, stop=True))
                        mmi(t.matmul(ps_bv, onesr_f[:, :], csb["hsel"][:1, :], start=True, stop=True))
                    MM_BV = C["MM"]

                    @block.vector
                    def _(v):
                        v.wait_ge(MM, MM_BV)
                        ve(v.tensor_copy(negc[:, :], ps_ng))
                        vspace(v)
                        vspace(v)
                        ve(v.tensor_scalar(biasv[:, :], ps_bv, negc[:, 0:1], None, ALU.add))
                VE_BV = C["VE"]

                with contextlib.ExitStack() as pat:
                    attacc = pat.enter_context(nc.sbuf_tensor("attacc", [128, H * TB], F32)).ap()
                    masks_sb = pat.enter_context(nc.sbuf_tensor("masks_sb", [128, 4 * TB], BF16)).ap()
                    ohr_sb = pat.enter_context(nc.sbuf_tensor("ohr_sb", [H, 128 * H], F32)).ap()

                    @block.sync
                    def _(sync):
                        wait_snap(sync, ALL_SEMS)
                        sy(sync.dma_start(masks_sb[:, :], din["masks"][:, :]))
                        sy(sync.dma_start(ohr_sb[:, :], din["ohr"][:, :]))
                    SY_MASKS = C["SY"]
                    kbuf = pat.enter_context(nc.sbuf_tensor("kbuf", [128, 2 * 4 * TB], BF16)).ap()
                    vbuf = pat.enter_context(nc.sbuf_tensor("vbuf", [128, 2 * 4 * TB], BF16)).ap()
                    abuf = pat.enter_context(nc.sbuf_tensor("abuf", [128, 4 * TB], BF16)).ap()
                    sums_sb = pat.enter_context(nc.sbuf_tensor("sums_sb", [H, TB], F32)).ap()
                    sumt = pat.enter_context(nc.sbuf_tensor("sumt", [128, 3 * TB], BF16)).ap()
                    ps_sc = pat.enter_context(nc.psum_tensor("ps_sc", [128, 4 * TB], F32)).ap()
                    ps_av = [pat.enter_context(nc.psum_tensor(nm("ps_av"), [128, TB], F32)).ap()
                             for _t in range(2)]
                    ps_sm = pat.enter_context(nc.psum_tensor("ps_sm", [H, TB], F32)).ap()

                    it_tab = [(0, 0, 0)] + [(0, 1, r) for r in range(8)] \
                        + [(1, 0, 1)] + [(1, 1, r) for r in range(8)] + [(1, 2, r) for r in range(8)]
                    assert len(it_tab) == NIT

                    kv_ld, kv_use = {}, {}
                    sc_free = []
                    av_mm = []
                    for itg, (tqb, kind, rank) in enumerate(it_tab):
                        local_first = (itg in (0, 9))
                        if local_first:
                            @block.vector
                            def _(v):
                                wait_snap(v, ALL_SEMS)
                                ve(v.memset(attacc[:, :], 0.0))
                            acc_zero = C["VE"]
                            av_ev = []
                            n_base = itg
                        def kv_load(hg):
                            gkey = itg * 4 + hg

                            @block.sync
                            def _(sync):
                                if gkey >= 2:
                                    sync.wait_ge(MM, kv_use[gkey - 2])
                                if gkey == 0:
                                    sync.wait_ge(CCs, CC_K)
                                    wait_snap(sync, ALL_SEMS)
                                if gkey == 1:
                                    sync.wait_ge(CCs, CC_V)
                                par = gkey % 2
                                if kind == 0:
                                    ab = tqb
                                    ksrc = ag_k_in[ab, 4 * hg:4 * (hg + 1)]
                                    vsrc = ag_v_in[ab, 4 * hg:4 * (hg + 1)]
                                else:
                                    ab = 0 if kind == 1 else 1
                                    ksrc = ag_k_out_t.ap()[rank, ab, 4 * hg:4 * (hg + 1)]
                                    vsrc = agv_gat_t.ap()[rank, ab, 4 * hg:4 * (hg + 1)]
                                kdst = kbuf.rearrange("q (b hh t) -> q b hh t", b=2, t=TB)
                                dmai("KV%d" % par, sync.dma_start(kdst[:, par, :, :],
                                                                  ksrc.rearrange("hh p t -> p hh t")))
                                vdst = vbuf.rearrange("q (b tp hh f) -> q b tp hh f",
                                                      b=2, tp=2, f=256)
                                for tp in range(2):
                                    dmai("KV%d" % par, sync.dma_start(
                                        vdst[:, par, tp, :, :],
                                        vsrc[:, tp].rearrange("hh p f -> p hh f")))
                                kv_ld[gkey] = C["KV%d" % par]

                        for h in range(H):
                            if h % 4 == 0:
                                kv_load(h // 4)
                            hg = h // 4
                            hh = h % 4
                            gkey = itg * 4 + hg
                            par = gkey % 2
                            idx = itg - n_base
                            lidx = len(av_ev)
                            fidx = len(sc_free)
                            kb = kbuf[:, par * 4 * TB + hh * TB:par * 4 * TB + (hh + 1) * TB]
                            vbase = par * 4 * TB
                            qs = q_sb[:, h * TC + tqb * TB:h * TC + (tqb + 1) * TB]

                            @block.tensor
                            def _(t):
                                t.wait_ge(sems["KV%d" % (gkey % 2)], kv_ld[gkey])
                                if lidx == 0:
                                    t.wait_ge(VE, max(VE_BV, acc_zero))
                                if fidx >= 1:
                                    psc, pve = sc_free[fidx - 1]
                                    t.wait_ge(SC, psc)
                                    if pve:
                                        t.wait_ge(VE, pve)
                                ins = None
                                for sub in range(2):
                                    ins = t.matmul(ps_sc[:, sub * TB:(sub + 1) * TB],
                                                   kb[:, sub * 128:(sub + 1) * 128], qs,
                                                   start=True, stop=True)
                                mmi(ins)
                                for sub in range(2, 4):
                                    ins = t.matmul(ps_sc[:, sub * TB:(sub + 1) * TB],
                                                   kb[:, sub * 128:(sub + 1) * 128], qs,
                                                   start=True, stop=True)
                                mmi(ins)
                            m_s = (C["MM"] - 1, C["MM"])

                            exp_sc = []
                            for half in range(2):
                                @block.scalar
                                def _(s, half=half):
                                    s.wait_ge(MM, m_s[half])
                                    if fidx >= 1 and half == 0:
                                        s.wait_ge(MM, av_mm[fidx - 1])
                                    exp_sc.append(scc(s.activation(
                                        abuf[:, half * TC:(half + 1) * TC],
                                        ps_sc[:, half * TC:(half + 1) * TC],
                                        ACT.Exp, bias=biasv[:, itg:itg + 1])))
                            mask_ve = 0
                            if kind == 0:
                                @block.vector
                                def _(v):
                                    v.wait_ge(SC, exp_sc[1])
                                    v.wait_ge(SY, SY_MASKS)
                                    ve(v.tensor_mul(abuf[:, :], abuf[:, :], masks_sb[:, :]))
                                mask_ve = C["VE"]
                            sc_free.append((exp_sc[1], mask_ve))

                            n_loc = (9 if tqb == 0 else 17) * H

                            @block.vector
                            def _(v):
                                if mask_ve:
                                    v.wait_ge(VE, mask_ve)
                                else:
                                    v.wait_ge(SC, exp_sc[1])
                                ve(v.tensor_add(sumt[:, 0:TB], abuf[:, 0:TB], abuf[:, TB:2 * TB]))
                                ve(v.tensor_add(sumt[:, TB:2 * TB], abuf[:, 2 * TB:3 * TB],
                                                abuf[:, 3 * TB:4 * TB]))
                                ve(v.tensor_add(sumt[:, 2 * TB:3 * TB], sumt[:, 0:TB],
                                                sumt[:, TB:2 * TB]))
                            sum_ve = C["VE"]

                            @block.tensor
                            def _(t):
                                if mask_ve:
                                    t.wait_ge(VE, mask_ve)
                                else:
                                    t.wait_ge(SC, exp_sc[1])
                                if lidx >= 2:
                                    t.wait_ge(VE, av_ev[lidx - 2])
                                apar = lidx % 2
                                for sub in range(4):
                                    vo = vbase + (sub // 2) * 1024 + hh * 256 + (sub % 2) * 128
                                    t.matmul(ps_av[apar], vbuf[:, vo:vo + 128],
                                             abuf[:, sub * TB:(sub + 1) * TB],
                                             start=(sub == 0), stop=(sub == 3))
                                t.wait_ge(VE, sum_ve)
                                ins = t.matmul(ps_sm,
                                               csb["ohot"][:, 16 * h:16 * (h + 1)],
                                               sumt[:, 2 * TB:3 * TB],
                                               start=(lidx == 0),
                                               stop=(lidx == n_loc - 1))
                                mmi(ins)
                            kv_use[gkey] = C["MM"]
                            av_mm.append(C["MM"])
                            mm_av = C["MM"]

                            @block.vector
                            def _(v):
                                v.wait_ge(MM, mm_av)
                                apar = lidx % 2
                                ve(v.scalar_tensor_tensor(
                                    attacc[:, h * TB:(h + 1) * TB], ps_av[apar], 1.0,
                                    attacc[:, h * TB:(h + 1) * TB], ALU.mult, ALU.add))
                            av_ev.append(C["VE"])

                        if itg in (8, 25):
                            MM_SM = C["MM"]

                            @block.vector
                            def _(v):
                                v.wait_ge(MM, MM_SM)
                                ve(v.tensor_scalar(sums_sb[:, :], ps_sm, float(2 ** 25),
                                                   None, ALU.mult))
                                vspace(v)
                                ve(v.reciprocal(sums_sb[:, :], sums_sb[:, :]))
                            VE_RS = C["VE"]

                            fin_ve = []
                            for h in range(H):
                                @block.tensor
                                def _(t, h=h):
                                    t.wait_ge(VE, VE_RS)
                                    if h >= 2:
                                        t.wait_ge(VE, fin_ve[h - 2])
                                    else:
                                        t.wait_ge(VE, av_ev[-1])
                                    if h == 0:
                                        t.wait_ge(SY, SY_MASKS)
                                    mmi(t.matmul(ps_av[h % 2], ohr_sb[:, 128 * h:128 * (h + 1)],
                                                 sums_sb[:, :], start=True, stop=True))
                                mm_rb = C["MM"]

                                @block.vector
                                def _(v, h=h, tqb=tqb, mm_rb=mm_rb):
                                    v.wait_ge(MM, mm_rb)
                                    ve(v.tensor_mul(xhat[:, h * TC + tqb * TB:h * TC + (tqb + 1) * TB],
                                                    attacc[:, h * TB:(h + 1) * TB], ps_av[h % 2]))
                                fin_ve.append(C["VE"])
                            tgt2 = dict(C)

                            @block.tensor
                            def _(t):
                                t.wait_ge(VE, tgt2["VE"])

                def evac_bo(oc, th, ps, mmv):
                    @block.vector
                    def _(v):
                        v.wait_ge(MM, mmv)
                        ve(v.scalar_tensor_tensor(
                            x_res[:, oc * TC + th * TB:oc * TC + (th + 1) * TB],
                            ps, csb["bout"][:, oc:oc + 1],
                            x_res[:, oc * TC + th * TB:oc * TC + (th + 1) * TB],
                            ALU.add, ALU.add))
                    return ("VE", C["VE"])

                def src_att(dc, th):
                    return xhat[:, dc * TC + th * TB:dc * TC + (th + 1) * TB]
                emit_proj(D, src_att, mk_mat_b(QKV_O), evac_bo)
            tap(3)
            barrier()

        if PH_FFN2:
            emit_ffn()

        tgt = dict(C)

        @block.scalar
        def _(s):
            for n in ["MM", "VE", "SC"]:
                if tgt[n]:
                    s.wait_ge(sems[n], tgt[n])
            src = x_res.rearrange("p (c t) -> p c t", t=TC)
            dst = out_d.rearrange("(c p) t -> p c t", p=128)
            for q in range(4):
                s2i(s.dma_start(dst[:, 4 * q:4 * (q + 1), :], src[:, 4 * q:4 * (q + 1), :]))
        S2_OUT = C["S2"]

        if DEBUG_TAPS:
            @block.sync
            def _(sync):
                sync.wait_ge(CCs, C["CC"])
                for r_ in range(NCORES):
                    sy(sync.dma_start(dbgwu[r_], ag_ug_out[r_]))
                    sy(sync.dma_start(dbgwd[r_], ag_dn_out[r_]))
                sy(sync.dma_start(dbgs[:, 0:9], qparams[:, :]))
                sy(sync.dma_start(dbgs[0:1, 9:16], scl[:, 0:7]))
                sy(sync.dma_start(dbgs[:, 16:20], absacc[:, 0:4]))
        SY_END = C["SY"]

        @block.sync
        def _(sync):
            sync.wait_ge(S2, S2_OUT)
            if DEBUG_TAPS:
                sync.wait_ge(SY, SY_END)

    return nc


def kernel(**inputs):
    in_maps = prepare_host(inputs)
    nc = build_nc()
    res = run_bass_kernel_spmd(nc, in_maps, core_ids=list(range(NCORES)))
    return assemble_host(res.results)
